# version 15
# baseline (speedup 1.0000x reference)
import contextlib
import numpy as np

import concourse.bass as bass
import concourse.mybir as mybir
from concourse.bass_utils import run_bass_kernel_spmd

F32 = mybir.dt.float32
BF16 = mybir.dt.bfloat16
FP8 = mybir.dt.float8e4
I32 = mybir.dt.int32
AF = mybir.ActivationFunctionType
ALU = mybir.AluOpType
AX = mybir.AxisListType

S, D, H = 2048, 1024, 4096
NT = S // 128
DC = D // 128
HC = H // 128
N_CORES = 8
PHASES = 3
ATTN_CUT = ""
FOLD_A = [64.0, 32.0, 16.0, 8.0, 4.0, 2.0, 1.0]
FOLD_C = [63.5, 32.0, 16.0, 8.0, 4.0, 2.0, 1.0]
PC = 1536
BIG = 4503599627370496.0


def build_nc(phases=PHASES):
    nc = bass.Bass("TRN2", target_bir_lowering=False)

    x_h = nc.dram_tensor("x", [S, D], F32, kind="ExternalInput").ap()
    wq_h = nc.dram_tensor("wq", [D, D], F32, kind="ExternalInput").ap()
    bq_h = nc.dram_tensor("bq", [D], F32, kind="ExternalInput").ap()
    wk_h = nc.dram_tensor("wk", [D, D], F32, kind="ExternalInput").ap()
    bk_h = nc.dram_tensor("bk", [D], F32, kind="ExternalInput").ap()
    wv_h = nc.dram_tensor("wv", [D, D], F32, kind="ExternalInput").ap()
    bv_h = nc.dram_tensor("bv", [D], F32, kind="ExternalInput").ap()
    wo_h = nc.dram_tensor("wo", [D, D], F32, kind="ExternalInput").ap()
    bo_h = nc.dram_tensor("bo", [D], F32, kind="ExternalInput").ap()
    w1_h = nc.dram_tensor("w1", [H, D], F32, kind="ExternalInput").ap()
    cw_h = nc.dram_tensor("cw", [H, 1, 3], F32, kind="ExternalInput").ap()
    cb_h = nc.dram_tensor("cb", [H], F32, kind="ExternalInput").ap()
    w2_h = nc.dram_tensor("w2", [D, H], F32, kind="ExternalInput").ap()
    y_h = nc.dram_tensor("y", [S, D], F32, kind="ExternalOutput").ap()

    sx1 = nc.dram_tensor("sx1", [S, D], F32, kind="Internal").ap()
    sx2 = nc.dram_tensor("sx2", [S, D], F32, kind="Internal").ap()
    sw1t = nc.dram_tensor("sw1t", [128, DC * H], FP8, kind="Internal").ap()
    sw2t = nc.dram_tensor("sw2t", [128, HC * D], FP8, kind="Internal").ap()

    gctx = contextlib.ExitStack()
    gsb = lambda name, shp, dt: gctx.enter_context(nc.sbuf_tensor(name, shp, dt))

    sems, cnt = {}, {}
    for nm in ("ldb", "lwt", "stw", "dve", "act", "pe", "pool",
               "ldx0", "ldx1", "ldx2", "ldx3", "lw0", "lw1", "lw2", "lw3",
               "st0", "st1"):
        sems[nm] = gctx.enter_context(nc.semaphore("s_" + nm))
        cnt[nm] = 0

    def inc(instr, nm, n=1):
        instr.then_inc(sems[nm], n)
        cnt[nm] += n
        return (nm, cnt[nm])

    _SER_OPS = {"tensor_scalar", "tensor_tensor", "scalar_tensor_tensor",
                "tensor_copy", "tensor_reduce", "reduce_max", "reduce_sum",
                "memset", "reciprocal", "activation", "copy", "add", "mul"}

    class SER:
        def __init__(self, eng, nm):
            self._e = eng
            self._nm = nm

        def __getattr__(self, name):
            f = getattr(self._e, name)
            if name in _SER_OPS:
                nm = self._nm

                def g(*a, **k):
                    if cnt[nm] > 0:
                        self._e.wait_ge(sems[nm], cnt[nm])
                    r = f(*a, **k)
                    inc(r, nm)
                    return r
                return g
            return f

    def wait(eng, tok, v=None):
        if tok == 0 or tok is None:
            return
        if v is None:
            nm, val = tok
        else:
            nm, val = tok, v
        if val > 0:
            eng.wait_ge(sems[nm], val)

    ident_b = gsb("ident_b", [128, 128], BF16)
    ident_f = gsb("ident_f", [128, 128], F32)
    m01_b = gsb("m01_b", [128, 128], BF16)
    ones_row_b = gsb("ones_row_b", [1, 128], BF16)
    ones_row_f = gsb("ones_row_f", [1, 128], F32)
    ones_col_f = gsb("ones_col_f", [128, 1], F32)
    io_t = gsb("io_t", [128, 128], I32)
    bq_t = gsb("bq_t", [128, DC], F32)
    bk_t = gsb("bk_t", [128, DC], F32)
    bv_row = gsb("bv_row", [1, D], BF16)
    bo_row = gsb("bo_row", [1, D], BF16)
    w0c_t = gsb("w0c_t", [128, HC], F32)
    w1c_t = gsb("w1c_t", [128, HC], F32)
    w2c_t = gsb("w2c_t", [128, HC], F32)
    cb_t = gsb("cb_t", [128, HC], F32)
    m1b = gsb("m1b", [128, 1], F32)
    m2b = gsb("m2b", [128, 1], F32)
    thrP1 = gsb("thrP1", [128, 1], F32)
    thrN1 = gsb("thrN1", [128, 1], F32)
    thrP2 = gsb("thrP2", [128, 1], F32)
    thrN2 = gsb("thrN2", [128, 1], F32)
    sc11 = gsb("sc11", [1, 4], F32)

    XS = 3
    xg = gsb("xg", [128, XS * D], F32)
    xslot = lambda s: xg[:, (s % XS) * D:(s % XS) * D + D]
    outx = gsb("outx", [128, 2 * D], F32)
    oslot = lambda s: outx[:, (s % 2) * D:(s % 2) * D + D]

    st_m = gsb("st_m", [128, NT], F32)
    st_var = gsb("st_var", [128, NT], F32)
    st_rstd = gsb("st_rstd", [128, NT], F32)
    st_mx = gsb("st_mx", [128, NT], F32)
    st_mn = gsb("st_mn", [128, NT], F32)
    st_A = gsb("st_A", [128, NT], F32)
    st_B = gsb("st_B", [128, NT], F32)
    st_c1 = gsb("st_c1", [128, NT], F32)
    st_t = gsb("st_t", [128, NT], F32)

    cst = gsb("cst", [128, 12], F32)
    CB = {1e-5: cst[:, 0:1], -63.5: cst[:, 1:2], -32.0: cst[:, 2:3], -16.0: cst[:, 3:4],
          -8.0: cst[:, 4:5], -4.0: cst[:, 5:6], -2.0: cst[:, 6:7], -1.0: cst[:, 7:8],
          -0.5: cst[:, 8:9]}
    SCR = {}

    class _ScrProxy:
        def __init__(self, key):
            self.key = key

        def __getitem__(self, idx):
            return SCR[self.key][idx]

    scrV = _ScrProxy("V")
    scrA = _ScrProxy("A")
    scrR = _ScrProxy("R")
    scrG = _ScrProxy("G")

    last_absA = [0]
    last_peel = [0]
    last_pool = [0]
    last_absA1 = [0]
    last_peel1 = [0]

    p1A = _ScrProxy("A1")
    p1R = _ScrProxy("R1")
    p1G = _ScrProxy("G1")

    def emit_peel_core(blk, sign_ap, out_ap, F, sA=scrA, sR=scrR, sG=scrG,
                       cell=last_peel, sign_stored=False):
        tk = {}

        @blk.vector
        def _(v):
            v = SER(v, "dve")
            v.tensor_scalar(sG[:, :F], sA[:, :F], FOLD_A[0], FOLD_A[0],
                            ALU.is_ge, ALU.mult)
            v.tensor_tensor(sR[:, :F], sA[:, :F], sG[:, :F], ALU.subtract)
            for c in FOLD_A[1:]:
                v.tensor_scalar(sG[:, :F], sR[:, :F], c, c, ALU.is_ge, ALU.mult)
                v.tensor_tensor(sR[:, :F], sR[:, :F], sG[:, :F], ALU.subtract)
            v.scalar_tensor_tensor(sG[:, :F], sR[:, :F], 0.5, sA[:, :F],
                                   ALU.is_ge, ALU.add)
            v.tensor_tensor(sA[:, :F], sG[:, :F], sR[:, :F], ALU.subtract)
            if sign_stored:
                r = v.scalar_tensor_tensor(out_ap, out_ap, -1.0, sA[:, :F],
                                           ALU.add, ALU.mult)
            else:
                if callable(sign_ap):
                    sign_ap(v)
                else:
                    v.tensor_scalar(sR[:, :F], sign_ap, 0.0, 2.0, ALU.is_ge, ALU.mult)
                r = v.scalar_tensor_tensor(out_ap, sR[:, :F], -1.0, sA[:, :F],
                                           ALU.add, ALU.mult)
            _ = r; tk["out"] = ("dve", cnt["dve"])

        cell[0] = tk["out"]
        return tk["out"]

    def emit_peel(blk, v_ap, out_ap, F, act_wait=None):
        @blk.scalar
        def _(a):
            a = SER(a, "act")
            if act_wait is not None:
                wait(a, *act_wait)
            wait(a, last_peel[0])
            r = a.activation(scrA[:, :F], v_ap, AF.Abs)
            _ = r; last_absA[0] = ("act", cnt["act"])

        @blk.vector
        def _(v):
            v = SER(v, "dve")
            wait(v, last_absA[0])
        return emit_peel_core(blk, v_ap, out_ap, F)

    wctx = contextlib.ExitStack()
    w1t = wctx.enter_context(nc.sbuf_tensor("w1t", [128, DC * H], FP8))
    w2t = wctx.enter_context(nc.sbuf_tensor("w2t", [128, HC * D], FP8))

    pctx = contextlib.ExitStack()
    psb = lambda name, shp, dt: pctx.enter_context(nc.sbuf_tensor(name, shp, dt))
    pps = lambda name, shp, dt: pctx.enter_context(nc.psum_tensor(name, shp, dt))

    with nc.Block("prologue") as blk:
        stg = psb("stg", [128, 2 * H], F32)
        sslot = lambda s: stg[:, (s % 2) * H:(s % 2) * H + H]
        scrT = psb("scrT", [128, H], BF16)
        SCR["V"] = psb("pscrV", [128, 1024], F32)
        SCR["R"] = psb("pscrR", [128, 1024], F32)
        w1abs = psb("w1abs", [128, HC], F32)
        w2abs = psb("w2abs", [128, DC], F32)
        ps_s = pps("ps_s", [128, 512], F32)
        ps_t = pps("ps_t", [128, 512], BF16)
        ps_t2 = pps("ps_t2", [128, 512], BF16)
        ps_1 = pps("ps_1", [1, 128], F32)

        @blk.gpsimd
        def _(g):
            r = g.iota(io_t[:], [[1, 128]], base=0, channel_multiplier=-1)
            inc(r, "pool")
        t_iota = ("pool", cnt["pool"])

        @blk.vector
        def _(v):
            v = SER(v, "dve")
            wait(v, t_iota)
            v.tensor_scalar(ident_b[:], io_t[:], 0.0, None, ALU.is_equal)
            v.tensor_scalar(ident_f[:], io_t[:], 0.0, None, ALU.is_equal)
            v.tensor_scalar(m01_b[:], io_t[:], 0.0, None, ALU.is_le)
            v.memset(ones_row_b[:], 1.0)
            v.memset(ones_row_f[:], 1.0)
            for val, ap in CB.items():
                v.memset(ap, val)
            r = v.memset(ones_col_f[:], 1.0)
            _ = ("dve", cnt["dve"])
        t_const = ("dve", cnt["dve"])

        @blk.sync
        def _(sy):
            inc(sy.dma_start(out=stg[0:8, 0:128], in_=bq_h.rearrange("(c p) -> c p", c=8)), "ldb", 16)
            inc(sy.dma_start(out=stg[0:8, 128:256], in_=bk_h.rearrange("(c p) -> c p", c=8)), "ldb", 16)
            inc(sy.dma_start(out=stg[0:32, 256:384], in_=cb_h.rearrange("(c p) -> c p", c=32)), "ldb", 16)
            inc(sy.dma_start(out=stg[0:32, 384:768],
                             in_=cw_h.rearrange("h o k -> h (o k)").rearrange("(c p) k -> c (p k)", c=32)), "ldb", 16)
            inc(sy.dma_start(out=stg[0:1, 1152:1152 + D], in_=bv_h.rearrange("(o d) -> o d", o=1)), "ldb", 16)
            inc(sy.dma_start(out=stg[0:1, 2176:2176 + D], in_=bo_h.rearrange("(o d) -> o d", o=1)), "ldb", 16)
        t_bld = ("ldb", cnt["ldb"])

        @blk.vector
        def _(v):
            v = SER(v, "dve")
            wait(v, t_bld)
            taps = stg[0:32, 384:768].rearrange("c (p k) -> c k p", k=3)
            for k in range(3):
                v.tensor_copy(stg[0:32, 768 + k * 128:768 + (k + 1) * 128],
                              taps[:, k:k + 1, :].rearrange("c one p -> c (one p)"))
            v.tensor_copy(bv_row[:], stg[0:1, 1152:1152 + D])
            r = v.tensor_copy(bo_row[:], stg[0:1, 2176:2176 + D])
            _ = ("dve", cnt["dve"])
        t_bup = ("dve", cnt["dve"])

        @blk.tensor
        def _(te):
            wait(te, t_bup)
            te.transpose(ps_s[:, 0:8], stg[0:8, 0:128], ident_f[0:8, 0:8])
            te.transpose(ps_s[:, 8:16], stg[0:8, 128:256], ident_f[0:8, 0:8])
            te.transpose(ps_s[:, 16:48], stg[0:32, 256:384], ident_f[0:32, 0:32])
            r = None
            for k in range(3):
                r = te.transpose(ps_s[:, 48 + 32 * k:48 + 32 * (k + 1)],
                                 stg[0:32, 768 + k * 128:768 + (k + 1) * 128],
                                 ident_f[0:32, 0:32])
            inc(r, "pe")
        t_btr = ("pe", cnt["pe"])

        @blk.vector
        def _(v):
            v = SER(v, "dve")
            wait(v, t_btr)
            v.tensor_copy(bq_t[:], ps_s[:, 0:8])
            v.tensor_copy(bk_t[:], ps_s[:, 8:16])
            v.tensor_copy(cb_t[:], ps_s[:, 16:48])
            v.tensor_copy(w0c_t[:], ps_s[:, 48:80])
            v.tensor_copy(w1c_t[:], ps_s[:, 80:112])
            r = v.tensor_copy(w2c_t[:], ps_s[:, 112:144])
            _ = ("dve", cnt["dve"])
        t_bdone = ("dve", cnt["dve"])

        t_wld, t_wred = {}, {}
        for c in range(HC + DC):
            @blk.sync
            def _(sy, c=c):
                if c == 0:
                    wait(sy, t_bdone)
                if c >= 2:
                    wait(sy, t_wred[c - 2])
                if c < HC:
                    r = sy.dma_start(out=sslot(c)[:, 0:D], in_=w1_h[c * 128:(c + 1) * 128, :])
                else:
                    r = sy.dma_start(out=sslot(c), in_=w2_h[(c - HC) * 128:(c - HC + 1) * 128, :])
                t_wld[c] = inc(r, "lw%d" % (c % 2), 16)

            @blk.vector
            def _(v, c=c):
                v = SER(v, "dve")
                wait(v, t_wld[c])
                if c < HC:
                    r = v.tensor_reduce(w1abs[:, c:c + 1], sslot(c)[:, 0:D], AX.X,
                                        ALU.add, apply_absolute_value=True)
                else:
                    r = v.tensor_reduce(w2abs[:, c - HC:c - HC + 1], sslot(c), AX.X,
                                        ALU.add, apply_absolute_value=True)
                _ = r; t_wred[c] = ("dve", cnt["dve"])

        @blk.vector
        def _(v):
            v = SER(v, "dve")
            wait(v, t_wred[HC + DC - 1])
            v.tensor_reduce(st_t[:, 0:1], w1abs[:], AX.X, ALU.add)
            r = v.tensor_reduce(st_t[:, 1:2], w2abs[:], AX.X, ALU.add)
            _ = ("dve", cnt["dve"])
        t_wsum = ("dve", cnt["dve"])

        @blk.tensor
        def _(te):
            wait(te, t_wsum)
            r = te.matmul(ps_1[0:1, 0:2], ones_col_f[:], st_t[:, 0:2], start=True, stop=True)
            inc(r, "pe")
        t_msum = ("pe", cnt["pe"])

        @blk.vector
        def _(v):
            v = SER(v, "dve")
            wait(v, t_msum)
            r = v.tensor_scalar(sc11[0:1, 0:2], ps_1[0:1, 0:2], 1.0 / (H * D), 1e-5,
                                ALU.mult, ALU.max)
            _ = ("dve", cnt["dve"])
        t_m = ("dve", cnt["dve"])

        @blk.tensor
        def _(te):
            wait(te, t_m)
            r = te.matmul(ps_s[:, 256:258], ones_row_f[:], sc11[0:1, 0:2],
                          start=True, stop=True)
            inc(r, "pe")
        t_mb = ("pe", cnt["pe"])

        @blk.vector
        def _(v):
            v = SER(v, "dve")
            wait(v, t_mb)
            v.tensor_copy(m1b[:], ps_s[:, 256:257])
            v.tensor_copy(m2b[:], ps_s[:, 257:258])
            v.tensor_scalar(thrP1[:], m1b[:], 0.5, None, ALU.mult)
            v.tensor_scalar(thrN1[:], m1b[:], -0.5, None, ALU.mult)
            v.tensor_scalar(thrP2[:], m2b[:], 0.5, None, ALU.mult)
            r = v.tensor_scalar(thrN2[:], m2b[:], -0.5, None, ALU.mult)
            _ = ("dve", cnt["dve"])
        t_thr = ("dve", cnt["dve"])

        t_ld2, t_tern, t_tp, t_ev = {}, {}, {}, {}
        for c in range(HC):
            @blk.sync
            def _(sy, c=c):
                if c == 0:
                    wait(sy, t_thr)
                if c >= 2:
                    wait(sy, t_tern[c - 2])
                r = sy.dma_start(out=sslot(c)[:, 0:D], in_=w1_h[c * 128:(c + 1) * 128, :])
                t_ld2[c] = inc(r, "lw%d" % (c % 2), 16)

            @blk.vector
            def _(v, c=c):
                v = SER(v, "dve")
                wait(v, t_ld2[c])
                if c >= 1:
                    wait(v, t_tp[c - 1])
                v.tensor_scalar(scrV[:, 0:D], sslot(c)[:, 0:D], thrP1[:], None, ALU.is_ge)
                v.tensor_scalar(scrR[:, 0:D], sslot(c)[:, 0:D], thrN1[:], None, ALU.is_le)
                r = v.tensor_tensor(scrT[:, 0:D], scrV[:, 0:D], scrR[:, 0:D], ALU.subtract)
                _ = r; t_tern[c] = ("dve", cnt["dve"])

            @blk.tensor
            def _(te, c=c):
                wait(te, t_tern[c])
                if c >= 1:
                    wait(te, t_ev[c - 1])
                r = None
                for dc in range(DC):
                    pt = ps_t if dc < 4 else ps_t2
                    r = te.transpose(pt[:, (dc % 4) * 128:(dc % 4) * 128 + 128],
                                     scrT[:, dc * 128:(dc + 1) * 128], ident_b[:])
                t_tp[c] = inc(r, "pe")

            @blk.scalar
            def _(a, c=c):
                a = SER(a, "act")
                wait(a, t_tp[c])
                r = None
                for dc in range(DC):
                    pt = ps_t if dc < 4 else ps_t2
                    r = a.copy(w1t[:, dc * H + c * 128:dc * H + (c + 1) * 128],
                               pt[:, (dc % 4) * 128:(dc % 4) * 128 + 128])
                _ = r; t_ev[c] = ("act", cnt["act"])

        t_ld3, t_tern2, t_tp2, t_ev2 = {}, {}, {}, {}
        for c in range(DC):
            @blk.sync
            def _(sy, c=c):
                if c >= 2:
                    wait(sy, t_tern2[c - 2])
                else:
                    wait(sy, t_tern[HC - 2 + c])
                r = sy.dma_start(out=sslot(c), in_=w2_h[c * 128:(c + 1) * 128, :])
                t_ld3[c] = inc(r, "lw%d" % (c % 2), 16)

            @blk.vector
            def _(v, c=c):
                v = SER(v, "dve")
                wait(v, t_ld3[c])
                if c == 0:
                    wait(v, t_tp[HC - 1])
                else:
                    wait(v, t_tp2[(c - 1, 3)])
                r = None
                for q in range(4):
                    sl = slice(q * 1024, (q + 1) * 1024)
                    v.tensor_scalar(scrV[:, 0:D], sslot(c)[:, sl], thrP2[:], None, ALU.is_ge)
                    v.tensor_scalar(scrR[:, 0:D], sslot(c)[:, sl], thrN2[:], None, ALU.is_le)
                    r = v.tensor_tensor(scrT[:, sl], scrV[:, 0:D], scrR[:, 0:D], ALU.subtract)
                _ = r; t_tern2[c] = ("dve", cnt["dve"])

            for b in range(4):
                @blk.tensor
                def _(te, c=c, b=b):
                    wait(te, t_tern2[c])
                    prev = (c, b - 1) if b > 0 else ((c - 1, 3) if c > 0 else None)
                    if prev is not None:
                        wait(te, t_ev2[prev])
                    elif c == 0 and b == 0:
                        wait(te, t_ev[HC - 1])
                    r = None
                    for j in range(8):
                        hh = b * 8 + j
                        pt = ps_t if j < 4 else ps_t2
                        r = te.transpose(pt[:, (j % 4) * 128:(j % 4) * 128 + 128],
                                         scrT[:, hh * 128:(hh + 1) * 128], ident_b[:])
                    t_tp2[(c, b)] = inc(r, "pe")

                @blk.scalar
                def _(a, c=c, b=b):
                    a = SER(a, "act")
                    wait(a, t_tp2[(c, b)])
                    r = None
                    for j in range(8):
                        hh = b * 8 + j
                        pt = ps_t if j < 4 else ps_t2
                        r = a.copy(w2t[:, hh * D + c * 128:hh * D + (c + 1) * 128],
                                   pt[:, (j % 4) * 128:(j % 4) * 128 + 128])
                    _ = r; t_ev2[(c, b)] = ("act", cnt["act"])

        @blk.gpsimd
        def _(g):
            wait(g, t_ev[HC - 1])
            r = g.dma_start(out=sw1t, in_=w1t[:])
            inc(r, "stw", 16)
            wait(g, t_ev2[(DC - 1, 3)])
            r = g.dma_start(out=sw2t, in_=w2t[:])
            inc(r, "stw", 16)
        t_wcache = ("stw", cnt["stw"])

    pctx.close()

    def ffn_phase(tag, src, dst, w1t, w2t, wait_src_st=0, preload_wt=0):
        fctx = contextlib.ExitStack()
        fsb = lambda name, shp, dt: fctx.enter_context(nc.sbuf_tensor(name + "_" + tag, shp, dt))
        fps = lambda name, shp, dt: fctx.enter_context(nc.psum_tensor(name + "_" + tag, shp, dt))

        SCR["V"] = fsb("scrV", [128, 1024], F32)
        SCR["A1"] = fsb("scrA1", [128, 1024], F32)
        SCR["R1"] = fsb("scrR1", [128, 1024], F32)
        SCR["G1"] = SCR["V"]
        SCR["A"] = fsb("scrA", [128, 2048], F32)
        SCR["R"] = fsb("scrR", [128, 2048], F32)
        SCR["G"] = fsb("scrG", [128, 2048], F32)
        hgB = fsb("hg", [128, 3 * H], F32)
        hslot = lambda g: hgB[:, (g % 3) * H:(g % 3) * H + H]
        hp = fsb("hp", [128, H], F32)
        n1g = fsb("n1g", [128, D], BF16)
        n1t = fsb("n1t", [128, 2 * D], BF16)
        n1ts = lambda g: n1t[:, (g % 2) * D:(g % 2) * D + D]
        n2g = fsb("n2g", [128, H], BF16)
        bnd = fsb("bnd", [128, HC], F32)
        c1B = fsb("c1B", [128, 128], F32)
        s2B = fsb("s2B", [128, 128], F32)
        c1row = fsb("c1row", [1, 128], F32)
        c2v = fsb("c2v", [128, 2], F32)
        s2v = fsb("s2v", [128, 1], F32)
        ps_h = [fps("ps_h%d" % i, [128, 512], F32) for i in range(2)]
        ps_o = [fps("ps_o%d" % i, [128, 512], F32) for i in range(2)]
        ps_tt = fps("ps_tt", [128, 512], BF16)
        ps_tt2 = fps("ps_tt2", [128, 512], BF16)
        ps_sm = fps("ps_sm", [128, 512], F32)

        with nc.Block("ffn_" + tag) as blk:
            if preload_wt:
                @blk.sync
                def _(sy):
                    wait(sy, preload_wt)
                    inc(sy.dma_start(out=w1t[:], in_=sw1t), "lwt", 16)
                    t_wt0 = inc(sy.dma_start(out=w2t[:], in_=sw2t), "lwt", 16)
            t_wt = ("lwt", cnt["lwt"])

            t_lds, t_sq, t_red = {}, {}, {}
            for g in range(NT):
                @blk.sync
                def _(sy, g=g):
                    if g == 0:
                        for _tk in (wait_src_st or []):
                            wait(sy, _tk)
                    if g >= XS:
                        wait(sy, t_red[g - XS])
                        wait(sy, t_sq[g - XS])
                    r = sy.dma_start(out=xslot(g), in_=src[g * 128:(g + 1) * 128, :])
                    t_lds[g] = inc(r, "ldx%d" % (g % XS), 16)

                @blk.scalar
                def _(a, g=g):
                    a = SER(a, "act")
                    wait(a, t_lds[g])
                    r = a.activation(scrV[:, 0:D] if g % 2 == 0 else scrG[:, 0:D],
                                     xslot(g), AF.Square, accum_out=st_var[:, g:g + 1])
                    _ = r; t_sq[g] = ("act", cnt["act"])

                @blk.vector
                def _(v, g=g):
                    v = SER(v, "dve")
                    wait(v, t_lds[g])
                    v.tensor_reduce(st_m[:, g:g + 1], xslot(g), AX.X, ALU.add)
                    v.tensor_reduce(st_mx[:, g:g + 1], xslot(g), AX.X, ALU.max)
                    r = v.tensor_reduce(st_mn[:, g:g + 1], xslot(g), AX.X, ALU.min)
                    _ = r; t_red[g] = ("dve", cnt["dve"])


            @blk.vector
            def _(v):
                v = SER(v, "dve")
                wait(v, t_red[NT - 1])
                wait(v, t_sq[NT - 1])
                v.tensor_scalar(st_m[:], st_m[:], 1.0 / D, None, ALU.mult)
                v.tensor_scalar(st_var[:], st_var[:], 1.0 / D, None, ALU.mult)
                v.tensor_tensor(st_t[:], st_m[:], st_m[:], ALU.mult)
                r = v.tensor_tensor(st_var[:], st_var[:], st_t[:], ALU.subtract)
                _ = ("dve", cnt["dve"])
            t_var = ("dve", cnt["dve"])

            @blk.scalar
            def _(a):
                a = SER(a, "act")
                wait(a, t_var)
                r = a.activation(st_rstd[:], st_var[:], AF.Sqrt, bias=CB[1e-5], scale=1.0)
                _ = ("act", cnt["act"])
            t_sd = ("act", cnt["act"])

            @blk.vector
            def _(v):
                v = SER(v, "dve")
                wait(v, t_sd)
                v.reciprocal(st_rstd[:], st_rstd[:])
                v.tensor_tensor(st_mx[:], st_mx[:], st_m[:], ALU.subtract)
                v.tensor_tensor(st_mn[:], st_m[:], st_mn[:], ALU.subtract)
                v.tensor_tensor(st_mx[:], st_mx[:], st_mn[:], ALU.max)
                v.tensor_tensor(st_mx[:], st_mx[:], st_rstd[:], ALU.mult)
                v.tensor_scalar(st_mx[:], st_mx[:], 1e-5, None, ALU.max)
                v.reciprocal(st_t[:], st_mx[:])
                v.tensor_scalar(st_t[:], st_t[:], 127.0, None, ALU.mult)
                v.tensor_tensor(st_A[:], st_rstd[:], st_t[:], ALU.mult)
                v.scalar_tensor_tensor(st_B[:], st_m[:], -1.0, st_A[:], ALU.mult, ALU.mult)
                v.tensor_scalar(st_c1[:], st_mx[:], 1.0 / 127.0, None, ALU.mult)
                r = None
                for g in range(NT):
                    r = v.tensor_tensor(st_c1[:, g:g + 1], st_c1[:, g:g + 1], m1b[:], ALU.mult)
                _ = ("dve", cnt["dve"])
            t_small = ("dve", cnt["dve"])

            t_ldx, t_v, t_n1, t_tp, t_n1t = {}, {}, {}, {}, {}
            t_c1b_pe, t_c1b = {}, {}
            t_mm1, t_hgev = {}, {}
            t_convA, t_convD, t_silu, t_qmx, t_qtr, t_s2, t_qtr2, t_s2b = {}, {}, {}, {}, {}, {}, {}, {}
            t_n2a, t_n2g, t_n2, t_absq0, t_sgn = {}, {}, {}, {}, {}
            t_n2p = {}
            t_mm2, t_ev2s, t_st = {}, {}, {}

            def emit_front_a(g):
                @blk.sync
                def _(sy):
                    if g < XS:
                        wait(sy, t_sq[NT - 1])
                        wait(sy, t_small)
                    else:
                        wait(sy, t_ev2s[g - XS])
                    r = sy.dma_start(out=xslot(g), in_=src[g * 128:(g + 1) * 128, :])
                    t_ldx[g] = inc(r, "ldx%d" % (g % XS), 16)

                @blk.scalar
                def _(a):
                    a = SER(a, "act")
                    wait(a, t_ldx[g])
                    wait(a, last_peel1[0])
                    r = a.activation(p1A[:, 0:D], xslot(g), AF.Abs,
                                     bias=st_B[:, g:g + 1], scale=st_A[:, g:g + 1])
                    _ = r; t_v[g] = ("act", cnt["act"])
                    last_absA1[0] = t_v[g]

                @blk.vector
                def _(v):
                    v = SER(v, "dve")
                    wait(v, last_absA1[0])

                t_n1[g] = emit_peel_core(
                    blk,
                    lambda v, g=g: v.tensor_scalar(p1R[:, 0:D], xslot(g),
                                                   st_m[:, g:g + 1], 2.0,
                                                   ALU.is_ge, ALU.mult),
                    n1g[:], D, sA=p1A, sR=p1R, sG=p1G, cell=last_peel1)

                @blk.tensor
                def _(te):
                    wait(te, t_n1[g])
                    if g >= 2:
                        wait(te, t_n1t[g - 2])
                    r = None
                    for dc in range(DC):
                        pt = ps_tt if dc < 4 else ps_tt2
                        r = te.transpose(pt[:, (dc % 4) * 128:(dc % 4) * 128 + 128],
                                         n1g[:, dc * 128:(dc + 1) * 128], ident_b[:])
                    t_tp[g] = inc(r, "pe")

                @blk.vector
                def _(v):
                    v = SER(v, "dve")
                    wait(v, t_tp[g])
                    if g >= 2:
                        wait(v, t_mm1[(g - 2, HC // 4 - 1)])
                    v.tensor_copy(n1ts(g)[:, 0:512], ps_tt[:])
                    v.tensor_copy(n1ts(g)[:, 512:1024], ps_tt2[:])
                    t_n1t[g] = ("dve", cnt["dve"])

                @blk.tensor
                def _(te):
                    wait(te, t_small)
                    if g >= 1:
                        wait(te, t_c1b[g - 1])
                    r = te.transpose(ps_sm[0:1, 128:256], st_c1[:, g:g + 1], ident_f[:])
                    t_c1b_pe[g] = inc(r, "pe")

                @blk.vector
                def _(v):
                    v = SER(v, "dve")
                    wait(v, t_c1b_pe[g])
                    r = v.tensor_copy(c1row[:], ps_sm[0:1, 128:256])
                    _ = r; t_c1b[("row", g)] = ("dve", cnt["dve"])

                @blk.tensor
                def _(te):
                    wait(te, t_c1b[("row", g)])
                    r = te.matmul(ps_sm[:, 0:128], ones_row_f[:], c1row[:],
                                  start=True, stop=True)
                    t_c1b_pe[("b", g)] = inc(r, "pe")

                @blk.vector
                def _(v):
                    v = SER(v, "dve")
                    wait(v, t_c1b_pe[("b", g)])
                    if g >= 1:
                        wait(v, t_hgev[(g - 1, HC // 4 - 1)])
                    r = v.tensor_copy(c1B[:], ps_sm[:, 0:128])
                    _ = r; t_c1b[g] = ("dve", cnt["dve"])

            def emit_front_mm(g):
                for hq in range(HC // 4):
                    bank = ps_h[hq % 2]

                    @blk.tensor
                    def _(te, hq=hq, bank=bank):
                        if hq == 0:
                            wait(te, t_n1t[g])
                            if preload_wt and g == 0:
                                wait(te, t_wt)
                        if hq >= 2:
                            wait(te, t_hgev[(g, hq - 2)])
                        elif g >= 1:
                            wait(te, t_hgev[(g - 1, HC // 4 + hq - 2)])
                        r = None
                        for slot in range(4):
                            hc = hq * 4 + slot
                            for dc in range(DC):
                                r = te.matmul(bank[:, slot * 128:slot * 128 + 128],
                                              w1t[:, dc * H + hc * 128:dc * H + (hc + 1) * 128],
                                              n1ts(g)[:, dc * 128:(dc + 1) * 128],
                                              start=(dc == 0), stop=(dc == DC - 1))
                        t_mm1[(g, hq)] = inc(r, "pe")

                    @blk.vector
                    def _(v, hq=hq, bank=bank):
                        v = SER(v, "dve")
                        wait(v, t_mm1[(g, hq)])
                        if hq == 0:
                            wait(v, t_c1b[g])
                        if g >= 3:
                            wait(v, t_convA[(g - 3, 4 * hq + 3)])
                            if hq == 0:
                                wait(v, t_convD[(g - 3, "bnd")])
                        h3 = hslot(g)[:, hq * 512:(hq + 1) * 512].rearrange(
                            "p (c t) -> p c t", t=128)
                        p3 = bank[:].rearrange("p (c t) -> p c t", t=128)
                        c3, _u = bass.broadcast_tensor_aps(
                            c1B[:].rearrange("p (c t) -> p c t", c=1), h3)
                        v.tensor_tensor(h3, p3, c3, ALU.mult)
                        t_hgev[(g, hq)] = ("dve", cnt["dve"])

            def emit_back_conv(g):
                last = (g == NT - 1)
                @blk.scalar
                def _(a):
                    a = SER(a, "act")
                    if g >= 1:
                        wait(a, t_sgn[g - 1])
                    for hc in range(HC):
                        if hc % 4 == 0:
                            wait(a, t_hgev[(g, hc // 4)])
                        a.activation(hp[:, hc * 128:(hc + 1) * 128],
                                     hslot(g)[:, hc * 128:(hc + 1) * 128],
                                     AF.Identity, bias=cb_t[:, hc:hc + 1],
                                     scale=w1c_t[:, hc:hc + 1])
                        t_convA[(g, hc)] = ("act", cnt["act"])
                        if hc == HC // 2 - 1:
                            t_convA[(g, "h0")] = t_convA[(g, hc)]

                for half in range(2):
                    @blk.vector
                    def _(v, half=half):
                        v = SER(v, "dve")
                        wait(v, last_pool[0])
                        if half == 0:
                            wait(v, t_convA[(g, "h0")])
                        else:
                            wait(v, t_convA[(g, HC - 1)])
                        co = half * (HC // 2)
                        sl = slice(half * 2048, (half + 1) * 2048)
                        h3 = hslot(g)[:, sl].rearrange("p (c t) -> p c t", t=128)
                        hp3 = hp[:, sl].rearrange("p (c t) -> p c t", t=128)
                        hw3 = scrA[:, :].rearrange("p (c t) -> p c t", t=128)
                        w0b, _u = bass.broadcast_tensor_aps(
                            w0c_t[:, co:co + HC // 2].rearrange("p (c o) -> p c o", o=1), h3)
                        w2b, _u = bass.broadcast_tensor_aps(
                            w2c_t[:, co:co + HC // 2].rearrange("p (c o) -> p c o", o=1), h3)
                        v.tensor_tensor(hw3, h3, w0b, ALU.mult)
                        v.tensor_tensor(hp3[:, :, 1:128], hp3[:, :, 1:128],
                                        hw3[:, :, 0:127], ALU.add)
                        v.tensor_tensor(hw3, h3, w2b, ALU.mult)
                        r = v.tensor_tensor(hp3[:, :, 0:127], hp3[:, :, 0:127],
                                            hw3[:, :, 1:128], ALU.add)
                        _ = r; t_convD[(g, half)] = ("dve", cnt["dve"])

                @blk.vector
                def _(v):
                    v = SER(v, "dve")
                    if not last:
                        wait(v, t_hgev[(g + 1, HC // 4 - 1)])
                    hview = lambda t_, tn: t_[:].rearrange("p (c t) -> p c t", t=128)[:, :, tn:tn + 1]
                    w3 = lambda w: w[:].rearrange("p (c o) -> p c o", o=1)
                    b3 = bnd[:].rearrange("p (c o) -> p c o", o=1)
                    if g > 0:
                        v.tensor_tensor(b3, hview(hslot(g - 1), 127), w3(w0c_t), ALU.mult)
                        v.tensor_tensor(hview(hp, 0), hview(hp, 0), b3, ALU.add)
                    if not last:
                        v.tensor_tensor(b3, hview(hslot(g + 1), 0), w3(w2c_t), ALU.mult)
                        v.tensor_tensor(hview(hp, 127), hview(hp, 127), b3, ALU.add)
                    t_convD[(g, "bnd")] = ("dve", cnt["dve"])

                @blk.scalar
                def _(a):
                    a = SER(a, "act")
                    wait(a, t_convD[(g, "bnd")])
                    r = a.activation(hp[:], hp[:], AF.Silu)
                    _ = r; t_silu[g] = ("act", cnt["act"])
                    wait(a, last_peel[0])
                    wait(a, last_pool[0])
                    r = a.activation(scrA[:, :], hp[:, 0:2048], AF.Abs)
                    _ = r; last_absA[0] = ("act", cnt["act"])
                    t_absq0[g] = last_absA[0]

                @blk.vector
                def _(v):
                    v = SER(v, "dve")
                    wait(v, t_silu[g])
                    wait(v, last_pool[0])
                    if g >= 1:
                        wait(v, t_s2[g - 1])
                    hpt = hp[:, :].rearrange("p (c t) -> p t c", t=128)
                    r = v.tensor_reduce(scrG[:, 1920:2048], hpt, AX.X, ALU.max,
                                        apply_absolute_value=True)
                    _ = r; t_qmx[g] = ("dve", cnt["dve"])
                    if g >= 1:
                        wait(v, t_mm2[(g - 1, 1)])
                    v.tensor_scalar(n2g[:, 0:2048], hp[:, 0:2048], 0.0, 2.0,
                                    ALU.is_ge, ALU.mult)
                    r = v.tensor_scalar(n2g[:, 2048:4096], hp[:, 2048:4096], 0.0, 2.0,
                                        ALU.is_ge, ALU.mult)
                    _ = r; t_sgn[g] = ("dve", cnt["dve"])

                @blk.tensor
                def _(te):
                    wait(te, t_qmx[g])
                    r = te.transpose(ps_sm[:, 128:256], scrG[:, 1920:2048], ident_f[:])
                    t_qtr[g] = inc(r, "pe")

                @blk.vector
                def _(v):
                    v = SER(v, "dve")
                    wait(v, t_qtr[g])
                    if g >= 2:
                        wait(v, t_ev2s[g - 2])
                    cs = slice(g % 2, g % 2 + 1)
                    v.tensor_reduce(s2v[:], ps_sm[:, 128:256], AX.X, ALU.max)
                    v.tensor_scalar(s2v[:], s2v[:], 1e-5, None, ALU.max)
                    v.tensor_scalar(c2v[:, cs], s2v[:], 0.5 / 127.0, None, ALU.mult)
                    v.tensor_tensor(c2v[:, cs], c2v[:, cs], m2b[:], ALU.mult)
                    v.reciprocal(s2v[:], s2v[:])
                    r = v.tensor_scalar(s2v[:], s2v[:], 127.0, None, ALU.mult)
                    _ = r; t_s2[g] = ("dve", cnt["dve"])

                @blk.tensor
                def _(te):
                    wait(te, t_s2[g])
                    r = te.transpose(ps_sm[0:1, 384:512], s2v[:], ident_f[:])
                    t_qtr2[g] = inc(r, "pe")

                @blk.vector
                def _(v):
                    v = SER(v, "dve")
                    wait(v, t_qtr2[g])
                    r = v.tensor_copy(scrR[0:1, 1920:2048], ps_sm[0:1, 384:512])
                    _ = r; t_s2b[("row", g)] = ("dve", cnt["dve"])

                @blk.tensor
                def _(te):
                    wait(te, t_s2b[("row", g)])
                    r = te.matmul(ps_sm[:, 256:384], ones_row_f[:],
                                  scrR[0:1, 1920:2048], start=True, stop=True)
                    t_qtr2[("b", g)] = inc(r, "pe")

                @blk.vector
                def _(v):
                    v = SER(v, "dve")
                    wait(v, t_qtr2[("b", g)])
                    wait(v, last_pool[0])
                    if g >= 1:
                        wait(v, t_n2a[(g - 1, 1)])
                    r = v.tensor_copy(s2B[:], ps_sm[:, 256:384])
                    _ = r; t_s2b[g] = ("dve", cnt["dve"])

            def emit_pool_peel(g, q, sl):
                @blk.gpsimd
                def _(gp):
                    gp = SER(gp, "pool")
                    wait(gp, last_absA[0])
                    wait(gp, t_s2b[g])
                    wait(gp, t_sgn[g])
                    pA = scrA[:, PC:2048]
                    pR = scrR[:, PC:2048]
                    pG = scrG[:, PC:2048]
                    a3 = pA.rearrange("p (c t) -> p c t", t=128)
                    b3, _u = bass.broadcast_tensor_aps(
                        s2B[:].rearrange("p (c t) -> p c t", c=1), a3)
                    gp.tensor_tensor(a3, a3, b3, ALU.mult)
                    src_r = pA
                    for c in FOLD_A:
                        gp.tensor_scalar(pG, src_r, c, None, ALU.subtract)
                        gp.tensor_scalar(pG, pG, 0.0, None, ALU.max)
                        gp.tensor_scalar(pG, pG, BIG, None, ALU.mult)
                        gp.tensor_scalar(pG, pG, c, None, ALU.min)
                        gp.tensor_tensor(pR, src_r, pG, ALU.subtract)
                        src_r = pR
                    gp.tensor_scalar(pG, pR, 0.5, None, ALU.subtract)
                    gp.tensor_scalar(pG, pG, 0.0, None, ALU.max)
                    gp.tensor_scalar(pG, pG, BIG, None, ALU.mult)
                    gp.tensor_scalar(pG, pG, 1.0, None, ALU.min)
                    gp.tensor_tensor(pG, pG, pA, ALU.add)
                    gp.tensor_tensor(pA, pG, pR, ALU.subtract)
                    n2p = n2g[:, sl.start + PC:sl.start + 2048]
                    gp.tensor_scalar(pG, n2p, -1.0, None, ALU.add)
                    r = gp.tensor_tensor(n2p, pG, pA, ALU.mult)
                    _ = r
                    t_n2p[(g, q)] = ("pool", cnt["pool"])
                    last_pool[0] = t_n2p[(g, q)]

            def _emit_q(g, q):
                sl = slice(q * 2048, (q + 1) * 2048)
                if q == 0:
                    last_absA[0] = t_absq0[g]
                else:
                    @blk.scalar
                    def _(a, q=q, sl=sl):
                        a = SER(a, "act")
                        wait(a, last_peel[0])
                        wait(a, last_pool[0])
                        r = a.activation(scrA[:, :], hp[:, sl], AF.Abs)
                        _ = r; last_absA[0] = ("act", cnt["act"])

                @blk.vector
                def _(v, q=q, sl=sl):
                    v = SER(v, "dve")
                    wait(v, last_absA[0])
                    wait(v, t_s2b[g])
                    a3 = scrA[:, 0:PC].rearrange("p (c t) -> p c t", t=128)
                    b3, _u = bass.broadcast_tensor_aps(
                        s2B[:].rearrange("p (c t) -> p c t", c=1), a3)
                    r = v.tensor_tensor(a3, a3, b3, ALU.mult)
                    _ = r; t_n2a[(g, q)] = ("dve", cnt["dve"])

                emit_pool_peel(g, q, sl)
                t_n2[(g, q)] = emit_peel_core(blk, None,
                                              n2g[:, sl.start:sl.start + PC],
                                              PC, sign_stored=True)

            def emit_back_restA(g):
                _emit_q(g, 0)
                @blk.scalar
                def _(a):
                    a = SER(a, "act")
                    wait(a, last_peel[0])
                    wait(a, last_pool[0])
                    r = a.activation(scrA[:, :], hp[:, 2048:4096], AF.Abs)
                    _ = r; last_absA[0] = ("act", cnt["act"])
                    t_absq0[(g, 1)] = last_absA[0]

            def emit_back_restB(g):
                last_absA[0] = t_absq0[(g, 1)]
                sl = slice(2048, 4096)

                @blk.vector
                def _(v):
                    v = SER(v, "dve")
                    wait(v, last_absA[0])
                    wait(v, t_s2b[g])
                    a3 = scrA[:, 0:PC].rearrange("p (c t) -> p c t", t=128)
                    b3, _u = bass.broadcast_tensor_aps(
                        s2B[:].rearrange("p (c t) -> p c t", c=1), a3)
                    r = v.tensor_tensor(a3, a3, b3, ALU.mult)
                    _ = r; t_n2a[(g, 1)] = ("dve", cnt["dve"])

                emit_pool_peel(g, 1, sl)
                t_n2[(g, 1)] = emit_peel_core(blk, None,
                                              n2g[:, sl.start:sl.start + PC],
                                              PC, sign_stored=True)

                for dh in range(2):
                    @blk.tensor
                    def _(te, dh=dh):
                        wait(te, t_n2[(g, 1)])
                        wait(te, t_n2p[(g, 1)])
                        wait(te, t_n2p[(g, 0)])
                        if g >= 1 and dh == 0:
                            wait(te, t_ev2s[g - 1])
                        r = None
                        for hc in range(HC):
                            r = te.matmul(ps_o[dh][:],
                                          n2g[:, hc * 128:(hc + 1) * 128],
                                          w2t[:, hc * D + dh * 512:hc * D + (dh + 1) * 512],
                                          start=(hc == 0), stop=(hc == HC - 1))
                        t_mm2[(g, dh)] = inc(r, "pe")

            def emit_evst(g):
                @blk.vector
                def _(v):
                    v = SER(v, "dve")
                    wait(v, t_mm2[(g, 1)])
                    if g >= 2:
                        wait(v, t_st[g - 2])
                    else:
                        for _tk in (wait_src_st or []):
                            wait(v, _tk)
                    r = None
                    for dh in range(2):
                        r = v.scalar_tensor_tensor(oslot(g)[:, dh * 512:(dh + 1) * 512],
                                                   ps_o[dh][:], c2v[:, g % 2:g % 2 + 1],
                                                   xslot(g)[:, dh * 512:(dh + 1) * 512],
                                                   ALU.mult, ALU.add)
                    _ = r; t_ev2s[g] = ("dve", cnt["dve"])

                @blk.gpsimd
                def _(gp):
                    wait(gp, t_ev2s[g])
                    r = gp.dma_start(out=dst[g * 128:(g + 1) * 128, :], in_=oslot(g))
                    t_st[g] = inc(r, "st%d" % (g % 2), 16)

            for it in range(NT + 2):
                if it == 0:
                    emit_front_a(0)
                    emit_front_mm(0)
                if 1 <= it < NT + 1:
                    emit_back_conv(it - 1)
                if 0 <= it - 2 < NT:
                    emit_evst(it - 2)
                if it + 1 < NT:
                    emit_front_a(it + 1)
                if 1 <= it < NT + 1:
                    emit_back_restA(it - 1)
                if it + 1 < NT:
                    emit_front_mm(it + 1)
                if 1 <= it < NT + 1:
                    emit_back_restB(it - 1)

        fctx.close()
        return [("st0", cnt["st0"]), ("st1", cnt["st1"])]

    def attn_phase(src, dst, wait_src_st):
        actx = contextlib.ExitStack()
        asb = lambda name, shp, dt: actx.enter_context(nc.sbuf_tensor(name, shp, dt))
        aps = lambda name, shp, dt: actx.enter_context(nc.psum_tensor(name, shp, dt))

        xnT = asb("xnT", [128, DC * S], BF16)
        kT = asb("kT", [128, DC * S], BF16)
        v_sb = asb("v_sb", [128, NT * D], BF16)
        wb0 = asb("wb0", [128, DC * D], BF16)
        wb1 = asb("wb1", [128, DC * D], BF16)
        stgA = asb("stgA", [128, 4 * D], F32)
        aslot = lambda s: stgA[:, (s % 4) * D:(s % 4) * D + D]
        xnb = asb("xnb", [128, D], BF16)
        qTc = asb("qTc", [128, D], BF16)
        pscr = asb("pscr", [128, S], F32)
        pbf = asb("pbf", [128, S], BF16)
        ptb = asb("ptb", [128, S], BF16)
        ybf = asb("ybf", [128, D], BF16)
        ytb = asb("ytb", [128, D], BF16)
        oscr = asb("oscr", [128, D], F32)
        rmx = asb("rmx", [128, 4], F32)
        nrm = asb("nrm", [128, 1], F32)
        ssum = asb("ssum", [128, 1], F32)
        ps_sc = [aps("ps_sc%d" % i, [128, 512], F32) for i in range(4)]
        ps_ax = [aps("ps_ax%d" % i, [128, 512], F32) for i in range(2)]
        ps_tr = aps("ps_tr", [128, 512], BF16)
        ps_tr2 = aps("ps_tr2", [128, 512], BF16)

        with nc.Block("attn") as blk:
            @blk.vector
            def _(v):
                wait(v, t_wcache)
                wait(v, last_pool[0])

            @blk.scalar
            def _(a):
                wait(a, t_wcache)
                wait(a, last_pool[0])

            @blk.sync
            def _(sy):
                wait(sy, t_wcache)

            @blk.gpsimd
            def _(gp):
                wait(gp, t_wcache)

            t_wk, t_wkc = {}, {}
            for c in range(DC):
                @blk.sync
                def _(sy, c=c):
                    if c == 0:
                        for _tk in (wait_src_st or []):
                            wait(sy, _tk)
                    if c >= 4:
                        wait(sy, t_wkc[c - 4])
                    r = sy.dma_start(out=aslot(c), in_=wk_h[c * 128:(c + 1) * 128, :])
                    t_wk[c] = inc(r, "lw%d" % (c % 4), 16)

                @blk.gpsimd
                def _(gp, c=c):
                    gp = SER(gp, "pool")
                    if c == 0:
                        for _tk in (wait_src_st or []):
                            wait(gp, _tk)
                    wait(gp, t_wk[c])
                    r = gp.tensor_copy(wb0[:, c * D:(c + 1) * D], aslot(c))
                    _ = r; t_wkc[c] = ("pool", cnt["pool"])

            t_lds, t_sq, t_red = {}, {}, {}
            for g in range(NT):
                @blk.sync
                def _(sy, g=g):
                    if g == 0:
                        for _tk in (wait_src_st or []):
                            wait(sy, _tk)
                    if g >= XS:
                        wait(sy, t_red[g - XS])
                        wait(sy, t_sq[g - XS])
                    r = sy.dma_start(out=xslot(g), in_=src[g * 128:(g + 1) * 128, :])
                    t_lds[g] = inc(r, "ldx%d" % (g % XS), 16)

                @blk.scalar
                def _(a, g=g):
                    a = SER(a, "act")
                    wait(a, t_lds[g])
                    r = a.activation(oscr[:], xslot(g), AF.Square,
                                     accum_out=st_var[:, g:g + 1])
                    _ = r; t_sq[g] = ("act", cnt["act"])

                @blk.vector
                def _(v, g=g):
                    v = SER(v, "dve")
                    wait(v, t_lds[g])
                    r = v.tensor_reduce(st_m[:, g:g + 1], xslot(g), AX.X, ALU.add)
                    _ = r; t_red[g] = ("dve", cnt["dve"])

            @blk.vector
            def _(v):
                v = SER(v, "dve")
                wait(v, t_red[NT - 1])
                wait(v, t_sq[NT - 1])
                v.tensor_scalar(st_m[:], st_m[:], 1.0 / D, None, ALU.mult)
                v.tensor_scalar(st_var[:], st_var[:], 1.0 / D, None, ALU.mult)
                v.tensor_tensor(st_t[:], st_m[:], st_m[:], ALU.mult)
                r = v.tensor_tensor(st_var[:], st_var[:], st_t[:], ALU.subtract)
                _ = ("dve", cnt["dve"])
            t_var = ("dve", cnt["dve"])

            @blk.scalar
            def _(a):
                a = SER(a, "act")
                wait(a, t_var)
                r = a.activation(st_rstd[:], st_var[:], AF.Sqrt, bias=CB[1e-5], scale=1.0)
                _ = ("act", cnt["act"])
            t_sd = ("act", cnt["act"])

            @blk.vector
            def _(v):
                v = SER(v, "dve")
                wait(v, t_sd)
                v.reciprocal(st_rstd[:], st_rstd[:])
                r = v.scalar_tensor_tensor(st_B[:], st_m[:], -1.0, st_rstd[:],
                                           ALU.mult, ALU.mult)
                _ = ("dve", cnt["dve"])
            t_ab = ("dve", cnt["dve"])

            t_wv, t_wvc = {}, {}
            for c in range(DC):
                @blk.sync
                def _(sy, c=c):
                    if c < 4:
                        wait(sy, t_wkc[c + 4])
                    else:
                        wait(sy, t_wvc[c - 4])
                    r = sy.dma_start(out=aslot(c), in_=wv_h[c * 128:(c + 1) * 128, :])
                    t_wv[c] = inc(r, "lw%d" % (c % 4), 16)

                @blk.gpsimd
                def _(gp, c=c):
                    gp = SER(gp, "pool")
                    wait(gp, t_wv[c])
                    r = gp.tensor_copy(wb1[:, c * D:(c + 1) * D], aslot(c))
                    _ = r; t_wvc[c] = ("pool", cnt["pool"])

            t_ldx2, t_xn, t_xtp, t_xnt = {}, {}, {}, {}
            for g in range(NT):
                @blk.sync
                def _(sy, g=g):
                    if g < XS:
                        wait(sy, t_sq[NT - 1])
                        wait(sy, t_red[NT - 1])
                    else:
                        wait(sy, t_xn[g - XS])
                    r = sy.dma_start(out=xslot(g), in_=src[g * 128:(g + 1) * 128, :])
                    t_ldx2[g] = inc(r, "ldx%d" % (g % XS), 16)

                @blk.scalar
                def _(a, g=g):
                    a = SER(a, "act")
                    wait(a, t_ldx2[g])
                    if g == 0:
                        wait(a, t_ab)
                    else:
                        wait(a, t_xtp[g - 1])
                    r = a.activation(xnb[:], xslot(g), AF.Identity,
                                     bias=st_B[:, g:g + 1], scale=st_rstd[:, g:g + 1])
                    _ = r; t_xn[g] = ("act", cnt["act"])

                @blk.tensor
                def _(te, g=g):
                    wait(te, t_xn[g])
                    if g >= 1:
                        wait(te, t_xnt[g - 1])
                    r = None
                    for dc in range(DC):
                        pt = ps_tr if dc < 4 else ps_tr2
                        r = te.transpose(pt[:, (dc % 4) * 128:(dc % 4) * 128 + 128],
                                         xnb[:, dc * 128:(dc + 1) * 128], ident_b[:])
                    t_xtp[g] = inc(r, "pe")

                @blk.vector
                def _(v, g=g):
                    v = SER(v, "dve")
                    wait(v, t_xtp[g])
                    xv = xnT[:].rearrange("p (c t) -> p c t", t=S)[:, :, g * 128:(g + 1) * 128]
                    v.tensor_copy(xv[:, 0:4, :], ps_tr[:].rearrange("p (c t) -> p c t", t=128))
                    v.tensor_copy(xv[:, 4:8, :], ps_tr2[:].rearrange("p (c t) -> p c t", t=128))
                    t_xnt[g] = ("dve", cnt["dve"])

            if ATTN_CUT == "ln":
                actx.close()
                return None
            kv_banks = [ps_ax[0], ps_ax[1], ps_sc[0], ps_sc[1]]
            t_kmm, t_kev = {}, {}
            for jc in range(DC):
                for sp in range(4):
                    i = jc * 4 + sp
                    bank = kv_banks[i % 4]

                    @blk.tensor
                    def _(te, jc=jc, sp=sp, i=i, bank=bank):
                        if i == 0:
                            wait(te, t_wkc[DC - 1])
                            wait(te, t_xnt[NT - 1])
                        if i >= 4:
                            wait(te, t_kev[i - 4])
                        r = None
                        for dc in range(DC):
                            r = te.matmul(bank[:],
                                          wb0[:, dc * D + jc * 128:dc * D + (jc + 1) * 128],
                                          xnT[:, dc * S + sp * 512:dc * S + (sp + 1) * 512],
                                          start=(dc == 0), stop=(dc == DC - 1))
                        t_kmm[i] = inc(r, "pe")

                @blk.scalar
                def _(a, jc=jc):
                    a = SER(a, "act")
                    for sp in range(4):
                        i = jc * 4 + sp
                        wait(a, t_kmm[i])
                        a.activation(kT[:, jc * S + sp * 512:jc * S + (sp + 1) * 512],
                                     kv_banks[i % 4][:], AF.Relu,
                                     bias=bk_t[:, jc:jc + 1], scale=1.0)
                        t_kev[i] = ("act", cnt["act"])

            t_vmm, t_vev = {}, {}
            for tc in range(NT):
                for jh in range(2):
                    i = tc * 2 + jh
                    bank = kv_banks[i % 4]

                    @blk.tensor
                    def _(te, tc=tc, jh=jh, i=i, bank=bank):
                        if i == 0:
                            wait(te, t_wvc[DC - 1])
                            wait(te, t_kev[DC * 4 - 1])
                        if i >= 4:
                            wait(te, t_vev[i - 4])
                        for dc in range(DC):
                            te.matmul(bank[:],
                                      xnT[:, dc * S + tc * 128:dc * S + (tc + 1) * 128],
                                      wb1[:, dc * D + jh * 512:dc * D + (jh + 1) * 512],
                                      start=(dc == 0), stop=False)
                        r = te.matmul(bank[:], ones_row_b[:],
                                      bv_row[:, jh * 512:(jh + 1) * 512],
                                      start=False, stop=True)
                        t_vmm[i] = inc(r, "pe")

                if tc % 2 == 1:
                    @blk.scalar
                    def _(a, tc=tc):
                        a = SER(a, "act")
                        for i in range((tc - 1) * 2, (tc + 1) * 2):
                            wait(a, t_vmm[i])
                            a.activation(v_sb[:, (i // 2) * D + (i % 2) * 512:
                                               (i // 2) * D + (i % 2 + 1) * 512],
                                         kv_banks[i % 4][:], AF.Relu)
                            t_vev[i] = ("act", cnt["act"])

            if ATTN_CUT == "kv":
                actx.close()
                return None
            t_wq, t_wqc, t_wo, t_woc = {}, {}, {}, {}
            for c in range(DC):
                @blk.sync
                def _(sy, c=c):
                    if c < 4:
                        wait(sy, t_wvc[c + 4])
                    else:
                        wait(sy, t_wqc[c - 4])
                    r = sy.dma_start(out=aslot(c), in_=wq_h[c * 128:(c + 1) * 128, :])
                    t_wq[c] = inc(r, "lw%d" % (c % 4), 16)

                @blk.gpsimd
                def _(gp, c=c):
                    gp = SER(gp, "pool")
                    wait(gp, t_wq[c])
                    wait(gp, t_kev[DC * 4 - 1])
                    r = gp.tensor_copy(wb0[:, c * D:(c + 1) * D], aslot(c))
                    _ = r; t_wqc[c] = ("pool", cnt["pool"])

            for c in range(DC):
                @blk.sync
                def _(sy, c=c):
                    if c < 4:
                        wait(sy, t_wqc[c + 4])
                    else:
                        wait(sy, t_woc[c - 4])
                    r = sy.dma_start(out=aslot(c), in_=wo_h[c * 128:(c + 1) * 128, :])
                    t_wo[c] = inc(r, "lw%d" % (c % 4), 16)

                @blk.gpsimd
                def _(gp, c=c):
                    gp = SER(gp, "pool")
                    wait(gp, t_wo[c])
                    wait(gp, t_vev[NT * 2 - 1])
                    r = gp.tensor_copy(wb1[:, c * D:(c + 1) * D], aslot(c))
                    _ = r; t_woc[c] = ("pool", cnt["pool"])

            if ATTN_CUT == "wq":
                actx.close()
                return None
            t_q, t_qe = {}, {}
            t_sc, t_mx, t_exp, t_pb = {}, {}, {}, {}
            t_pt, t_ptd = {}, {}
            t_ymm, t_yev, t_ytp, t_ytd = {}, {}, {}, {}
            t_omm, t_oev, t_res, t_xst, t_xl3 = {}, {}, {}, {}, {}

            for i in range(NT):
                W = (i + 1) * 128
                nb = (W + 511) // 512

                if ATTN_CUT not in ("", "full"):
                    pass
                else:
                    @blk.sync
                    def _(sy, i=i):
                        if i < XS:
                            wait(sy, t_xn[NT - 1])
                        elif (i - XS) in t_res:
                            wait(sy, t_res[i - XS])
                        r = sy.dma_start(out=xslot(i), in_=src[i * 128:(i + 1) * 128, :])
                        t_xl3[i] = inc(r, "ldx%d" % (i % XS), 16)

                if ATTN_CUT == "xl":
                    continue
                qreg = lambda jc: ps_ax[jc % 2][:, (jc // 2) * 128:(jc // 2 + 1) * 128]
                for jc in range(DC):
                    @blk.tensor
                    def _(te, jc=jc, i=i):
                        if i == 0 and jc == 0:
                            wait(te, t_woc[DC - 1])
                            wait(te, t_vev[NT * 2 - 1])
                        if jc == 0 and i >= 1:
                            if (i - 1) in t_oev:
                                wait(te, t_oev[i - 1])
                            else:
                                wait(te, t_qe[(i - 1, DC - 1)])
                        r = None
                        for dc in range(DC):
                            r = te.matmul(qreg(jc),
                                          wb0[:, dc * D + jc * 128:dc * D + (jc + 1) * 128],
                                          xnT[:, dc * S + i * 128:dc * S + (i + 1) * 128],
                                          start=(dc == 0), stop=(dc == DC - 1))
                        t_q[(i, jc)] = inc(r, "pe")

                @blk.scalar
                def _(a, i=i):
                    a = SER(a, "act")
                    for jc in range(DC):
                        wait(a, t_q[(i, jc)])
                    for jc in range(DC):
                        a.activation(qTc[:, jc * 128:(jc + 1) * 128], qreg(jc),
                                     AF.Relu, bias=bq_t[:, jc:jc + 1], scale=1.0)
                        t_qe[(i, jc)] = ("act", cnt["act"])

                if ATTN_CUT == "q":
                    continue
                for tkb in range(nb):
                    wdt = min(512, W - tkb * 512)

                    @blk.tensor
                    def _(te, tkb=tkb, wdt=wdt, i=i):
                        if tkb == 0:
                            wait(te, t_qe[(i, DC - 1)])
                            if i >= 1:
                                wait(te, t_mx[i - 1])
                                if (i - 1) in t_yev:
                                    wait(te, t_yev[i - 1])
                                else:
                                    wait(te, t_exp[(i - 1, (i * 128 + 511) // 512 - 1)])
                        r = None
                        for jc in range(DC):
                            r = te.matmul(ps_sc[tkb][:, 0:wdt],
                                          qTc[:, jc * 128:(jc + 1) * 128],
                                          kT[:, jc * S + tkb * 512:jc * S + tkb * 512 + wdt],
                                          start=(jc == 0), stop=(jc == DC - 1))
                        t_sc[(i, tkb)] = inc(r, "pe")

                @blk.vector
                def _(v, i=i, nb=nb, W=W):
                    v = SER(v, "dve")
                    for tkb in range(nb):
                        wdt = min(512, W - tkb * 512)
                        wait(v, t_sc[(i, tkb)])
                        v.tensor_reduce(rmx[:, tkb:tkb + 1], ps_sc[tkb][:, 0:wdt],
                                        AX.X, ALU.max)
                    if i >= 1:
                        wait(v, t_exp[(i - 1, (i * 128 + 511) // 512 - 1)])
                    v.tensor_reduce(nrm[:], rmx[:, 0:nb], AX.X, ALU.max)
                    r = v.tensor_scalar(nrm[:], nrm[:], -1.0 / 32.0, None, ALU.mult)
                    _ = r; t_mx[i] = ("dve", cnt["dve"])

                @blk.scalar
                def _(a, i=i, nb=nb, W=W):
                    a = SER(a, "act")
                    wait(a, t_mx[i])
                    for tkb in range(nb):
                        wdt = min(512, W - tkb * 512)
                        a.activation(pscr[:, tkb * 512:tkb * 512 + wdt],
                                     ps_sc[tkb][:, 0:wdt], AF.Exp,
                                     bias=nrm[:], scale=1.0 / 32.0)
                        t_exp[(i, tkb)] = ("act", cnt["act"])

                @blk.vector
                def _(v, i=i, nb=nb, W=W):
                    v = SER(v, "dve")
                    wait(v, t_exp[(i, nb - 1)])
                    if i >= 1 and (i - 1, i - 1) in t_pt:
                        wait(v, t_pt[(i - 1, i - 1)])
                    v.tensor_tensor(pscr[:, i * 128:W], pscr[:, i * 128:W], m01_b[:], ALU.mult)
                    v.tensor_reduce(ssum[:], pscr[:, 0:W], AX.X, ALU.add)
                    v.reciprocal(ssum[:], ssum[:])
                    r = v.tensor_scalar(pbf[:, 0:W], pscr[:, 0:W], ssum[:], None, ALU.mult)
                    _ = r; t_pb[i] = ("dve", cnt["dve"])

                if ATTN_CUT == "sc":
                    continue
                nsb = (i + 8) // 8
                for sb in range(nsb):
                    lo, hi = sb * 8, min(i + 1, sb * 8 + 8)

                    @blk.tensor
                    def _(te, sb=sb, lo=lo, hi=hi, i=i):
                        if sb == 0:
                            wait(te, t_pb[i])
                            if i >= 1:
                                if (i - 1) in t_ytd:
                                    wait(te, t_ytd[i - 1])
                                else:
                                    wait(te, t_ptd.get((i - 1, (i + 7) // 8 - 1), 0))
                        else:
                            wait(te, t_ptd[(i, sb - 1)])
                        r = None
                        for tkc in range(lo, hi):
                            pt = ps_tr if (tkc % 8) < 4 else ps_tr2
                            r = te.transpose(pt[:, (tkc % 4) * 128:(tkc % 4) * 128 + 128],
                                             pbf[:, tkc * 128:(tkc + 1) * 128], ident_b[:])
                        t_pt[(i, sb)] = inc(r, "pe")

                    @blk.vector
                    def _(v, sb=sb, lo=lo, hi=hi, i=i):
                        v = SER(v, "dve")
                        wait(v, t_pt[(i, sb)])
                        n1 = min(hi - lo, 4)
                        v.tensor_copy(ptb[:, lo * 128:(lo + n1) * 128], ps_tr[:, 0:n1 * 128])
                        if hi - lo > 4:
                            n2 = hi - lo - 4
                            v.tensor_copy(ptb[:, (lo + 4) * 128:hi * 128], ps_tr2[:, 0:n2 * 128])
                        t_ptd[(i, sb)] = ("dve", cnt["dve"])

                if ATTN_CUT == "pt":
                    continue
                for dh in range(2):
                    @blk.tensor
                    def _(te, dh=dh, i=i):
                        if dh == 0:
                            wait(te, t_ptd[(i, (i + 8) // 8 - 1)])
                        r = None
                        for tkc in range(i + 1):
                            r = te.matmul(ps_sc[dh][:],
                                          ptb[:, tkc * 128:(tkc + 1) * 128],
                                          v_sb[:, tkc * D + dh * 512:tkc * D + (dh + 1) * 512],
                                          start=(tkc == 0), stop=(tkc == i))
                        t_ymm[(i, dh)] = inc(r, "pe")

                @blk.scalar
                def _(a, i=i):
                    a = SER(a, "act")
                    wait(a, t_ymm[(i, 1)])
                    a.activation(ybf[:, 0:512], ps_sc[0][:], AF.Relu)
                    r = a.activation(ybf[:, 512:1024], ps_sc[1][:], AF.Relu)
                    _ = r; t_yev[i] = ("act", cnt["act"])

                if ATTN_CUT == "pv":
                    continue
                @blk.tensor
                def _(te, i=i):
                    wait(te, t_yev[i])
                    wait(te, t_ptd[(i, (i + 8) // 8 - 1)])
                    r = None
                    for dc in range(DC):
                        pt = ps_tr if dc < 4 else ps_tr2
                        r = te.transpose(pt[:, (dc % 4) * 128:(dc % 4) * 128 + 128],
                                         ybf[:, dc * 128:(dc + 1) * 128], ident_b[:])
                    t_ytp[i] = inc(r, "pe")

                @blk.vector
                def _(v, i=i):
                    v = SER(v, "dve")
                    wait(v, t_ytp[i])
                    v.tensor_copy(ytb[:, 0:512], ps_tr[:])
                    v.tensor_copy(ytb[:, 512:1024], ps_tr2[:])
                    t_ytd[i] = ("dve", cnt["dve"])

                for jh in range(2):
                    @blk.tensor
                    def _(te, jh=jh, i=i):
                        wait(te, t_ytd[i])
                        for dc in range(DC):
                            te.matmul(ps_ax[jh][:],
                                      ytb[:, dc * 128:(dc + 1) * 128],
                                      wb1[:, dc * D + jh * 512:dc * D + (jh + 1) * 512],
                                      start=(dc == 0), stop=False)
                        r = te.matmul(ps_ax[jh][:], ones_row_b[:],
                                      bo_row[:, jh * 512:(jh + 1) * 512],
                                      start=False, stop=True)
                        t_omm[(i, jh)] = inc(r, "pe")

                @blk.scalar
                def _(a, i=i):
                    a = SER(a, "act")
                    wait(a, t_omm[(i, 1)])
                    a.activation(oscr[:, 0:512], ps_ax[0][:], AF.Relu)
                    r = a.activation(oscr[:, 512:1024], ps_ax[1][:], AF.Relu)
                    _ = r; t_oev[i] = ("act", cnt["act"])

                @blk.vector
                def _(v, i=i):
                    v = SER(v, "dve")
                    wait(v, t_oev[i])
                    wait(v, t_xl3[i])
                    if i >= 2:
                        wait(v, t_xst[i - 2])
                    else:
                        for _tk in (wait_src_st or []):
                            wait(v, _tk)
                    r = v.tensor_tensor(oslot(i), oscr[:], xslot(i), ALU.add)
                    _ = r; t_res[i] = ("dve", cnt["dve"])

                @blk.gpsimd
                def _(gp, i=i):
                    wait(gp, t_res[i])
                    r = gp.dma_start(out=dst[i * 128:(i + 1) * 128, :], in_=oslot(i))
                    t_xst[i] = inc(r, "st%d" % (i % 2), 16)

        actx.close()
        return [("st0", cnt["st0"]), ("st1", cnt["st1"])]

    if phases == 4:
        wctx.close()
        attn_phase(x_h, y_h, None)
    elif phases == 1:
        ffn_phase("f1", x_h, y_h, w1t, w2t)
        wctx.close()
    elif phases == 2:
        t1 = ffn_phase("f1", x_h, sx1, w1t, w2t)
        wctx.close()
        attn_phase(sx1, y_h, t1)
    else:
        t1 = ffn_phase("f1", x_h, sx1, w1t, w2t)
        wctx.close()
        t2 = attn_phase(sx1, sx2, t1)
        w3ctx = contextlib.ExitStack()
        w1t3 = w3ctx.enter_context(nc.sbuf_tensor("w1t3", [128, DC * H], FP8))
        w2t3 = w3ctx.enter_context(nc.sbuf_tensor("w2t3", [128, HC * D], FP8))
        ffn_phase("f3", sx2, y_h, w1t3, w2t3, wait_src_st=t2, preload_wt=t_wcache)
        w3ctx.close()

    with nc.Block("fin") as blk:
        @blk.gpsimd
        def _(g):
            wait(g, "st0", cnt["st0"])
            wait(g, "st1", cnt["st1"])
            wait(g, "stw", cnt["stw"])

    gctx.close()
    return nc


def kernel(L0_attn_x, L0_attn_Wq, L0_attn_bq, L0_attn_Wk, L0_attn_bk,
           L0_attn_Wv, L0_attn_bv, L0_attn_Wo, L0_attn_bo,
           L0_ffn_w1, L0_ffn_conv_w, L0_ffn_conv_b, L0_ffn_w2):
    nc = build_nc(PHASES)
    x = np.ascontiguousarray(np.asarray(L0_attn_x, dtype=np.float32))
    com = {
        "wq": np.ascontiguousarray(np.asarray(L0_attn_Wq, np.float32)),
        "bq": np.ascontiguousarray(np.asarray(L0_attn_bq, np.float32)),
        "wk": np.ascontiguousarray(np.asarray(L0_attn_Wk, np.float32)),
        "bk": np.ascontiguousarray(np.asarray(L0_attn_bk, np.float32)),
        "wv": np.ascontiguousarray(np.asarray(L0_attn_Wv, np.float32)),
        "bv": np.ascontiguousarray(np.asarray(L0_attn_bv, np.float32)),
        "wo": np.ascontiguousarray(np.asarray(L0_attn_Wo, np.float32)),
        "bo": np.ascontiguousarray(np.asarray(L0_attn_bo, np.float32)),
        "w1": np.ascontiguousarray(np.asarray(L0_ffn_w1, np.float32)),
        "cw": np.ascontiguousarray(np.asarray(L0_ffn_conv_w, np.float32)),
        "cb": np.ascontiguousarray(np.asarray(L0_ffn_conv_b, np.float32)),
        "w2": np.ascontiguousarray(np.asarray(L0_ffn_w2, np.float32)),
    }
    in_maps = [dict(com, x=np.ascontiguousarray(x[b])) for b in range(N_CORES)]
    res = run_bass_kernel_spmd(nc, in_maps, core_ids=list(range(N_CORES)))
    return np.stack([np.asarray(r["y"], dtype=np.float32) for r in res.results], axis=0)


# version 19
# speedup vs baseline: 4.3611x; 4.3611x over previous
import contextlib
import numpy as np

import concourse.bass as bass
import concourse.mybir as mybir
from concourse.bass_utils import run_bass_kernel_spmd

F32 = mybir.dt.float32
BF16 = mybir.dt.bfloat16
FP8 = mybir.dt.float8e4
I32 = mybir.dt.int32
AF = mybir.ActivationFunctionType
ALU = mybir.AluOpType
AX = mybir.AxisListType

S, D, H = 2048, 1024, 4096
NT = S // 128
DC = D // 128
HC = H // 128
N_CORES = 8
PHASES = 3
ATTN_CUT = ""
FOLD_A = [64.0, 32.0, 16.0, 8.0, 4.0, 2.0, 1.0]
FOLD_C = [63.5, 32.0, 16.0, 8.0, 4.0, 2.0, 1.0]


def build_nc(phases=PHASES):
    nc = bass.Bass("TRN2", target_bir_lowering=False)

    x_h = nc.dram_tensor("x", [S, D], F32, kind="ExternalInput").ap()
    wq_h = nc.dram_tensor("wq", [D, D], F32, kind="ExternalInput").ap()
    bq_h = nc.dram_tensor("bq", [D], F32, kind="ExternalInput").ap()
    wk_h = nc.dram_tensor("wk", [D, D], F32, kind="ExternalInput").ap()
    bk_h = nc.dram_tensor("bk", [D], F32, kind="ExternalInput").ap()
    wv_h = nc.dram_tensor("wv", [D, D], F32, kind="ExternalInput").ap()
    bv_h = nc.dram_tensor("bv", [D], F32, kind="ExternalInput").ap()
    wo_h = nc.dram_tensor("wo", [D, D], F32, kind="ExternalInput").ap()
    bo_h = nc.dram_tensor("bo", [D], F32, kind="ExternalInput").ap()
    w1_h = nc.dram_tensor("w1", [H, D], F32, kind="ExternalInput").ap()
    cw_h = nc.dram_tensor("cw", [H, 1, 3], F32, kind="ExternalInput").ap()
    cb_h = nc.dram_tensor("cb", [H], F32, kind="ExternalInput").ap()
    w2_h = nc.dram_tensor("w2", [D, H], F32, kind="ExternalInput").ap()
    y_h = nc.dram_tensor("y", [S, D], F32, kind="ExternalOutput").ap()

    sx1 = nc.dram_tensor("sx1", [S, D], F32, kind="Internal").ap()
    sx2 = nc.dram_tensor("sx2", [S, D], F32, kind="Internal").ap()
    sw1t = nc.dram_tensor("sw1t", [128, DC * H], FP8, kind="Internal").ap()
    sw2t = nc.dram_tensor("sw2t", [128, HC * D], FP8, kind="Internal").ap()

    gctx = contextlib.ExitStack()
    gsb = lambda name, shp, dt: gctx.enter_context(nc.sbuf_tensor(name, shp, dt))

    sems, cnt = {}, {}
    for nm in ("ldb", "lwt", "stw", "dve", "act", "pe", "pool",
               "ldx0", "ldx1", "ldx2", "ldx3", "lw0", "lw1", "lw2", "lw3",
               "st0", "st1"):
        sems[nm] = gctx.enter_context(nc.semaphore("s_" + nm))
        cnt[nm] = 0

    def inc(instr, nm, n=1):
        instr.then_inc(sems[nm], n)
        cnt[nm] += n
        return (nm, cnt[nm])

    _SER_OPS = {"tensor_scalar", "tensor_tensor", "scalar_tensor_tensor",
                "tensor_copy", "tensor_reduce", "reduce_max", "reduce_sum",
                "memset", "reciprocal", "activation", "copy", "add", "mul"}

    class SER:
        def __init__(self, eng, nm):
            self._e = eng
            self._nm = nm

        def __getattr__(self, name):
            f = getattr(self._e, name)
            if name in _SER_OPS:
                nm = self._nm

                def g(*a, **k):
                    if cnt[nm] > 0:
                        self._e.wait_ge(sems[nm], cnt[nm])
                    r = f(*a, **k)
                    inc(r, nm)
                    return r
                return g
            return f

    def wait(eng, tok, v=None):
        if tok == 0 or tok is None:
            return
        if v is None:
            nm, val = tok
        else:
            nm, val = tok, v
        if val > 0:
            eng.wait_ge(sems[nm], val)

    ident_b = gsb("ident_b", [128, 128], BF16)
    ident_f = gsb("ident_f", [128, 128], F32)
    m01_b = gsb("m01_b", [128, 128], BF16)
    ones_row_b = gsb("ones_row_b", [1, 128], BF16)
    ones_row_f = gsb("ones_row_f", [1, 128], F32)
    ones_col_f = gsb("ones_col_f", [128, 1], F32)
    io_t = gsb("io_t", [128, 128], I32)
    bq_t = gsb("bq_t", [128, DC], F32)
    bk_t = gsb("bk_t", [128, DC], F32)
    bv_row = gsb("bv_row", [1, D], BF16)
    bo_row = gsb("bo_row", [1, D], BF16)
    w0c_t = gsb("w0c_t", [128, HC], F32)
    w1c_t = gsb("w1c_t", [128, HC], F32)
    w2c_t = gsb("w2c_t", [128, HC], F32)
    cb_t = gsb("cb_t", [128, HC], F32)
    m1b = gsb("m1b", [128, 1], F32)
    m2b = gsb("m2b", [128, 1], F32)
    thrP1 = gsb("thrP1", [128, 1], F32)
    thrN1 = gsb("thrN1", [128, 1], F32)
    thrP2 = gsb("thrP2", [128, 1], F32)
    thrN2 = gsb("thrN2", [128, 1], F32)
    sc11 = gsb("sc11", [1, 4], F32)

    XS = 3
    xg = gsb("xg", [128, XS * D], F32)
    xslot = lambda s: xg[:, (s % XS) * D:(s % XS) * D + D]
    outx = gsb("outx", [128, 2 * D], F32)
    oslot = lambda s: outx[:, (s % 2) * D:(s % 2) * D + D]

    st_m = gsb("st_m", [128, NT], F32)
    st_var = gsb("st_var", [128, NT], F32)
    st_rstd = gsb("st_rstd", [128, NT], F32)
    st_mx = gsb("st_mx", [128, NT], F32)
    st_mn = gsb("st_mn", [128, NT], F32)
    st_A = gsb("st_A", [128, NT], F32)
    st_B = gsb("st_B", [128, NT], F32)
    st_c1 = gsb("st_c1", [128, NT], F32)
    st_t = gsb("st_t", [128, NT], F32)

    cst = gsb("cst", [128, 12], F32)
    CB = {1e-5: cst[:, 0:1], -63.5: cst[:, 1:2], -32.0: cst[:, 2:3], -16.0: cst[:, 3:4],
          -8.0: cst[:, 4:5], -4.0: cst[:, 5:6], -2.0: cst[:, 6:7], -1.0: cst[:, 7:8],
          -0.5: cst[:, 8:9]}
    SCR = {}

    class _ScrProxy:
        def __init__(self, key):
            self.key = key

        def __getitem__(self, idx):
            return SCR[self.key][idx]

    scrV = _ScrProxy("V")
    scrA = _ScrProxy("A")
    scrR = _ScrProxy("R")
    scrG = _ScrProxy("G")
    scrI = _ScrProxy("I")

    last_absA = [0]
    last_peel = [0]
    last_absA1 = [0]
    last_peel1 = [0]

    p1A = _ScrProxy("A1")
    p1R = _ScrProxy("R1")
    p1G = _ScrProxy("G1")

    def emit_peel_core(blk, sign_ap, out_ap, F, sA=scrA, sR=scrR, sG=scrG,
                       cell=last_peel, sign_stored=False):
        tk = {}

        @blk.vector
        def _(v):
            v = SER(v, "dve")
            for off in range(0, F, 512):
                v.tensor_copy(scrI[:, 0:512], sA[:, off:off + 512])
                v.tensor_copy(sA[:, off:off + 512], scrI[:, 0:512])
            if sign_stored:
                r = v.scalar_tensor_tensor(out_ap, out_ap, -1.0, sA[:, :F],
                                           ALU.add, ALU.mult)
            else:
                if callable(sign_ap):
                    sign_ap(v)
                else:
                    v.tensor_scalar(sR[:, :F], sign_ap, 0.0, 2.0, ALU.is_ge, ALU.mult)
                r = v.scalar_tensor_tensor(out_ap, sR[:, :F], -1.0, sA[:, :F],
                                           ALU.add, ALU.mult)
            _ = r; tk["out"] = ("dve", cnt["dve"])

        cell[0] = tk["out"]
        return tk["out"]

    def emit_peel(blk, v_ap, out_ap, F, act_wait=None):
        @blk.scalar
        def _(a):
            a = SER(a, "act")
            if act_wait is not None:
                wait(a, *act_wait)
            wait(a, last_peel[0])
            r = a.activation(scrA[:, :F], v_ap, AF.Abs)
            _ = r; last_absA[0] = ("act", cnt["act"])

        @blk.vector
        def _(v):
            v = SER(v, "dve")
            wait(v, last_absA[0])
        return emit_peel_core(blk, v_ap, out_ap, F)

    wctx = contextlib.ExitStack()
    w1t = wctx.enter_context(nc.sbuf_tensor("w1t", [128, DC * H], FP8))
    w2t = wctx.enter_context(nc.sbuf_tensor("w2t", [128, HC * D], FP8))

    pctx = contextlib.ExitStack()
    psb = lambda name, shp, dt: pctx.enter_context(nc.sbuf_tensor(name, shp, dt))
    pps = lambda name, shp, dt: pctx.enter_context(nc.psum_tensor(name, shp, dt))

    with nc.Block("prologue") as blk:
        stg = psb("stg", [128, 2 * H], F32)
        sslot = lambda s: stg[:, (s % 2) * H:(s % 2) * H + H]
        scrT = psb("scrT", [128, H], BF16)
        SCR["V"] = psb("pscrV", [128, 1024], F32)
        SCR["R"] = psb("pscrR", [128, 1024], F32)
        w1abs = psb("w1abs", [128, HC], F32)
        w2abs = psb("w2abs", [128, DC], F32)
        ps_s = pps("ps_s", [128, 512], F32)
        ps_t = pps("ps_t", [128, 512], BF16)
        ps_t2 = pps("ps_t2", [128, 512], BF16)
        ps_1 = pps("ps_1", [1, 128], F32)

        @blk.gpsimd
        def _(g):
            r = g.iota(io_t[:], [[1, 128]], base=0, channel_multiplier=-1)
            inc(r, "pool")
        t_iota = ("pool", cnt["pool"])

        @blk.vector
        def _(v):
            v = SER(v, "dve")
            wait(v, t_iota)
            v.tensor_scalar(ident_b[:], io_t[:], 0.0, None, ALU.is_equal)
            v.tensor_scalar(ident_f[:], io_t[:], 0.0, None, ALU.is_equal)
            v.tensor_scalar(m01_b[:], io_t[:], 0.0, None, ALU.is_le)
            v.memset(ones_row_b[:], 1.0)
            v.memset(ones_row_f[:], 1.0)
            for val, ap in CB.items():
                v.memset(ap, val)
            r = v.memset(ones_col_f[:], 1.0)
            _ = ("dve", cnt["dve"])
        t_const = ("dve", cnt["dve"])

        @blk.sync
        def _(sy):
            inc(sy.dma_start(out=stg[0:8, 0:128], in_=bq_h.rearrange("(c p) -> c p", c=8)), "ldb", 16)
            inc(sy.dma_start(out=stg[0:8, 128:256], in_=bk_h.rearrange("(c p) -> c p", c=8)), "ldb", 16)
            inc(sy.dma_start(out=stg[0:32, 256:384], in_=cb_h.rearrange("(c p) -> c p", c=32)), "ldb", 16)
            inc(sy.dma_start(out=stg[0:32, 384:768],
                             in_=cw_h.rearrange("h o k -> h (o k)").rearrange("(c p) k -> c (p k)", c=32)), "ldb", 16)
            inc(sy.dma_start(out=stg[0:1, 1152:1152 + D], in_=bv_h.rearrange("(o d) -> o d", o=1)), "ldb", 16)
            inc(sy.dma_start(out=stg[0:1, 2176:2176 + D], in_=bo_h.rearrange("(o d) -> o d", o=1)), "ldb", 16)
        t_bld = ("ldb", cnt["ldb"])

        @blk.vector
        def _(v):
            v = SER(v, "dve")
            wait(v, t_bld)
            taps = stg[0:32, 384:768].rearrange("c (p k) -> c k p", k=3)
            for k in range(3):
                v.tensor_copy(stg[0:32, 768 + k * 128:768 + (k + 1) * 128],
                              taps[:, k:k + 1, :].rearrange("c one p -> c (one p)"))
            v.tensor_copy(bv_row[:], stg[0:1, 1152:1152 + D])
            r = v.tensor_copy(bo_row[:], stg[0:1, 2176:2176 + D])
            _ = ("dve", cnt["dve"])
        t_bup = ("dve", cnt["dve"])

        @blk.tensor
        def _(te):
            wait(te, t_bup)
            te.transpose(ps_s[:, 0:8], stg[0:8, 0:128], ident_f[0:8, 0:8])
            te.transpose(ps_s[:, 8:16], stg[0:8, 128:256], ident_f[0:8, 0:8])
            te.transpose(ps_s[:, 16:48], stg[0:32, 256:384], ident_f[0:32, 0:32])
            r = None
            for k in range(3):
                r = te.transpose(ps_s[:, 48 + 32 * k:48 + 32 * (k + 1)],
                                 stg[0:32, 768 + k * 128:768 + (k + 1) * 128],
                                 ident_f[0:32, 0:32])
            inc(r, "pe")
        t_btr = ("pe", cnt["pe"])

        @blk.vector
        def _(v):
            v = SER(v, "dve")
            wait(v, t_btr)
            v.tensor_copy(bq_t[:], ps_s[:, 0:8])
            v.tensor_copy(bk_t[:], ps_s[:, 8:16])
            v.tensor_copy(cb_t[:], ps_s[:, 16:48])
            v.tensor_copy(w0c_t[:], ps_s[:, 48:80])
            v.tensor_copy(w1c_t[:], ps_s[:, 80:112])
            r = v.tensor_copy(w2c_t[:], ps_s[:, 112:144])
            _ = ("dve", cnt["dve"])
        t_bdone = ("dve", cnt["dve"])

        t_wld, t_wred = {}, {}
        for c in range(HC + DC):
            @blk.sync
            def _(sy, c=c):
                if c == 0:
                    wait(sy, t_bdone)
                if c >= 2:
                    wait(sy, t_wred[c - 2])
                if c < HC:
                    r = sy.dma_start(out=sslot(c)[:, 0:D], in_=w1_h[c * 128:(c + 1) * 128, :])
                else:
                    r = sy.dma_start(out=sslot(c), in_=w2_h[(c - HC) * 128:(c - HC + 1) * 128, :])
                t_wld[c] = inc(r, "lw%d" % (c % 2), 16)

            @blk.vector
            def _(v, c=c):
                v = SER(v, "dve")
                wait(v, t_wld[c])
                if c < HC:
                    r = v.tensor_reduce(w1abs[:, c:c + 1], sslot(c)[:, 0:D], AX.X,
                                        ALU.add, apply_absolute_value=True)
                else:
                    r = v.tensor_reduce(w2abs[:, c - HC:c - HC + 1], sslot(c), AX.X,
                                        ALU.add, apply_absolute_value=True)
                _ = r; t_wred[c] = ("dve", cnt["dve"])

        @blk.vector
        def _(v):
            v = SER(v, "dve")
            wait(v, t_wred[HC + DC - 1])
            v.tensor_reduce(st_t[:, 0:1], w1abs[:], AX.X, ALU.add)
            r = v.tensor_reduce(st_t[:, 1:2], w2abs[:], AX.X, ALU.add)
            _ = ("dve", cnt["dve"])
        t_wsum = ("dve", cnt["dve"])

        @blk.tensor
        def _(te):
            wait(te, t_wsum)
            r = te.matmul(ps_1[0:1, 0:2], ones_col_f[:], st_t[:, 0:2], start=True, stop=True)
            inc(r, "pe")
        t_msum = ("pe", cnt["pe"])

        @blk.vector
        def _(v):
            v = SER(v, "dve")
            wait(v, t_msum)
            r = v.tensor_scalar(sc11[0:1, 0:2], ps_1[0:1, 0:2], 1.0 / (H * D), 1e-5,
                                ALU.mult, ALU.max)
            _ = ("dve", cnt["dve"])
        t_m = ("dve", cnt["dve"])

        @blk.tensor
        def _(te):
            wait(te, t_m)
            r = te.matmul(ps_s[:, 256:258], ones_row_f[:], sc11[0:1, 0:2],
                          start=True, stop=True)
            inc(r, "pe")
        t_mb = ("pe", cnt["pe"])

        @blk.vector
        def _(v):
            v = SER(v, "dve")
            wait(v, t_mb)
            v.tensor_copy(m1b[:], ps_s[:, 256:257])
            v.tensor_copy(m2b[:], ps_s[:, 257:258])
            v.tensor_scalar(thrP1[:], m1b[:], 0.5, None, ALU.mult)
            v.tensor_scalar(thrN1[:], m1b[:], -0.5, None, ALU.mult)
            v.tensor_scalar(thrP2[:], m2b[:], 0.5, None, ALU.mult)
            r = v.tensor_scalar(thrN2[:], m2b[:], -0.5, None, ALU.mult)
            _ = ("dve", cnt["dve"])
        t_thr = ("dve", cnt["dve"])

        t_ld2, t_tern, t_tp, t_ev = {}, {}, {}, {}
        for c in range(HC):
            @blk.sync
            def _(sy, c=c):
                if c == 0:
                    wait(sy, t_thr)
                if c >= 2:
                    wait(sy, t_tern[c - 2])
                r = sy.dma_start(out=sslot(c)[:, 0:D], in_=w1_h[c * 128:(c + 1) * 128, :])
                t_ld2[c] = inc(r, "lw%d" % (c % 2), 16)

            @blk.vector
            def _(v, c=c):
                v = SER(v, "dve")
                wait(v, t_ld2[c])
                if c >= 1:
                    wait(v, t_tp[c - 1])
                v.tensor_scalar(scrV[:, 0:D], sslot(c)[:, 0:D], thrP1[:], None, ALU.is_ge)
                v.tensor_scalar(scrR[:, 0:D], sslot(c)[:, 0:D], thrN1[:], None, ALU.is_le)
                r = v.tensor_tensor(scrT[:, 0:D], scrV[:, 0:D], scrR[:, 0:D], ALU.subtract)
                _ = r; t_tern[c] = ("dve", cnt["dve"])

            @blk.tensor
            def _(te, c=c):
                wait(te, t_tern[c])
                if c >= 1:
                    wait(te, t_ev[c - 1])
                r = None
                for dc in range(DC):
                    pt = ps_t if dc < 4 else ps_t2
                    r = te.transpose(pt[:, (dc % 4) * 128:(dc % 4) * 128 + 128],
                                     scrT[:, dc * 128:(dc + 1) * 128], ident_b[:])
                t_tp[c] = inc(r, "pe")

            @blk.scalar
            def _(a, c=c):
                a = SER(a, "act")
                wait(a, t_tp[c])
                r = None
                for dc in range(DC):
                    pt = ps_t if dc < 4 else ps_t2
                    r = a.copy(w1t[:, dc * H + c * 128:dc * H + (c + 1) * 128],
                               pt[:, (dc % 4) * 128:(dc % 4) * 128 + 128])
                _ = r; t_ev[c] = ("act", cnt["act"])

        t_ld3, t_tern2, t_tp2, t_ev2 = {}, {}, {}, {}
        for c in range(DC):
            @blk.sync
            def _(sy, c=c):
                if c >= 2:
                    wait(sy, t_tern2[c - 2])
                else:
                    wait(sy, t_tern[HC - 2 + c])
                r = sy.dma_start(out=sslot(c), in_=w2_h[c * 128:(c + 1) * 128, :])
                t_ld3[c] = inc(r, "lw%d" % (c % 2), 16)

            @blk.vector
            def _(v, c=c):
                v = SER(v, "dve")
                wait(v, t_ld3[c])
                if c == 0:
                    wait(v, t_tp[HC - 1])
                else:
                    wait(v, t_tp2[(c - 1, 3)])
                r = None
                for q in range(4):
                    sl = slice(q * 1024, (q + 1) * 1024)
                    v.tensor_scalar(scrV[:, 0:D], sslot(c)[:, sl], thrP2[:], None, ALU.is_ge)
                    v.tensor_scalar(scrR[:, 0:D], sslot(c)[:, sl], thrN2[:], None, ALU.is_le)
                    r = v.tensor_tensor(scrT[:, sl], scrV[:, 0:D], scrR[:, 0:D], ALU.subtract)
                _ = r; t_tern2[c] = ("dve", cnt["dve"])

            for b in range(4):
                @blk.tensor
                def _(te, c=c, b=b):
                    wait(te, t_tern2[c])
                    prev = (c, b - 1) if b > 0 else ((c - 1, 3) if c > 0 else None)
                    if prev is not None:
                        wait(te, t_ev2[prev])
                    elif c == 0 and b == 0:
                        wait(te, t_ev[HC - 1])
                    r = None
                    for j in range(8):
                        hh = b * 8 + j
                        pt = ps_t if j < 4 else ps_t2
                        r = te.transpose(pt[:, (j % 4) * 128:(j % 4) * 128 + 128],
                                         scrT[:, hh * 128:(hh + 1) * 128], ident_b[:])
                    t_tp2[(c, b)] = inc(r, "pe")

                @blk.scalar
                def _(a, c=c, b=b):
                    a = SER(a, "act")
                    wait(a, t_tp2[(c, b)])
                    r = None
                    for j in range(8):
                        hh = b * 8 + j
                        pt = ps_t if j < 4 else ps_t2
                        r = a.copy(w2t[:, hh * D + c * 128:hh * D + (c + 1) * 128],
                                   pt[:, (j % 4) * 128:(j % 4) * 128 + 128])
                    _ = r; t_ev2[(c, b)] = ("act", cnt["act"])

        @blk.gpsimd
        def _(g):
            wait(g, t_ev[HC - 1])
            r = g.dma_start(out=sw1t, in_=w1t[:])
            inc(r, "stw", 16)
            wait(g, t_ev2[(DC - 1, 3)])
            r = g.dma_start(out=sw2t, in_=w2t[:])
            inc(r, "stw", 16)
        t_wcache = ("stw", cnt["stw"])

    pctx.close()

    def ffn_phase(tag, src, dst, w1t, w2t, wait_src_st=0, preload_wt=0):
        fctx = contextlib.ExitStack()
        fsb = lambda name, shp, dt: fctx.enter_context(nc.sbuf_tensor(name + "_" + tag, shp, dt))
        fps = lambda name, shp, dt: fctx.enter_context(nc.psum_tensor(name + "_" + tag, shp, dt))

        SCR["V"] = fsb("scrV", [128, 1024], F32)
        SCR["A1"] = fsb("scrA1", [128, 1024], F32)
        SCR["R1"] = fsb("scrR1", [128, 1024], F32)
        SCR["G1"] = SCR["V"]
        SCR["A"] = fsb("scrA", [128, 2048], F32)
        SCR["R"] = fsb("scrR", [128, 2048], F32)
        SCR["G"] = fsb("scrG", [128, 1152], F32)
        SCR["I"] = fsb("i32f", [128, 512], I32)
        hgB = fsb("hg", [128, 3 * H], F32)
        hslot = lambda g: hgB[:, (g % 3) * H:(g % 3) * H + H]
        hp = fsb("hp", [128, H], F32)
        n1g = fsb("n1g", [128, D], BF16)
        n1t = fsb("n1t", [128, 2 * D], BF16)
        n1ts = lambda g: n1t[:, (g % 2) * D:(g % 2) * D + D]
        n2g = fsb("n2g", [128, H], BF16)
        bnd = fsb("bnd", [128, HC], F32)
        c1B = fsb("c1B", [128, 128], F32)
        s2B = fsb("s2B", [128, 128], F32)
        c1row = fsb("c1row", [1, 128], F32)
        c2v = fsb("c2v", [128, 2], F32)
        s2v = fsb("s2v", [128, 1], F32)
        ps_h = [fps("ps_h%d" % i, [128, 512], F32) for i in range(2)]
        ps_o = [fps("ps_o%d" % i, [128, 512], F32) for i in range(2)]
        ps_tt = fps("ps_tt", [128, 512], BF16)
        ps_tt2 = fps("ps_tt2", [128, 512], BF16)
        ps_sm = fps("ps_sm", [128, 512], F32)

        with nc.Block("ffn_" + tag) as blk:
            if preload_wt:
                @blk.sync
                def _(sy):
                    wait(sy, preload_wt)
                    inc(sy.dma_start(out=w1t[:], in_=sw1t), "lwt", 16)
                    t_wt0 = inc(sy.dma_start(out=w2t[:], in_=sw2t), "lwt", 16)
            t_wt = ("lwt", cnt["lwt"])

            t_lds, t_sq, t_red = {}, {}, {}
            for g in range(NT):
                @blk.sync
                def _(sy, g=g):
                    if g == 0:
                        for _tk in (wait_src_st or []):
                            wait(sy, _tk)
                    if g >= XS:
                        wait(sy, t_red[g - XS])
                        wait(sy, t_sq[g - XS])
                    r = sy.dma_start(out=xslot(g), in_=src[g * 128:(g + 1) * 128, :])
                    t_lds[g] = inc(r, "ldx%d" % (g % XS), 16)

                @blk.scalar
                def _(a, g=g):
                    a = SER(a, "act")
                    wait(a, t_lds[g])
                    r = a.activation(scrV[:, 0:D] if g % 2 == 0 else scrG[:, 0:D],
                                     xslot(g), AF.Square, accum_out=st_var[:, g:g + 1])
                    _ = r; t_sq[g] = ("act", cnt["act"])

                @blk.vector
                def _(v, g=g):
                    v = SER(v, "dve")
                    wait(v, t_lds[g])
                    v.tensor_reduce(st_m[:, g:g + 1], xslot(g), AX.X, ALU.add)
                    v.tensor_reduce(st_mx[:, g:g + 1], xslot(g), AX.X, ALU.max)
                    r = v.tensor_reduce(st_mn[:, g:g + 1], xslot(g), AX.X, ALU.min)
                    _ = r; t_red[g] = ("dve", cnt["dve"])


            @blk.vector
            def _(v):
                v = SER(v, "dve")
                wait(v, t_red[NT - 1])
                wait(v, t_sq[NT - 1])
                v.tensor_scalar(st_m[:], st_m[:], 1.0 / D, None, ALU.mult)
                v.tensor_scalar(st_var[:], st_var[:], 1.0 / D, None, ALU.mult)
                v.tensor_tensor(st_t[:], st_m[:], st_m[:], ALU.mult)
                r = v.tensor_tensor(st_var[:], st_var[:], st_t[:], ALU.subtract)
                _ = ("dve", cnt["dve"])
            t_var = ("dve", cnt["dve"])

            @blk.scalar
            def _(a):
                a = SER(a, "act")
                wait(a, t_var)
                r = a.activation(st_rstd[:], st_var[:], AF.Sqrt, bias=CB[1e-5], scale=1.0)
                _ = ("act", cnt["act"])
            t_sd = ("act", cnt["act"])

            @blk.vector
            def _(v):
                v = SER(v, "dve")
                wait(v, t_sd)
                v.reciprocal(st_rstd[:], st_rstd[:])
                v.tensor_tensor(st_mx[:], st_mx[:], st_m[:], ALU.subtract)
                v.tensor_tensor(st_mn[:], st_m[:], st_mn[:], ALU.subtract)
                v.tensor_tensor(st_mx[:], st_mx[:], st_mn[:], ALU.max)
                v.tensor_tensor(st_mx[:], st_mx[:], st_rstd[:], ALU.mult)
                v.tensor_scalar(st_mx[:], st_mx[:], 1e-5, None, ALU.max)
                v.reciprocal(st_t[:], st_mx[:])
                v.tensor_scalar(st_t[:], st_t[:], 127.0, None, ALU.mult)
                v.tensor_tensor(st_A[:], st_rstd[:], st_t[:], ALU.mult)
                v.scalar_tensor_tensor(st_B[:], st_m[:], -1.0, st_A[:], ALU.mult, ALU.mult)
                v.tensor_scalar(st_c1[:], st_mx[:], 1.0 / 127.0, None, ALU.mult)
                r = None
                for g in range(NT):
                    r = v.tensor_tensor(st_c1[:, g:g + 1], st_c1[:, g:g + 1], m1b[:], ALU.mult)
                _ = ("dve", cnt["dve"])
            t_small = ("dve", cnt["dve"])

            t_ldx, t_v, t_n1, t_tp, t_n1t = {}, {}, {}, {}, {}
            t_c1b_pe, t_c1b = {}, {}
            t_mm1, t_hgev = {}, {}
            t_convA, t_convD, t_silu, t_qmx, t_qtr, t_s2, t_qtr2, t_s2b = {}, {}, {}, {}, {}, {}, {}, {}
            t_n2a, t_n2g, t_n2, t_absq0, t_sgn = {}, {}, {}, {}, {}
            t_mm2, t_ev2s, t_st = {}, {}, {}

            def emit_front_a(g):
                @blk.sync
                def _(sy):
                    if g < XS:
                        wait(sy, t_sq[NT - 1])
                        wait(sy, t_small)
                    else:
                        wait(sy, t_ev2s[g - XS])
                    r = sy.dma_start(out=xslot(g), in_=src[g * 128:(g + 1) * 128, :])
                    t_ldx[g] = inc(r, "ldx%d" % (g % XS), 16)

                @blk.scalar
                def _(a):
                    a = SER(a, "act")
                    wait(a, t_ldx[g])
                    wait(a, last_peel1[0])
                    r = a.activation(p1A[:, 0:D], xslot(g), AF.Abs,
                                     bias=st_B[:, g:g + 1], scale=st_A[:, g:g + 1])
                    _ = r; t_v[g] = ("act", cnt["act"])
                    last_absA1[0] = t_v[g]

                @blk.vector
                def _(v):
                    v = SER(v, "dve")
                    wait(v, last_absA1[0])

                t_n1[g] = emit_peel_core(
                    blk,
                    lambda v, g=g: v.tensor_scalar(p1R[:, 0:D], xslot(g),
                                                   st_m[:, g:g + 1], 2.0,
                                                   ALU.is_ge, ALU.mult),
                    n1g[:], D, sA=p1A, sR=p1R, sG=p1G, cell=last_peel1)

                @blk.tensor
                def _(te):
                    wait(te, t_n1[g])
                    if g >= 2:
                        wait(te, t_n1t[g - 2])
                    r = None
                    for dc in range(DC):
                        pt = ps_tt if dc < 4 else ps_tt2
                        r = te.transpose(pt[:, (dc % 4) * 128:(dc % 4) * 128 + 128],
                                         n1g[:, dc * 128:(dc + 1) * 128], ident_b[:])
                    t_tp[g] = inc(r, "pe")

                @blk.vector
                def _(v):
                    v = SER(v, "dve")
                    wait(v, t_tp[g])
                    if g >= 2:
                        wait(v, t_mm1[(g - 2, HC // 4 - 1)])
                    v.tensor_copy(n1ts(g)[:, 0:512], ps_tt[:])
                    v.tensor_copy(n1ts(g)[:, 512:1024], ps_tt2[:])
                    t_n1t[g] = ("dve", cnt["dve"])

                @blk.tensor
                def _(te):
                    wait(te, t_small)
                    if g >= 1:
                        wait(te, t_c1b[g - 1])
                    r = te.transpose(ps_sm[0:1, 128:256], st_c1[:, g:g + 1], ident_f[:])
                    t_c1b_pe[g] = inc(r, "pe")

                @blk.vector
                def _(v):
                    v = SER(v, "dve")
                    wait(v, t_c1b_pe[g])
                    r = v.tensor_copy(c1row[:], ps_sm[0:1, 128:256])
                    _ = r; t_c1b[("row", g)] = ("dve", cnt["dve"])

                @blk.tensor
                def _(te):
                    wait(te, t_c1b[("row", g)])
                    r = te.matmul(ps_sm[:, 0:128], ones_row_f[:], c1row[:],
                                  start=True, stop=True)
                    t_c1b_pe[("b", g)] = inc(r, "pe")

                @blk.vector
                def _(v):
                    v = SER(v, "dve")
                    wait(v, t_c1b_pe[("b", g)])
                    if g >= 1:
                        wait(v, t_hgev[(g - 1, HC // 4 - 1)])
                    r = v.tensor_copy(c1B[:], ps_sm[:, 0:128])
                    _ = r; t_c1b[g] = ("dve", cnt["dve"])

            def emit_front_mm(g):
                for hq in range(HC // 4):
                    bank = ps_h[hq % 2]

                    @blk.tensor
                    def _(te, hq=hq, bank=bank):
                        if hq == 0:
                            wait(te, t_n1t[g])
                            if preload_wt and g == 0:
                                wait(te, t_wt)
                        if hq >= 2:
                            wait(te, t_hgev[(g, hq - 2)])
                        elif g >= 1:
                            wait(te, t_hgev[(g - 1, HC // 4 + hq - 2)])
                        r = None
                        for slot in range(4):
                            hc = hq * 4 + slot
                            for dc in range(DC):
                                r = te.matmul(bank[:, slot * 128:slot * 128 + 128],
                                              w1t[:, dc * H + hc * 128:dc * H + (hc + 1) * 128],
                                              n1ts(g)[:, dc * 128:(dc + 1) * 128],
                                              start=(dc == 0), stop=(dc == DC - 1))
                        t_mm1[(g, hq)] = inc(r, "pe")

                    @blk.vector
                    def _(v, hq=hq, bank=bank):
                        v = SER(v, "dve")
                        wait(v, t_mm1[(g, hq)])
                        if hq == 0:
                            wait(v, t_c1b[g])
                        if g >= 3:
                            wait(v, t_convA[(g - 3, 4 * hq + 3)])
                            if hq == 0:
                                wait(v, t_convD[(g - 3, "bnd")])
                        h3 = hslot(g)[:, hq * 512:(hq + 1) * 512].rearrange(
                            "p (c t) -> p c t", t=128)
                        p3 = bank[:].rearrange("p (c t) -> p c t", t=128)
                        c3, _u = bass.broadcast_tensor_aps(
                            c1B[:].rearrange("p (c t) -> p c t", c=1), h3)
                        v.tensor_tensor(h3, p3, c3, ALU.mult)
                        t_hgev[(g, hq)] = ("dve", cnt["dve"])

            def emit_back_conv(g):
                last = (g == NT - 1)
                @blk.scalar
                def _(a):
                    a = SER(a, "act")
                    if g >= 1:
                        wait(a, t_sgn[g - 1])
                    for hc in range(HC):
                        if hc % 4 == 0:
                            wait(a, t_hgev[(g, hc // 4)])
                        a.activation(hp[:, hc * 128:(hc + 1) * 128],
                                     hslot(g)[:, hc * 128:(hc + 1) * 128],
                                     AF.Identity, bias=cb_t[:, hc:hc + 1],
                                     scale=w1c_t[:, hc:hc + 1])
                        t_convA[(g, hc)] = ("act", cnt["act"])
                        if hc == HC // 2 - 1:
                            t_convA[(g, "h0")] = t_convA[(g, hc)]

                for half in range(2):
                    @blk.vector
                    def _(v, half=half):
                        v = SER(v, "dve")
                        if half == 0:
                            wait(v, t_convA[(g, "h0")])
                        else:
                            wait(v, t_convA[(g, HC - 1)])
                        co = half * (HC // 2)
                        sl = slice(half * 2048, (half + 1) * 2048)
                        h3 = hslot(g)[:, sl].rearrange("p (c t) -> p c t", t=128)
                        hp3 = hp[:, sl].rearrange("p (c t) -> p c t", t=128)
                        hw3 = scrA[:, :].rearrange("p (c t) -> p c t", t=128)
                        w0b, _u = bass.broadcast_tensor_aps(
                            w0c_t[:, co:co + HC // 2].rearrange("p (c o) -> p c o", o=1), h3)
                        w2b, _u = bass.broadcast_tensor_aps(
                            w2c_t[:, co:co + HC // 2].rearrange("p (c o) -> p c o", o=1), h3)
                        v.tensor_tensor(hw3, h3, w0b, ALU.mult)
                        v.tensor_tensor(hp3[:, :, 1:128], hp3[:, :, 1:128],
                                        hw3[:, :, 0:127], ALU.add)
                        v.tensor_tensor(hw3, h3, w2b, ALU.mult)
                        r = v.tensor_tensor(hp3[:, :, 0:127], hp3[:, :, 0:127],
                                            hw3[:, :, 1:128], ALU.add)
                        _ = r; t_convD[(g, half)] = ("dve", cnt["dve"])

                @blk.vector
                def _(v):
                    v = SER(v, "dve")
                    if not last:
                        wait(v, t_hgev[(g + 1, HC // 4 - 1)])
                    hview = lambda t_, tn: t_[:].rearrange("p (c t) -> p c t", t=128)[:, :, tn:tn + 1]
                    w3 = lambda w: w[:].rearrange("p (c o) -> p c o", o=1)
                    b3 = bnd[:].rearrange("p (c o) -> p c o", o=1)
                    if g > 0:
                        v.tensor_tensor(b3, hview(hslot(g - 1), 127), w3(w0c_t), ALU.mult)
                        v.tensor_tensor(hview(hp, 0), hview(hp, 0), b3, ALU.add)
                    if not last:
                        v.tensor_tensor(b3, hview(hslot(g + 1), 0), w3(w2c_t), ALU.mult)
                        v.tensor_tensor(hview(hp, 127), hview(hp, 127), b3, ALU.add)
                    t_convD[(g, "bnd")] = ("dve", cnt["dve"])

                @blk.scalar
                def _(a):
                    a = SER(a, "act")
                    wait(a, t_convD[(g, "bnd")])
                    r = a.activation(hp[:], hp[:], AF.Silu)
                    _ = r; t_silu[g] = ("act", cnt["act"])
                    wait(a, last_peel[0])
                    r = a.activation(scrA[:, :], hp[:, 0:2048], AF.Abs)
                    _ = r; last_absA[0] = ("act", cnt["act"])
                    t_absq0[g] = last_absA[0]

                @blk.vector
                def _(v):
                    v = SER(v, "dve")
                    wait(v, t_silu[g])
                    if g >= 1:
                        wait(v, t_s2[g - 1])
                    hpt = hp[:, :].rearrange("p (c t) -> p t c", t=128)
                    r = v.tensor_reduce(scrG[:, 1024:1152], hpt, AX.X, ALU.max,
                                        apply_absolute_value=True)
                    _ = r; t_qmx[g] = ("dve", cnt["dve"])
                    if g >= 1:
                        wait(v, t_mm2[(g - 1, 1)])
                    v.tensor_scalar(n2g[:, 0:2048], hp[:, 0:2048], 0.0, 2.0,
                                    ALU.is_ge, ALU.mult)
                    r = v.tensor_scalar(n2g[:, 2048:4096], hp[:, 2048:4096], 0.0, 2.0,
                                        ALU.is_ge, ALU.mult)
                    _ = r; t_sgn[g] = ("dve", cnt["dve"])

                @blk.tensor
                def _(te):
                    wait(te, t_qmx[g])
                    r = te.transpose(ps_sm[:, 128:256], scrG[:, 1024:1152], ident_f[:])
                    t_qtr[g] = inc(r, "pe")

                @blk.vector
                def _(v):
                    v = SER(v, "dve")
                    wait(v, t_qtr[g])
                    if g >= 2:
                        wait(v, t_ev2s[g - 2])
                    cs = slice(g % 2, g % 2 + 1)
                    v.tensor_reduce(s2v[:], ps_sm[:, 128:256], AX.X, ALU.max)
                    v.tensor_scalar(s2v[:], s2v[:], 1e-5, None, ALU.max)
                    v.tensor_scalar(c2v[:, cs], s2v[:], 0.5 / 127.0, None, ALU.mult)
                    v.tensor_tensor(c2v[:, cs], c2v[:, cs], m2b[:], ALU.mult)
                    v.reciprocal(s2v[:], s2v[:])
                    r = v.tensor_scalar(s2v[:], s2v[:], 127.0, None, ALU.mult)
                    _ = r; t_s2[g] = ("dve", cnt["dve"])

                @blk.tensor
                def _(te):
                    wait(te, t_s2[g])
                    r = te.transpose(ps_sm[0:1, 384:512], s2v[:], ident_f[:])
                    t_qtr2[g] = inc(r, "pe")

                @blk.vector
                def _(v):
                    v = SER(v, "dve")
                    wait(v, t_qtr2[g])
                    r = v.tensor_copy(scrR[0:1, 1920:2048], ps_sm[0:1, 384:512])
                    _ = r; t_s2b[("row", g)] = ("dve", cnt["dve"])

                @blk.tensor
                def _(te):
                    wait(te, t_s2b[("row", g)])
                    r = te.matmul(ps_sm[:, 256:384], ones_row_f[:],
                                  scrR[0:1, 1920:2048], start=True, stop=True)
                    t_qtr2[("b", g)] = inc(r, "pe")

                @blk.vector
                def _(v):
                    v = SER(v, "dve")
                    wait(v, t_qtr2[("b", g)])
                    if g >= 1:
                        wait(v, t_n2a[(g - 1, 1)])
                    r = v.tensor_copy(s2B[:], ps_sm[:, 256:384])
                    _ = r; t_s2b[g] = ("dve", cnt["dve"])

            def _emit_q(g, q):
                sl = slice(q * 2048, (q + 1) * 2048)
                if q == 0:
                    last_absA[0] = t_absq0[g]
                else:
                    @blk.scalar
                    def _(a, q=q, sl=sl):
                        a = SER(a, "act")
                        wait(a, last_peel[0])
                        r = a.activation(scrA[:, :], hp[:, sl], AF.Abs)
                        _ = r; last_absA[0] = ("act", cnt["act"])

                @blk.vector
                def _(v, q=q, sl=sl):
                    v = SER(v, "dve")
                    wait(v, last_absA[0])
                    wait(v, t_s2b[g])
                    a3 = scrA[:, :].rearrange("p (c t) -> p c t", t=128)
                    b3, _u = bass.broadcast_tensor_aps(
                        s2B[:].rearrange("p (c t) -> p c t", c=1), a3)
                    r = v.tensor_tensor(a3, a3, b3, ALU.mult)
                    _ = r; t_n2a[(g, q)] = ("dve", cnt["dve"])

                t_n2[(g, q)] = emit_peel_core(blk, None, n2g[:, sl], 2048,
                                              sign_stored=True)

            def emit_back_restA(g):
                _emit_q(g, 0)
                @blk.scalar
                def _(a):
                    a = SER(a, "act")
                    wait(a, last_peel[0])
                    r = a.activation(scrA[:, :], hp[:, 2048:4096], AF.Abs)
                    _ = r; last_absA[0] = ("act", cnt["act"])
                    t_absq0[(g, 1)] = last_absA[0]

            def emit_back_restB(g):
                last_absA[0] = t_absq0[(g, 1)]
                sl = slice(2048, 4096)

                @blk.vector
                def _(v):
                    v = SER(v, "dve")
                    wait(v, last_absA[0])
                    wait(v, t_s2b[g])
                    a3 = scrA[:, :].rearrange("p (c t) -> p c t", t=128)
                    b3, _u = bass.broadcast_tensor_aps(
                        s2B[:].rearrange("p (c t) -> p c t", c=1), a3)
                    r = v.tensor_tensor(a3, a3, b3, ALU.mult)
                    _ = r; t_n2a[(g, 1)] = ("dve", cnt["dve"])

                t_n2[(g, 1)] = emit_peel_core(blk, None, n2g[:, sl], 2048,
                                              sign_stored=True)

                for dh in range(2):
                    @blk.tensor
                    def _(te, dh=dh):
                        wait(te, t_n2[(g, 1)])
                        if g >= 1 and dh == 0:
                            wait(te, t_ev2s[g - 1])
                        r = None
                        for hc in range(HC):
                            r = te.matmul(ps_o[dh][:],
                                          n2g[:, hc * 128:(hc + 1) * 128],
                                          w2t[:, hc * D + dh * 512:hc * D + (dh + 1) * 512],
                                          start=(hc == 0), stop=(hc == HC - 1))
                        t_mm2[(g, dh)] = inc(r, "pe")

            def emit_evst(g):
                @blk.vector
                def _(v):
                    v = SER(v, "dve")
                    wait(v, t_mm2[(g, 1)])
                    if g >= 2:
                        wait(v, t_st[g - 2])
                    else:
                        for _tk in (wait_src_st or []):
                            wait(v, _tk)
                    r = None
                    for dh in range(2):
                        r = v.scalar_tensor_tensor(oslot(g)[:, dh * 512:(dh + 1) * 512],
                                                   ps_o[dh][:], c2v[:, g % 2:g % 2 + 1],
                                                   xslot(g)[:, dh * 512:(dh + 1) * 512],
                                                   ALU.mult, ALU.add)
                    _ = r; t_ev2s[g] = ("dve", cnt["dve"])

                @blk.gpsimd
                def _(gp):
                    wait(gp, t_ev2s[g])
                    r = gp.dma_start(out=dst[g * 128:(g + 1) * 128, :], in_=oslot(g))
                    t_st[g] = inc(r, "st%d" % (g % 2), 16)

            for it in range(NT + 2):
                if it == 0:
                    emit_front_a(0)
                    emit_front_mm(0)
                if 1 <= it < NT + 1:
                    emit_back_conv(it - 1)
                if 0 <= it - 2 < NT:
                    emit_evst(it - 2)
                if it + 1 < NT:
                    emit_front_a(it + 1)
                if 1 <= it < NT + 1:
                    emit_back_restA(it - 1)
                if it + 1 < NT:
                    emit_front_mm(it + 1)
                if 1 <= it < NT + 1:
                    emit_back_restB(it - 1)

        fctx.close()
        return [("st0", cnt["st0"]), ("st1", cnt["st1"])]

    def attn_phase(src, dst, wait_src_st):
        actx = contextlib.ExitStack()
        asb = lambda name, shp, dt: actx.enter_context(nc.sbuf_tensor(name, shp, dt))
        aps = lambda name, shp, dt: actx.enter_context(nc.psum_tensor(name, shp, dt))

        xnT = asb("xnT", [128, DC * S], BF16)
        kT = asb("kT", [128, DC * S], BF16)
        v_sb = asb("v_sb", [128, NT * D], BF16)
        wb0 = asb("wb0", [128, DC * D], BF16)
        wb1 = asb("wb1", [128, DC * D], BF16)
        stgA = asb("stgA", [128, 4 * D], F32)
        aslot = lambda s: stgA[:, (s % 4) * D:(s % 4) * D + D]
        xnb = asb("xnb", [128, D], BF16)
        qTc = asb("qTc", [128, D], BF16)
        pscr = asb("pscr", [128, S], F32)
        pbf = asb("pbf", [128, S], BF16)
        ptb = asb("ptb", [128, S], BF16)
        ybf = asb("ybf", [128, D], BF16)
        ytb = asb("ytb", [128, D], BF16)
        oscr = asb("oscr", [128, D], F32)
        rmx = asb("rmx", [128, 4], F32)
        nrm = asb("nrm", [128, 1], F32)
        ssum = asb("ssum", [128, 1], F32)
        ps_sc = [aps("ps_sc%d" % i, [128, 512], F32) for i in range(4)]
        ps_ax = [aps("ps_ax%d" % i, [128, 512], F32) for i in range(2)]
        ps_tr = aps("ps_tr", [128, 512], BF16)
        ps_tr2 = aps("ps_tr2", [128, 512], BF16)

        with nc.Block("attn") as blk:
            @blk.vector
            def _(v):
                wait(v, t_wcache)

            @blk.scalar
            def _(a):
                wait(a, t_wcache)

            @blk.sync
            def _(sy):
                wait(sy, t_wcache)

            @blk.gpsimd
            def _(gp):
                wait(gp, t_wcache)

            t_wk, t_wkc = {}, {}
            for c in range(DC):
                @blk.sync
                def _(sy, c=c):
                    if c == 0:
                        for _tk in (wait_src_st or []):
                            wait(sy, _tk)
                    if c >= 4:
                        wait(sy, t_wkc[c - 4])
                    r = sy.dma_start(out=aslot(c), in_=wk_h[c * 128:(c + 1) * 128, :])
                    t_wk[c] = inc(r, "lw%d" % (c % 4), 16)

                @blk.gpsimd
                def _(gp, c=c):
                    gp = SER(gp, "pool")
                    if c == 0:
                        for _tk in (wait_src_st or []):
                            wait(gp, _tk)
                    wait(gp, t_wk[c])
                    r = gp.tensor_copy(wb0[:, c * D:(c + 1) * D], aslot(c))
                    _ = r; t_wkc[c] = ("pool", cnt["pool"])

            t_lds, t_sq, t_red = {}, {}, {}
            for g in range(NT):
                @blk.sync
                def _(sy, g=g):
                    if g == 0:
                        for _tk in (wait_src_st or []):
                            wait(sy, _tk)
                    if g >= XS:
                        wait(sy, t_red[g - XS])
                        wait(sy, t_sq[g - XS])
                    r = sy.dma_start(out=xslot(g), in_=src[g * 128:(g + 1) * 128, :])
                    t_lds[g] = inc(r, "ldx%d" % (g % XS), 16)

                @blk.scalar
                def _(a, g=g):
                    a = SER(a, "act")
                    wait(a, t_lds[g])
                    r = a.activation(oscr[:], xslot(g), AF.Square,
                                     accum_out=st_var[:, g:g + 1])
                    _ = r; t_sq[g] = ("act", cnt["act"])

                @blk.vector
                def _(v, g=g):
                    v = SER(v, "dve")
                    wait(v, t_lds[g])
                    r = v.tensor_reduce(st_m[:, g:g + 1], xslot(g), AX.X, ALU.add)
                    _ = r; t_red[g] = ("dve", cnt["dve"])

            @blk.vector
            def _(v):
                v = SER(v, "dve")
                wait(v, t_red[NT - 1])
                wait(v, t_sq[NT - 1])
                v.tensor_scalar(st_m[:], st_m[:], 1.0 / D, None, ALU.mult)
                v.tensor_scalar(st_var[:], st_var[:], 1.0 / D, None, ALU.mult)
                v.tensor_tensor(st_t[:], st_m[:], st_m[:], ALU.mult)
                r = v.tensor_tensor(st_var[:], st_var[:], st_t[:], ALU.subtract)
                _ = ("dve", cnt["dve"])
            t_var = ("dve", cnt["dve"])

            @blk.scalar
            def _(a):
                a = SER(a, "act")
                wait(a, t_var)
                r = a.activation(st_rstd[:], st_var[:], AF.Sqrt, bias=CB[1e-5], scale=1.0)
                _ = ("act", cnt["act"])
            t_sd = ("act", cnt["act"])

            @blk.vector
            def _(v):
                v = SER(v, "dve")
                wait(v, t_sd)
                v.reciprocal(st_rstd[:], st_rstd[:])
                r = v.scalar_tensor_tensor(st_B[:], st_m[:], -1.0, st_rstd[:],
                                           ALU.mult, ALU.mult)
                _ = ("dve", cnt["dve"])
            t_ab = ("dve", cnt["dve"])

            t_wv, t_wvc = {}, {}
            for c in range(DC):
                @blk.sync
                def _(sy, c=c):
                    if c < 4:
                        wait(sy, t_wkc[c + 4])
                    else:
                        wait(sy, t_wvc[c - 4])
                    r = sy.dma_start(out=aslot(c), in_=wv_h[c * 128:(c + 1) * 128, :])
                    t_wv[c] = inc(r, "lw%d" % (c % 4), 16)

                @blk.gpsimd
                def _(gp, c=c):
                    gp = SER(gp, "pool")
                    wait(gp, t_wv[c])
                    r = gp.tensor_copy(wb1[:, c * D:(c + 1) * D], aslot(c))
                    _ = r; t_wvc[c] = ("pool", cnt["pool"])

            t_ldx2, t_xn, t_xtp, t_xnt = {}, {}, {}, {}
            for g in range(NT):
                @blk.sync
                def _(sy, g=g):
                    if g < XS:
                        wait(sy, t_sq[NT - 1])
                        wait(sy, t_red[NT - 1])
                    else:
                        wait(sy, t_xn[g - XS])
                    r = sy.dma_start(out=xslot(g), in_=src[g * 128:(g + 1) * 128, :])
                    t_ldx2[g] = inc(r, "ldx%d" % (g % XS), 16)

                @blk.scalar
                def _(a, g=g):
                    a = SER(a, "act")
                    wait(a, t_ldx2[g])
                    if g == 0:
                        wait(a, t_ab)
                    else:
                        wait(a, t_xtp[g - 1])
                    r = a.activation(xnb[:], xslot(g), AF.Identity,
                                     bias=st_B[:, g:g + 1], scale=st_rstd[:, g:g + 1])
                    _ = r; t_xn[g] = ("act", cnt["act"])

                @blk.tensor
                def _(te, g=g):
                    wait(te, t_xn[g])
                    if g >= 1:
                        wait(te, t_xnt[g - 1])
                    r = None
                    for dc in range(DC):
                        pt = ps_tr if dc < 4 else ps_tr2
                        r = te.transpose(pt[:, (dc % 4) * 128:(dc % 4) * 128 + 128],
                                         xnb[:, dc * 128:(dc + 1) * 128], ident_b[:])
                    t_xtp[g] = inc(r, "pe")

                @blk.vector
                def _(v, g=g):
                    v = SER(v, "dve")
                    wait(v, t_xtp[g])
                    xv = xnT[:].rearrange("p (c t) -> p c t", t=S)[:, :, g * 128:(g + 1) * 128]
                    v.tensor_copy(xv[:, 0:4, :], ps_tr[:].rearrange("p (c t) -> p c t", t=128))
                    v.tensor_copy(xv[:, 4:8, :], ps_tr2[:].rearrange("p (c t) -> p c t", t=128))
                    t_xnt[g] = ("dve", cnt["dve"])

            if ATTN_CUT == "ln":
                actx.close()
                return None
            kv_banks = [ps_ax[0], ps_ax[1], ps_sc[0], ps_sc[1]]
            t_kmm, t_kev = {}, {}
            for jc in range(DC):
                for sp in range(4):
                    i = jc * 4 + sp
                    bank = kv_banks[i % 4]

                    @blk.tensor
                    def _(te, jc=jc, sp=sp, i=i, bank=bank):
                        if i == 0:
                            wait(te, t_wkc[DC - 1])
                            wait(te, t_xnt[NT - 1])
                        if i >= 4:
                            wait(te, t_kev[i - 4])
                        r = None
                        for dc in range(DC):
                            r = te.matmul(bank[:],
                                          wb0[:, dc * D + jc * 128:dc * D + (jc + 1) * 128],
                                          xnT[:, dc * S + sp * 512:dc * S + (sp + 1) * 512],
                                          start=(dc == 0), stop=(dc == DC - 1))
                        t_kmm[i] = inc(r, "pe")

                @blk.scalar
                def _(a, jc=jc):
                    a = SER(a, "act")
                    for sp in range(4):
                        i = jc * 4 + sp
                        wait(a, t_kmm[i])
                        a.activation(kT[:, jc * S + sp * 512:jc * S + (sp + 1) * 512],
                                     kv_banks[i % 4][:], AF.Relu,
                                     bias=bk_t[:, jc:jc + 1], scale=1.0)
                        t_kev[i] = ("act", cnt["act"])

            t_vmm, t_vev = {}, {}
            for tc in range(NT):
                for jh in range(2):
                    i = tc * 2 + jh
                    bank = kv_banks[i % 4]

                    @blk.tensor
                    def _(te, tc=tc, jh=jh, i=i, bank=bank):
                        if i == 0:
                            wait(te, t_wvc[DC - 1])
                            wait(te, t_kev[DC * 4 - 1])
                        if i >= 4:
                            wait(te, t_vev[i - 4])
                        for dc in range(DC):
                            te.matmul(bank[:],
                                      xnT[:, dc * S + tc * 128:dc * S + (tc + 1) * 128],
                                      wb1[:, dc * D + jh * 512:dc * D + (jh + 1) * 512],
                                      start=(dc == 0), stop=False)
                        r = te.matmul(bank[:], ones_row_b[:],
                                      bv_row[:, jh * 512:(jh + 1) * 512],
                                      start=False, stop=True)
                        t_vmm[i] = inc(r, "pe")

                if tc % 2 == 1:
                    @blk.scalar
                    def _(a, tc=tc):
                        a = SER(a, "act")
                        for i in range((tc - 1) * 2, (tc + 1) * 2):
                            wait(a, t_vmm[i])
                            a.activation(v_sb[:, (i // 2) * D + (i % 2) * 512:
                                               (i // 2) * D + (i % 2 + 1) * 512],
                                         kv_banks[i % 4][:], AF.Relu)
                            t_vev[i] = ("act", cnt["act"])

            if ATTN_CUT == "kv":
                actx.close()
                return None
            t_wq, t_wqc, t_wo, t_woc = {}, {}, {}, {}
            for c in range(DC):
                @blk.sync
                def _(sy, c=c):
                    if c < 4:
                        wait(sy, t_wvc[c + 4])
                    else:
                        wait(sy, t_wqc[c - 4])
                    r = sy.dma_start(out=aslot(c), in_=wq_h[c * 128:(c + 1) * 128, :])
                    t_wq[c] = inc(r, "lw%d" % (c % 4), 16)

                @blk.gpsimd
                def _(gp, c=c):
                    gp = SER(gp, "pool")
                    wait(gp, t_wq[c])
                    wait(gp, t_kev[DC * 4 - 1])
                    r = gp.tensor_copy(wb0[:, c * D:(c + 1) * D], aslot(c))
                    _ = r; t_wqc[c] = ("pool", cnt["pool"])

            for c in range(DC):
                @blk.sync
                def _(sy, c=c):
                    if c < 4:
                        wait(sy, t_wqc[c + 4])
                    else:
                        wait(sy, t_woc[c - 4])
                    r = sy.dma_start(out=aslot(c), in_=wo_h[c * 128:(c + 1) * 128, :])
                    t_wo[c] = inc(r, "lw%d" % (c % 4), 16)

                @blk.gpsimd
                def _(gp, c=c):
                    gp = SER(gp, "pool")
                    wait(gp, t_wo[c])
                    wait(gp, t_vev[NT * 2 - 1])
                    r = gp.tensor_copy(wb1[:, c * D:(c + 1) * D], aslot(c))
                    _ = r; t_woc[c] = ("pool", cnt["pool"])

            if ATTN_CUT == "wq":
                actx.close()
                return None
            t_q, t_qe = {}, {}
            t_sc, t_mx, t_exp, t_pb = {}, {}, {}, {}
            t_pt, t_ptd = {}, {}
            t_ymm, t_yev, t_ytp, t_ytd = {}, {}, {}, {}
            t_omm, t_oev, t_res, t_xst, t_xl3 = {}, {}, {}, {}, {}

            for i in range(NT):
                W = (i + 1) * 128
                nb = (W + 511) // 512

                if ATTN_CUT not in ("", "full"):
                    pass
                else:
                    @blk.sync
                    def _(sy, i=i):
                        if i < XS:
                            wait(sy, t_xn[NT - 1])
                        elif (i - XS) in t_res:
                            wait(sy, t_res[i - XS])
                        r = sy.dma_start(out=xslot(i), in_=src[i * 128:(i + 1) * 128, :])
                        t_xl3[i] = inc(r, "ldx%d" % (i % XS), 16)

                if ATTN_CUT == "xl":
                    continue
                qreg = lambda jc: ps_ax[jc % 2][:, (jc // 2) * 128:(jc // 2 + 1) * 128]
                for jc in range(DC):
                    @blk.tensor
                    def _(te, jc=jc, i=i):
                        if i == 0 and jc == 0:
                            wait(te, t_woc[DC - 1])
                            wait(te, t_vev[NT * 2 - 1])
                        if jc == 0 and i >= 1:
                            if (i - 1) in t_oev:
                                wait(te, t_oev[i - 1])
                            else:
                                wait(te, t_qe[(i - 1, DC - 1)])
                        r = None
                        for dc in range(DC):
                            r = te.matmul(qreg(jc),
                                          wb0[:, dc * D + jc * 128:dc * D + (jc + 1) * 128],
                                          xnT[:, dc * S + i * 128:dc * S + (i + 1) * 128],
                                          start=(dc == 0), stop=(dc == DC - 1))
                        t_q[(i, jc)] = inc(r, "pe")

                @blk.scalar
                def _(a, i=i):
                    a = SER(a, "act")
                    for jc in range(DC):
                        wait(a, t_q[(i, jc)])
                    for jc in range(DC):
                        a.activation(qTc[:, jc * 128:(jc + 1) * 128], qreg(jc),
                                     AF.Relu, bias=bq_t[:, jc:jc + 1], scale=1.0)
                        t_qe[(i, jc)] = ("act", cnt["act"])

                if ATTN_CUT == "q":
                    continue
                for tkb in range(nb):
                    wdt = min(512, W - tkb * 512)

                    @blk.tensor
                    def _(te, tkb=tkb, wdt=wdt, i=i):
                        if tkb == 0:
                            wait(te, t_qe[(i, DC - 1)])
                            if i >= 1:
                                wait(te, t_mx[i - 1])
                                if (i - 1) in t_yev:
                                    wait(te, t_yev[i - 1])
                                else:
                                    wait(te, t_exp[(i - 1, (i * 128 + 511) // 512 - 1)])
                        r = None
                        for jc in range(DC):
                            r = te.matmul(ps_sc[tkb][:, 0:wdt],
                                          qTc[:, jc * 128:(jc + 1) * 128],
                                          kT[:, jc * S + tkb * 512:jc * S + tkb * 512 + wdt],
                                          start=(jc == 0), stop=(jc == DC - 1))
                        t_sc[(i, tkb)] = inc(r, "pe")

                @blk.vector
                def _(v, i=i, nb=nb, W=W):
                    v = SER(v, "dve")
                    for tkb in range(nb):
                        wdt = min(512, W - tkb * 512)
                        wait(v, t_sc[(i, tkb)])
                        v.tensor_reduce(rmx[:, tkb:tkb + 1], ps_sc[tkb][:, 0:wdt],
                                        AX.X, ALU.max)
                    if i >= 1:
                        wait(v, t_exp[(i - 1, (i * 128 + 511) // 512 - 1)])
                    v.tensor_reduce(nrm[:], rmx[:, 0:nb], AX.X, ALU.max)
                    r = v.tensor_scalar(nrm[:], nrm[:], -1.0 / 32.0, None, ALU.mult)
                    _ = r; t_mx[i] = ("dve", cnt["dve"])

                @blk.scalar
                def _(a, i=i, nb=nb, W=W):
                    a = SER(a, "act")
                    wait(a, t_mx[i])
                    for tkb in range(nb):
                        wdt = min(512, W - tkb * 512)
                        a.activation(pscr[:, tkb * 512:tkb * 512 + wdt],
                                     ps_sc[tkb][:, 0:wdt], AF.Exp,
                                     bias=nrm[:], scale=1.0 / 32.0)
                        t_exp[(i, tkb)] = ("act", cnt["act"])

                @blk.vector
                def _(v, i=i, nb=nb, W=W):
                    v = SER(v, "dve")
                    wait(v, t_exp[(i, nb - 1)])
                    if i >= 1 and (i - 1, i - 1) in t_pt:
                        wait(v, t_pt[(i - 1, i - 1)])
                    v.tensor_tensor(pscr[:, i * 128:W], pscr[:, i * 128:W], m01_b[:], ALU.mult)
                    v.tensor_reduce(ssum[:], pscr[:, 0:W], AX.X, ALU.add)
                    v.reciprocal(ssum[:], ssum[:])
                    r = v.tensor_scalar(pbf[:, 0:W], pscr[:, 0:W], ssum[:], None, ALU.mult)
                    _ = r; t_pb[i] = ("dve", cnt["dve"])

                if ATTN_CUT == "sc":
                    continue
                nsb = (i + 8) // 8
                for sb in range(nsb):
                    lo, hi = sb * 8, min(i + 1, sb * 8 + 8)

                    @blk.tensor
                    def _(te, sb=sb, lo=lo, hi=hi, i=i):
                        if sb == 0:
                            wait(te, t_pb[i])
                            if i >= 1:
                                if (i - 1) in t_ytd:
                                    wait(te, t_ytd[i - 1])
                                else:
                                    wait(te, t_ptd.get((i - 1, (i + 7) // 8 - 1), 0))
                        else:
                            wait(te, t_ptd[(i, sb - 1)])
                        r = None
                        for tkc in range(lo, hi):
                            pt = ps_tr if (tkc % 8) < 4 else ps_tr2
                            r = te.transpose(pt[:, (tkc % 4) * 128:(tkc % 4) * 128 + 128],
                                             pbf[:, tkc * 128:(tkc + 1) * 128], ident_b[:])
                        t_pt[(i, sb)] = inc(r, "pe")

                    @blk.vector
                    def _(v, sb=sb, lo=lo, hi=hi, i=i):
                        v = SER(v, "dve")
                        wait(v, t_pt[(i, sb)])
                        n1 = min(hi - lo, 4)
                        v.tensor_copy(ptb[:, lo * 128:(lo + n1) * 128], ps_tr[:, 0:n1 * 128])
                        if hi - lo > 4:
                            n2 = hi - lo - 4
                            v.tensor_copy(ptb[:, (lo + 4) * 128:hi * 128], ps_tr2[:, 0:n2 * 128])
                        t_ptd[(i, sb)] = ("dve", cnt["dve"])

                if ATTN_CUT == "pt":
                    continue
                for dh in range(2):
                    @blk.tensor
                    def _(te, dh=dh, i=i):
                        if dh == 0:
                            wait(te, t_ptd[(i, (i + 8) // 8 - 1)])
                        r = None
                        for tkc in range(i + 1):
                            r = te.matmul(ps_sc[dh][:],
                                          ptb[:, tkc * 128:(tkc + 1) * 128],
                                          v_sb[:, tkc * D + dh * 512:tkc * D + (dh + 1) * 512],
                                          start=(tkc == 0), stop=(tkc == i))
                        t_ymm[(i, dh)] = inc(r, "pe")

                @blk.scalar
                def _(a, i=i):
                    a = SER(a, "act")
                    wait(a, t_ymm[(i, 1)])
                    a.activation(ybf[:, 0:512], ps_sc[0][:], AF.Relu)
                    r = a.activation(ybf[:, 512:1024], ps_sc[1][:], AF.Relu)
                    _ = r; t_yev[i] = ("act", cnt["act"])

                if ATTN_CUT == "pv":
                    continue
                @blk.tensor
                def _(te, i=i):
                    wait(te, t_yev[i])
                    wait(te, t_ptd[(i, (i + 8) // 8 - 1)])
                    r = None
                    for dc in range(DC):
                        pt = ps_tr if dc < 4 else ps_tr2
                        r = te.transpose(pt[:, (dc % 4) * 128:(dc % 4) * 128 + 128],
                                         ybf[:, dc * 128:(dc + 1) * 128], ident_b[:])
                    t_ytp[i] = inc(r, "pe")

                @blk.vector
                def _(v, i=i):
                    v = SER(v, "dve")
                    wait(v, t_ytp[i])
                    v.tensor_copy(ytb[:, 0:512], ps_tr[:])
                    v.tensor_copy(ytb[:, 512:1024], ps_tr2[:])
                    t_ytd[i] = ("dve", cnt["dve"])

                for jh in range(2):
                    @blk.tensor
                    def _(te, jh=jh, i=i):
                        wait(te, t_ytd[i])
                        for dc in range(DC):
                            te.matmul(ps_ax[jh][:],
                                      ytb[:, dc * 128:(dc + 1) * 128],
                                      wb1[:, dc * D + jh * 512:dc * D + (jh + 1) * 512],
                                      start=(dc == 0), stop=False)
                        r = te.matmul(ps_ax[jh][:], ones_row_b[:],
                                      bo_row[:, jh * 512:(jh + 1) * 512],
                                      start=False, stop=True)
                        t_omm[(i, jh)] = inc(r, "pe")

                @blk.scalar
                def _(a, i=i):
                    a = SER(a, "act")
                    wait(a, t_omm[(i, 1)])
                    a.activation(oscr[:, 0:512], ps_ax[0][:], AF.Relu)
                    r = a.activation(oscr[:, 512:1024], ps_ax[1][:], AF.Relu)
                    _ = r; t_oev[i] = ("act", cnt["act"])

                @blk.vector
                def _(v, i=i):
                    v = SER(v, "dve")
                    wait(v, t_oev[i])
                    wait(v, t_xl3[i])
                    if i >= 2:
                        wait(v, t_xst[i - 2])
                    else:
                        for _tk in (wait_src_st or []):
                            wait(v, _tk)
                    r = v.tensor_tensor(oslot(i), oscr[:], xslot(i), ALU.add)
                    _ = r; t_res[i] = ("dve", cnt["dve"])

                @blk.gpsimd
                def _(gp, i=i):
                    wait(gp, t_res[i])
                    r = gp.dma_start(out=dst[i * 128:(i + 1) * 128, :], in_=oslot(i))
                    t_xst[i] = inc(r, "st%d" % (i % 2), 16)

        actx.close()
        return [("st0", cnt["st0"]), ("st1", cnt["st1"])]

    if phases == 4:
        wctx.close()
        attn_phase(x_h, y_h, None)
    elif phases == 1:
        ffn_phase("f1", x_h, y_h, w1t, w2t)
        wctx.close()
    elif phases == 2:
        t1 = ffn_phase("f1", x_h, sx1, w1t, w2t)
        wctx.close()
        attn_phase(sx1, y_h, t1)
    else:
        t1 = ffn_phase("f1", x_h, sx1, w1t, w2t)
        wctx.close()
        t2 = attn_phase(sx1, sx2, t1)
        w3ctx = contextlib.ExitStack()
        w1t3 = w3ctx.enter_context(nc.sbuf_tensor("w1t3", [128, DC * H], FP8))
        w2t3 = w3ctx.enter_context(nc.sbuf_tensor("w2t3", [128, HC * D], FP8))
        ffn_phase("f3", sx2, y_h, w1t3, w2t3, wait_src_st=t2, preload_wt=t_wcache)
        w3ctx.close()

    with nc.Block("fin") as blk:
        @blk.gpsimd
        def _(g):
            wait(g, "st0", cnt["st0"])
            wait(g, "st1", cnt["st1"])
            wait(g, "stw", cnt["stw"])

    gctx.close()
    return nc


def kernel(L0_attn_x, L0_attn_Wq, L0_attn_bq, L0_attn_Wk, L0_attn_bk,
           L0_attn_Wv, L0_attn_bv, L0_attn_Wo, L0_attn_bo,
           L0_ffn_w1, L0_ffn_conv_w, L0_ffn_conv_b, L0_ffn_w2):
    nc = build_nc(PHASES)
    x = np.ascontiguousarray(np.asarray(L0_attn_x, dtype=np.float32))
    com = {
        "wq": np.ascontiguousarray(np.asarray(L0_attn_Wq, np.float32)),
        "bq": np.ascontiguousarray(np.asarray(L0_attn_bq, np.float32)),
        "wk": np.ascontiguousarray(np.asarray(L0_attn_Wk, np.float32)),
        "bk": np.ascontiguousarray(np.asarray(L0_attn_bk, np.float32)),
        "wv": np.ascontiguousarray(np.asarray(L0_attn_Wv, np.float32)),
        "bv": np.ascontiguousarray(np.asarray(L0_attn_bv, np.float32)),
        "wo": np.ascontiguousarray(np.asarray(L0_attn_Wo, np.float32)),
        "bo": np.ascontiguousarray(np.asarray(L0_attn_bo, np.float32)),
        "w1": np.ascontiguousarray(np.asarray(L0_ffn_w1, np.float32)),
        "cw": np.ascontiguousarray(np.asarray(L0_ffn_conv_w, np.float32)),
        "cb": np.ascontiguousarray(np.asarray(L0_ffn_conv_b, np.float32)),
        "w2": np.ascontiguousarray(np.asarray(L0_ffn_w2, np.float32)),
    }
    in_maps = [dict(com, x=np.ascontiguousarray(x[b])) for b in range(N_CORES)]
    res = run_bass_kernel_spmd(nc, in_maps, core_ids=list(range(N_CORES)))
    return np.stack([np.asarray(r["y"], dtype=np.float32) for r in res.results], axis=0)


# version 20 (final)
# speedup vs baseline: 4.4783x; 1.0269x over previous
import contextlib
import numpy as np

import concourse.bass as bass
import concourse.mybir as mybir
from concourse.bass_utils import run_bass_kernel_spmd

F32 = mybir.dt.float32
BF16 = mybir.dt.bfloat16
FP8 = mybir.dt.float8e4
I32 = mybir.dt.int32
AF = mybir.ActivationFunctionType
ALU = mybir.AluOpType
AX = mybir.AxisListType

S, D, H = 2048, 1024, 4096
NT = S // 128
DC = D // 128
HC = H // 128
N_CORES = 8
PHASES = 3
ATTN_CUT = ""
FOLD_A = [64.0, 32.0, 16.0, 8.0, 4.0, 2.0, 1.0]
FOLD_C = [63.5, 32.0, 16.0, 8.0, 4.0, 2.0, 1.0]


def build_nc(phases=PHASES):
    nc = bass.Bass("TRN2", target_bir_lowering=False)

    x_h = nc.dram_tensor("x", [S, D], F32, kind="ExternalInput").ap()
    wq_h = nc.dram_tensor("wq", [D, D], F32, kind="ExternalInput").ap()
    bq_h = nc.dram_tensor("bq", [D], F32, kind="ExternalInput").ap()
    wk_h = nc.dram_tensor("wk", [D, D], F32, kind="ExternalInput").ap()
    bk_h = nc.dram_tensor("bk", [D], F32, kind="ExternalInput").ap()
    wv_h = nc.dram_tensor("wv", [D, D], F32, kind="ExternalInput").ap()
    bv_h = nc.dram_tensor("bv", [D], F32, kind="ExternalInput").ap()
    wo_h = nc.dram_tensor("wo", [D, D], F32, kind="ExternalInput").ap()
    bo_h = nc.dram_tensor("bo", [D], F32, kind="ExternalInput").ap()
    w1_h = nc.dram_tensor("w1", [H, D], F32, kind="ExternalInput").ap()
    cw_h = nc.dram_tensor("cw", [H, 1, 3], F32, kind="ExternalInput").ap()
    cb_h = nc.dram_tensor("cb", [H], F32, kind="ExternalInput").ap()
    w2_h = nc.dram_tensor("w2", [D, H], F32, kind="ExternalInput").ap()
    y_h = nc.dram_tensor("y", [S, D], F32, kind="ExternalOutput").ap()

    sx1 = nc.dram_tensor("sx1", [S, D], F32, kind="Internal").ap()
    sx2 = nc.dram_tensor("sx2", [S, D], F32, kind="Internal").ap()
    sw1t = nc.dram_tensor("sw1t", [128, DC * H], FP8, kind="Internal").ap()
    sw2t = nc.dram_tensor("sw2t", [128, HC * D], FP8, kind="Internal").ap()

    gctx = contextlib.ExitStack()
    gsb = lambda name, shp, dt: gctx.enter_context(nc.sbuf_tensor(name, shp, dt))

    sems, cnt = {}, {}
    for nm in ("ldb", "lwt", "stw", "dve", "act", "pe", "pool",
               "ldx0", "ldx1", "ldx2", "ldx3", "lw0", "lw1", "lw2", "lw3",
               "st0", "st1"):
        sems[nm] = gctx.enter_context(nc.semaphore("s_" + nm))
        cnt[nm] = 0

    def inc(instr, nm, n=1):
        instr.then_inc(sems[nm], n)
        cnt[nm] += n
        return (nm, cnt[nm])

    _SER_OPS = {"tensor_scalar", "tensor_tensor", "scalar_tensor_tensor",
                "tensor_copy", "tensor_reduce", "reduce_max", "reduce_sum",
                "memset", "reciprocal", "activation", "copy", "add", "mul"}

    class SER:
        def __init__(self, eng, nm):
            self._e = eng
            self._nm = nm

        def __getattr__(self, name):
            f = getattr(self._e, name)
            if name in _SER_OPS:
                nm = self._nm

                def g(*a, **k):
                    if cnt[nm] > 0:
                        self._e.wait_ge(sems[nm], cnt[nm])
                    r = f(*a, **k)
                    inc(r, nm)
                    return r
                return g
            return f

    def wait(eng, tok, v=None):
        if tok == 0 or tok is None:
            return
        if v is None:
            nm, val = tok
        else:
            nm, val = tok, v
        if val > 0:
            eng.wait_ge(sems[nm], val)

    ident_b = gsb("ident_b", [128, 128], BF16)
    ident_f = gsb("ident_f", [128, 128], F32)
    m01_b = gsb("m01_b", [128, 128], BF16)
    ones_row_b = gsb("ones_row_b", [1, 128], BF16)
    ones_row_f = gsb("ones_row_f", [1, 128], F32)
    ones_col_f = gsb("ones_col_f", [128, 1], F32)
    io_t = gsb("io_t", [128, 128], I32)
    bq_t = gsb("bq_t", [128, DC], F32)
    bk_t = gsb("bk_t", [128, DC], F32)
    bv_row = gsb("bv_row", [1, D], BF16)
    bo_row = gsb("bo_row", [1, D], BF16)
    w0c_t = gsb("w0c_t", [128, HC], F32)
    w1c_t = gsb("w1c_t", [128, HC], F32)
    w2c_t = gsb("w2c_t", [128, HC], F32)
    cb_t = gsb("cb_t", [128, HC], F32)
    m1b = gsb("m1b", [128, 1], F32)
    m2b = gsb("m2b", [128, 1], F32)
    thrP1 = gsb("thrP1", [128, 1], F32)
    thrN1 = gsb("thrN1", [128, 1], F32)
    thrP2 = gsb("thrP2", [128, 1], F32)
    thrN2 = gsb("thrN2", [128, 1], F32)
    sc11 = gsb("sc11", [1, 4], F32)

    XS = 3
    xg = gsb("xg", [128, XS * D], F32)
    xslot = lambda s: xg[:, (s % XS) * D:(s % XS) * D + D]
    outx = gsb("outx", [128, 2 * D], F32)
    oslot = lambda s: outx[:, (s % 2) * D:(s % 2) * D + D]

    st_m = gsb("st_m", [128, NT], F32)
    st_var = gsb("st_var", [128, NT], F32)
    st_rstd = gsb("st_rstd", [128, NT], F32)
    st_mx = gsb("st_mx", [128, NT], F32)
    st_mn = gsb("st_mn", [128, NT], F32)
    st_A = gsb("st_A", [128, NT], F32)
    st_B = gsb("st_B", [128, NT], F32)
    st_c1 = gsb("st_c1", [128, NT], F32)
    st_t = gsb("st_t", [128, NT], F32)

    cst = gsb("cst", [128, 12], F32)
    CB = {1e-5: cst[:, 0:1], -63.5: cst[:, 1:2], -32.0: cst[:, 2:3], -16.0: cst[:, 3:4],
          -8.0: cst[:, 4:5], -4.0: cst[:, 5:6], -2.0: cst[:, 6:7], -1.0: cst[:, 7:8],
          -0.5: cst[:, 8:9]}
    SCR = {}

    class _ScrProxy:
        def __init__(self, key):
            self.key = key

        def __getitem__(self, idx):
            return SCR[self.key][idx]

    scrV = _ScrProxy("V")
    scrA = _ScrProxy("A")
    scrR = _ScrProxy("R")
    scrG = _ScrProxy("G")
    scrI = _ScrProxy("I")

    last_absA = [0]
    last_peel = [0]
    last_absA1 = [0]
    last_peel1 = [0]

    p1A = _ScrProxy("A1")
    p1R = _ScrProxy("R1")
    p1G = _ScrProxy("G1")

    def emit_peel_core(blk, sign_ap, out_ap, F, sA=scrA, sR=scrR, sG=scrG,
                       cell=last_peel, sign_stored=False):
        tk = {}

        @blk.vector
        def _(v):
            v = SER(v, "dve")
            for off in range(0, F, 512):
                v.tensor_copy(scrI[:, 0:512], sA[:, off:off + 512])
                v.tensor_copy(sA[:, off:off + 512], scrI[:, 0:512])
            if sign_stored:
                r = v.scalar_tensor_tensor(out_ap, out_ap, -1.0, sA[:, :F],
                                           ALU.add, ALU.mult)
            else:
                if callable(sign_ap):
                    sign_ap(v)
                else:
                    v.tensor_scalar(sR[:, :F], sign_ap, 0.0, 2.0, ALU.is_ge, ALU.mult)
                r = v.scalar_tensor_tensor(out_ap, sR[:, :F], -1.0, sA[:, :F],
                                           ALU.add, ALU.mult)
            _ = r; tk["out"] = ("dve", cnt["dve"])

        cell[0] = tk["out"]
        return tk["out"]

    def emit_peel(blk, v_ap, out_ap, F, act_wait=None):
        @blk.scalar
        def _(a):
            a = SER(a, "act")
            if act_wait is not None:
                wait(a, *act_wait)
            wait(a, last_peel[0])
            r = a.activation(scrA[:, :F], v_ap, AF.Abs)
            _ = r; last_absA[0] = ("act", cnt["act"])

        @blk.vector
        def _(v):
            v = SER(v, "dve")
            wait(v, last_absA[0])
        return emit_peel_core(blk, v_ap, out_ap, F)

    wctx = contextlib.ExitStack()
    w1t = wctx.enter_context(nc.sbuf_tensor("w1t", [128, DC * H], FP8))
    w2t = wctx.enter_context(nc.sbuf_tensor("w2t", [128, HC * D], FP8))

    pctx = contextlib.ExitStack()
    psb = lambda name, shp, dt: pctx.enter_context(nc.sbuf_tensor(name, shp, dt))
    pps = lambda name, shp, dt: pctx.enter_context(nc.psum_tensor(name, shp, dt))

    with nc.Block("prologue") as blk:
        stg = psb("stg", [128, 2 * H], F32)
        sslot = lambda s: stg[:, (s % 2) * H:(s % 2) * H + H]
        scrT = psb("scrT", [128, H], BF16)
        SCR["V"] = psb("pscrV", [128, 1024], F32)
        SCR["R"] = psb("pscrR", [128, 1024], F32)
        w1abs = psb("w1abs", [128, HC], F32)
        w2abs = psb("w2abs", [128, DC], F32)
        ps_s = pps("ps_s", [128, 512], F32)
        ps_t = pps("ps_t", [128, 512], BF16)
        ps_t2 = pps("ps_t2", [128, 512], BF16)
        ps_1 = pps("ps_1", [1, 128], F32)

        @blk.gpsimd
        def _(g):
            r = g.iota(io_t[:], [[1, 128]], base=0, channel_multiplier=-1)
            inc(r, "pool")
        t_iota = ("pool", cnt["pool"])

        @blk.vector
        def _(v):
            v = SER(v, "dve")
            wait(v, t_iota)
            v.tensor_scalar(ident_b[:], io_t[:], 0.0, None, ALU.is_equal)
            v.tensor_scalar(ident_f[:], io_t[:], 0.0, None, ALU.is_equal)
            v.tensor_scalar(m01_b[:], io_t[:], 0.0, None, ALU.is_le)
            v.memset(ones_row_b[:], 1.0)
            v.memset(ones_row_f[:], 1.0)
            for val, ap in CB.items():
                v.memset(ap, val)
            r = v.memset(ones_col_f[:], 1.0)
            _ = ("dve", cnt["dve"])
        t_const = ("dve", cnt["dve"])

        @blk.sync
        def _(sy):
            inc(sy.dma_start(out=stg[0:8, 0:128], in_=bq_h.rearrange("(c p) -> c p", c=8)), "ldb", 16)
            inc(sy.dma_start(out=stg[0:8, 128:256], in_=bk_h.rearrange("(c p) -> c p", c=8)), "ldb", 16)
            inc(sy.dma_start(out=stg[0:32, 256:384], in_=cb_h.rearrange("(c p) -> c p", c=32)), "ldb", 16)
            inc(sy.dma_start(out=stg[0:32, 384:768],
                             in_=cw_h.rearrange("h o k -> h (o k)").rearrange("(c p) k -> c (p k)", c=32)), "ldb", 16)
            inc(sy.dma_start(out=stg[0:1, 1152:1152 + D], in_=bv_h.rearrange("(o d) -> o d", o=1)), "ldb", 16)
            inc(sy.dma_start(out=stg[0:1, 2176:2176 + D], in_=bo_h.rearrange("(o d) -> o d", o=1)), "ldb", 16)
        t_bld = ("ldb", cnt["ldb"])

        @blk.vector
        def _(v):
            v = SER(v, "dve")
            wait(v, t_bld)
            taps = stg[0:32, 384:768].rearrange("c (p k) -> c k p", k=3)
            for k in range(3):
                v.tensor_copy(stg[0:32, 768 + k * 128:768 + (k + 1) * 128],
                              taps[:, k:k + 1, :].rearrange("c one p -> c (one p)"))
            v.tensor_copy(bv_row[:], stg[0:1, 1152:1152 + D])
            r = v.tensor_copy(bo_row[:], stg[0:1, 2176:2176 + D])
            _ = ("dve", cnt["dve"])
        t_bup = ("dve", cnt["dve"])

        @blk.tensor
        def _(te):
            wait(te, t_bup)
            te.transpose(ps_s[:, 0:8], stg[0:8, 0:128], ident_f[0:8, 0:8])
            te.transpose(ps_s[:, 8:16], stg[0:8, 128:256], ident_f[0:8, 0:8])
            te.transpose(ps_s[:, 16:48], stg[0:32, 256:384], ident_f[0:32, 0:32])
            r = None
            for k in range(3):
                r = te.transpose(ps_s[:, 48 + 32 * k:48 + 32 * (k + 1)],
                                 stg[0:32, 768 + k * 128:768 + (k + 1) * 128],
                                 ident_f[0:32, 0:32])
            inc(r, "pe")
        t_btr = ("pe", cnt["pe"])

        @blk.vector
        def _(v):
            v = SER(v, "dve")
            wait(v, t_btr)
            v.tensor_copy(bq_t[:], ps_s[:, 0:8])
            v.tensor_copy(bk_t[:], ps_s[:, 8:16])
            v.tensor_copy(cb_t[:], ps_s[:, 16:48])
            v.tensor_copy(w0c_t[:], ps_s[:, 48:80])
            v.tensor_copy(w1c_t[:], ps_s[:, 80:112])
            r = v.tensor_copy(w2c_t[:], ps_s[:, 112:144])
            _ = ("dve", cnt["dve"])
        t_bdone = ("dve", cnt["dve"])

        t_wld, t_wred = {}, {}
        for c in range(HC + DC):
            @blk.sync
            def _(sy, c=c):
                if c == 0:
                    wait(sy, t_bdone)
                if c >= 2:
                    wait(sy, t_wred[c - 2])
                if c < HC:
                    r = sy.dma_start(out=sslot(c)[:, 0:D], in_=w1_h[c * 128:(c + 1) * 128, :])
                else:
                    r = sy.dma_start(out=sslot(c), in_=w2_h[(c - HC) * 128:(c - HC + 1) * 128, :])
                t_wld[c] = inc(r, "lw%d" % (c % 2), 16)

            @blk.vector
            def _(v, c=c):
                v = SER(v, "dve")
                wait(v, t_wld[c])
                if c < HC:
                    r = v.tensor_reduce(w1abs[:, c:c + 1], sslot(c)[:, 0:D], AX.X,
                                        ALU.add, apply_absolute_value=True)
                else:
                    r = v.tensor_reduce(w2abs[:, c - HC:c - HC + 1], sslot(c), AX.X,
                                        ALU.add, apply_absolute_value=True)
                _ = r; t_wred[c] = ("dve", cnt["dve"])

        @blk.vector
        def _(v):
            v = SER(v, "dve")
            wait(v, t_wred[HC + DC - 1])
            v.tensor_reduce(st_t[:, 0:1], w1abs[:], AX.X, ALU.add)
            r = v.tensor_reduce(st_t[:, 1:2], w2abs[:], AX.X, ALU.add)
            _ = ("dve", cnt["dve"])
        t_wsum = ("dve", cnt["dve"])

        @blk.tensor
        def _(te):
            wait(te, t_wsum)
            r = te.matmul(ps_1[0:1, 0:2], ones_col_f[:], st_t[:, 0:2], start=True, stop=True)
            inc(r, "pe")
        t_msum = ("pe", cnt["pe"])

        @blk.vector
        def _(v):
            v = SER(v, "dve")
            wait(v, t_msum)
            r = v.tensor_scalar(sc11[0:1, 0:2], ps_1[0:1, 0:2], 1.0 / (H * D), 1e-5,
                                ALU.mult, ALU.max)
            _ = ("dve", cnt["dve"])
        t_m = ("dve", cnt["dve"])

        @blk.tensor
        def _(te):
            wait(te, t_m)
            r = te.matmul(ps_s[:, 256:258], ones_row_f[:], sc11[0:1, 0:2],
                          start=True, stop=True)
            inc(r, "pe")
        t_mb = ("pe", cnt["pe"])

        @blk.vector
        def _(v):
            v = SER(v, "dve")
            wait(v, t_mb)
            v.tensor_copy(m1b[:], ps_s[:, 256:257])
            v.tensor_copy(m2b[:], ps_s[:, 257:258])
            v.tensor_scalar(thrP1[:], m1b[:], 0.5, None, ALU.mult)
            v.tensor_scalar(thrN1[:], m1b[:], -0.5, None, ALU.mult)
            v.tensor_scalar(thrP2[:], m2b[:], 0.5, None, ALU.mult)
            r = v.tensor_scalar(thrN2[:], m2b[:], -0.5, None, ALU.mult)
            _ = ("dve", cnt["dve"])
        t_thr = ("dve", cnt["dve"])

        t_ld2, t_tern, t_tp, t_ev = {}, {}, {}, {}
        for c in range(HC):
            @blk.sync
            def _(sy, c=c):
                if c == 0:
                    wait(sy, t_thr)
                if c >= 2:
                    wait(sy, t_tern[c - 2])
                r = sy.dma_start(out=sslot(c)[:, 0:D], in_=w1_h[c * 128:(c + 1) * 128, :])
                t_ld2[c] = inc(r, "lw%d" % (c % 2), 16)

            @blk.vector
            def _(v, c=c):
                v = SER(v, "dve")
                wait(v, t_ld2[c])
                if c >= 1:
                    wait(v, t_tp[c - 1])
                v.tensor_scalar(scrV[:, 0:D], sslot(c)[:, 0:D], thrP1[:], None, ALU.is_ge)
                v.tensor_scalar(scrR[:, 0:D], sslot(c)[:, 0:D], thrN1[:], None, ALU.is_le)
                r = v.tensor_tensor(scrT[:, 0:D], scrV[:, 0:D], scrR[:, 0:D], ALU.subtract)
                _ = r; t_tern[c] = ("dve", cnt["dve"])

            @blk.tensor
            def _(te, c=c):
                wait(te, t_tern[c])
                if c >= 1:
                    wait(te, t_ev[c - 1])
                r = None
                for dc in range(DC):
                    pt = ps_t if dc < 4 else ps_t2
                    r = te.transpose(pt[:, (dc % 4) * 128:(dc % 4) * 128 + 128],
                                     scrT[:, dc * 128:(dc + 1) * 128], ident_b[:])
                t_tp[c] = inc(r, "pe")

            @blk.scalar
            def _(a, c=c):
                a = SER(a, "act")
                wait(a, t_tp[c])
                r = None
                for dc in range(DC):
                    pt = ps_t if dc < 4 else ps_t2
                    r = a.copy(w1t[:, dc * H + c * 128:dc * H + (c + 1) * 128],
                               pt[:, (dc % 4) * 128:(dc % 4) * 128 + 128])
                _ = r; t_ev[c] = ("act", cnt["act"])

        t_ld3, t_tern2, t_tp2, t_ev2 = {}, {}, {}, {}
        for c in range(DC):
            @blk.sync
            def _(sy, c=c):
                if c >= 2:
                    wait(sy, t_tern2[c - 2])
                else:
                    wait(sy, t_tern[HC - 2 + c])
                r = sy.dma_start(out=sslot(c), in_=w2_h[c * 128:(c + 1) * 128, :])
                t_ld3[c] = inc(r, "lw%d" % (c % 2), 16)

            @blk.vector
            def _(v, c=c):
                v = SER(v, "dve")
                wait(v, t_ld3[c])
                if c == 0:
                    wait(v, t_tp[HC - 1])
                else:
                    wait(v, t_tp2[(c - 1, 3)])
                r = None
                for q in range(4):
                    sl = slice(q * 1024, (q + 1) * 1024)
                    v.tensor_scalar(scrV[:, 0:D], sslot(c)[:, sl], thrP2[:], None, ALU.is_ge)
                    v.tensor_scalar(scrR[:, 0:D], sslot(c)[:, sl], thrN2[:], None, ALU.is_le)
                    r = v.tensor_tensor(scrT[:, sl], scrV[:, 0:D], scrR[:, 0:D], ALU.subtract)
                _ = r; t_tern2[c] = ("dve", cnt["dve"])

            for b in range(4):
                @blk.tensor
                def _(te, c=c, b=b):
                    wait(te, t_tern2[c])
                    prev = (c, b - 1) if b > 0 else ((c - 1, 3) if c > 0 else None)
                    if prev is not None:
                        wait(te, t_ev2[prev])
                    elif c == 0 and b == 0:
                        wait(te, t_ev[HC - 1])
                    r = None
                    for j in range(8):
                        hh = b * 8 + j
                        pt = ps_t if j < 4 else ps_t2
                        r = te.transpose(pt[:, (j % 4) * 128:(j % 4) * 128 + 128],
                                         scrT[:, hh * 128:(hh + 1) * 128], ident_b[:])
                    t_tp2[(c, b)] = inc(r, "pe")

                @blk.scalar
                def _(a, c=c, b=b):
                    a = SER(a, "act")
                    wait(a, t_tp2[(c, b)])
                    r = None
                    for j in range(8):
                        hh = b * 8 + j
                        pt = ps_t if j < 4 else ps_t2
                        r = a.copy(w2t[:, hh * D + c * 128:hh * D + (c + 1) * 128],
                                   pt[:, (j % 4) * 128:(j % 4) * 128 + 128])
                    _ = r; t_ev2[(c, b)] = ("act", cnt["act"])

        @blk.gpsimd
        def _(g):
            wait(g, t_ev[HC - 1])
            r = g.dma_start(out=sw1t, in_=w1t[:])
            inc(r, "stw", 16)
            wait(g, t_ev2[(DC - 1, 3)])
            r = g.dma_start(out=sw2t, in_=w2t[:])
            inc(r, "stw", 16)
        t_wcache = ("stw", cnt["stw"])

    pctx.close()

    def ffn_phase(tag, src, dst, w1t, w2t, wait_src_st=0, preload_wt=0):
        fctx = contextlib.ExitStack()
        fsb = lambda name, shp, dt: fctx.enter_context(nc.sbuf_tensor(name + "_" + tag, shp, dt))
        fps = lambda name, shp, dt: fctx.enter_context(nc.psum_tensor(name + "_" + tag, shp, dt))

        SCR["V"] = fsb("scrV", [128, 1024], F32)
        SCR["A1"] = fsb("scrA1", [128, 1024], F32)
        SCR["R1"] = fsb("scrR1", [128, 1024], F32)
        SCR["G1"] = SCR["V"]
        SCR["A"] = fsb("scrA", [128, 2048], F32)
        SCR["R"] = fsb("scrR", [128, 2048], F32)
        SCR["G"] = fsb("scrG", [128, 1152], F32)
        SCR["I"] = fsb("i32f", [128, 512], I32)
        hgB = fsb("hg", [128, 3 * H], F32)
        hslot = lambda g: hgB[:, (g % 3) * H:(g % 3) * H + H]
        hp = fsb("hp", [128, H], F32)
        n1g = fsb("n1g", [128, D], BF16)
        n1t = fsb("n1t", [128, 2 * D], BF16)
        n1ts = lambda g: n1t[:, (g % 2) * D:(g % 2) * D + D]
        n2g = fsb("n2g", [128, H], BF16)
        bnd = fsb("bnd", [128, HC], F32)
        c1B = fsb("c1B", [128, 128], F32)
        s2B = fsb("s2B", [128, 128], F32)
        c1row = fsb("c1row", [1, 128], F32)
        c2v = fsb("c2v", [128, 2], F32)
        s2v = fsb("s2v", [128, 1], F32)
        ps_h = [fps("ps_h%d" % i, [128, 512], F32) for i in range(2)]
        ps_o = [fps("ps_o%d" % i, [128, 512], F32) for i in range(2)]
        ps_tt = fps("ps_tt", [128, 512], BF16)
        ps_tt2 = fps("ps_tt2", [128, 512], BF16)
        ps_sm = fps("ps_sm", [128, 512], F32)

        with nc.Block("ffn_" + tag) as blk:
            if preload_wt:
                @blk.sync
                def _(sy):
                    wait(sy, preload_wt)
                    inc(sy.dma_start(out=w1t[:], in_=sw1t), "lwt", 16)
                    t_wt0 = inc(sy.dma_start(out=w2t[:], in_=sw2t), "lwt", 16)
            t_wt = ("lwt", cnt["lwt"])

            t_lds, t_sq, t_red = {}, {}, {}
            for g in range(NT):
                @blk.sync
                def _(sy, g=g):
                    if g == 0:
                        for _tk in (wait_src_st or []):
                            wait(sy, _tk)
                    if g >= XS:
                        wait(sy, t_red[g - XS])
                        wait(sy, t_sq[g - XS])
                    r = sy.dma_start(out=xslot(g), in_=src[g * 128:(g + 1) * 128, :])
                    t_lds[g] = inc(r, "ldx%d" % (g % XS), 16)

                @blk.scalar
                def _(a, g=g):
                    a = SER(a, "act")
                    wait(a, t_lds[g])
                    r = a.activation(scrV[:, 0:D] if g % 2 == 0 else scrG[:, 0:D],
                                     xslot(g), AF.Square, accum_out=st_var[:, g:g + 1])
                    _ = r; t_sq[g] = ("act", cnt["act"])

                @blk.vector
                def _(v, g=g):
                    v = SER(v, "dve")
                    wait(v, t_lds[g])
                    v.tensor_reduce(st_m[:, g:g + 1], xslot(g), AX.X, ALU.add)
                    v.tensor_reduce(st_mx[:, g:g + 1], xslot(g), AX.X, ALU.max)
                    r = v.tensor_reduce(st_mn[:, g:g + 1], xslot(g), AX.X, ALU.min)
                    _ = r; t_red[g] = ("dve", cnt["dve"])


            @blk.vector
            def _(v):
                v = SER(v, "dve")
                wait(v, t_red[NT - 1])
                wait(v, t_sq[NT - 1])
                v.tensor_scalar(st_m[:], st_m[:], 1.0 / D, None, ALU.mult)
                v.tensor_scalar(st_var[:], st_var[:], 1.0 / D, None, ALU.mult)
                v.tensor_tensor(st_t[:], st_m[:], st_m[:], ALU.mult)
                r = v.tensor_tensor(st_var[:], st_var[:], st_t[:], ALU.subtract)
                _ = ("dve", cnt["dve"])
            t_var = ("dve", cnt["dve"])

            @blk.scalar
            def _(a):
                a = SER(a, "act")
                wait(a, t_var)
                r = a.activation(st_rstd[:], st_var[:], AF.Sqrt, bias=CB[1e-5], scale=1.0)
                _ = ("act", cnt["act"])
            t_sd = ("act", cnt["act"])

            @blk.vector
            def _(v):
                v = SER(v, "dve")
                wait(v, t_sd)
                v.reciprocal(st_rstd[:], st_rstd[:])
                v.tensor_tensor(st_mx[:], st_mx[:], st_m[:], ALU.subtract)
                v.tensor_tensor(st_mn[:], st_m[:], st_mn[:], ALU.subtract)
                v.tensor_tensor(st_mx[:], st_mx[:], st_mn[:], ALU.max)
                v.tensor_tensor(st_mx[:], st_mx[:], st_rstd[:], ALU.mult)
                v.tensor_scalar(st_mx[:], st_mx[:], 1e-5, None, ALU.max)
                v.reciprocal(st_t[:], st_mx[:])
                v.tensor_scalar(st_t[:], st_t[:], 127.0, None, ALU.mult)
                v.tensor_tensor(st_A[:], st_rstd[:], st_t[:], ALU.mult)
                v.scalar_tensor_tensor(st_B[:], st_m[:], -1.0, st_A[:], ALU.mult, ALU.mult)
                v.tensor_scalar(st_c1[:], st_mx[:], 1.0 / 127.0, None, ALU.mult)
                r = None
                for g in range(NT):
                    r = v.tensor_tensor(st_c1[:, g:g + 1], st_c1[:, g:g + 1], m1b[:], ALU.mult)
                _ = ("dve", cnt["dve"])
            t_small = ("dve", cnt["dve"])

            t_ldx, t_v, t_n1, t_tp, t_n1t = {}, {}, {}, {}, {}
            t_c1b_pe, t_c1b = {}, {}
            t_mm1, t_hgev = {}, {}
            t_convA, t_convD, t_silu, t_qmx, t_qtr, t_s2, t_qtr2, t_s2b = {}, {}, {}, {}, {}, {}, {}, {}
            t_n2a, t_n2g, t_n2, t_absq0, t_sgn = {}, {}, {}, {}, {}
            t_mm2, t_ev2s, t_st = {}, {}, {}

            def emit_front_a(g):
                @blk.sync
                def _(sy):
                    if g < XS:
                        wait(sy, t_sq[NT - 1])
                        wait(sy, t_small)
                    else:
                        wait(sy, t_ev2s[g - XS])
                    r = sy.dma_start(out=xslot(g), in_=src[g * 128:(g + 1) * 128, :])
                    t_ldx[g] = inc(r, "ldx%d" % (g % XS), 16)

                @blk.scalar
                def _(a):
                    a = SER(a, "act")
                    wait(a, t_ldx[g])
                    wait(a, last_peel1[0])
                    r = a.activation(p1A[:, 0:D], xslot(g), AF.Identity,
                                     bias=st_B[:, g:g + 1], scale=st_A[:, g:g + 1])
                    _ = r; t_v[g] = ("act", cnt["act"])
                    last_absA1[0] = t_v[g]

                @blk.vector
                def _(v):
                    v = SER(v, "dve")
                    wait(v, last_absA1[0])
                    for off in range(0, D, 512):
                        v.tensor_copy(scrI[:, 0:512], p1A[:, off:off + 512])
                        r = v.tensor_copy(n1g[:, off:off + 512], scrI[:, 0:512])
                    _ = r
                    t_n1[g] = ("dve", cnt["dve"])
                    last_peel1[0] = t_n1[g]

                @blk.tensor
                def _(te):
                    wait(te, t_n1[g])
                    if g >= 2:
                        wait(te, t_n1t[g - 2])
                    r = None
                    for dc in range(DC):
                        pt = ps_tt if dc < 4 else ps_tt2
                        r = te.transpose(pt[:, (dc % 4) * 128:(dc % 4) * 128 + 128],
                                         n1g[:, dc * 128:(dc + 1) * 128], ident_b[:])
                    t_tp[g] = inc(r, "pe")

                @blk.vector
                def _(v):
                    v = SER(v, "dve")
                    wait(v, t_tp[g])
                    if g >= 2:
                        wait(v, t_mm1[(g - 2, HC // 4 - 1)])
                    v.tensor_copy(n1ts(g)[:, 0:512], ps_tt[:])
                    v.tensor_copy(n1ts(g)[:, 512:1024], ps_tt2[:])
                    t_n1t[g] = ("dve", cnt["dve"])

                @blk.tensor
                def _(te):
                    wait(te, t_small)
                    if g >= 1:
                        wait(te, t_c1b[g - 1])
                    r = te.transpose(ps_sm[0:1, 128:256], st_c1[:, g:g + 1], ident_f[:])
                    t_c1b_pe[g] = inc(r, "pe")

                @blk.vector
                def _(v):
                    v = SER(v, "dve")
                    wait(v, t_c1b_pe[g])
                    r = v.tensor_copy(c1row[:], ps_sm[0:1, 128:256])
                    _ = r; t_c1b[("row", g)] = ("dve", cnt["dve"])

                @blk.tensor
                def _(te):
                    wait(te, t_c1b[("row", g)])
                    r = te.matmul(ps_sm[:, 0:128], ones_row_f[:], c1row[:],
                                  start=True, stop=True)
                    t_c1b_pe[("b", g)] = inc(r, "pe")

                @blk.vector
                def _(v):
                    v = SER(v, "dve")
                    wait(v, t_c1b_pe[("b", g)])
                    if g >= 1:
                        wait(v, t_hgev[(g - 1, HC // 4 - 1)])
                    r = v.tensor_copy(c1B[:], ps_sm[:, 0:128])
                    _ = r; t_c1b[g] = ("dve", cnt["dve"])

            def emit_front_mm(g):
                for hq in range(HC // 4):
                    bank = ps_h[hq % 2]

                    @blk.tensor
                    def _(te, hq=hq, bank=bank):
                        if hq == 0:
                            wait(te, t_n1t[g])
                            if preload_wt and g == 0:
                                wait(te, t_wt)
                        if hq >= 2:
                            wait(te, t_hgev[(g, hq - 2)])
                        elif g >= 1:
                            wait(te, t_hgev[(g - 1, HC // 4 + hq - 2)])
                        r = None
                        for slot in range(4):
                            hc = hq * 4 + slot
                            for dc in range(DC):
                                r = te.matmul(bank[:, slot * 128:slot * 128 + 128],
                                              w1t[:, dc * H + hc * 128:dc * H + (hc + 1) * 128],
                                              n1ts(g)[:, dc * 128:(dc + 1) * 128],
                                              start=(dc == 0), stop=(dc == DC - 1))
                        t_mm1[(g, hq)] = inc(r, "pe")

                    @blk.vector
                    def _(v, hq=hq, bank=bank):
                        v = SER(v, "dve")
                        wait(v, t_mm1[(g, hq)])
                        if hq == 0:
                            wait(v, t_c1b[g])
                        if g >= 3:
                            wait(v, t_convA[(g - 3, 4 * hq + 3)])
                            if hq == 0:
                                wait(v, t_convD[(g - 3, "bnd")])
                        h3 = hslot(g)[:, hq * 512:(hq + 1) * 512].rearrange(
                            "p (c t) -> p c t", t=128)
                        p3 = bank[:].rearrange("p (c t) -> p c t", t=128)
                        c3, _u = bass.broadcast_tensor_aps(
                            c1B[:].rearrange("p (c t) -> p c t", c=1), h3)
                        v.tensor_tensor(h3, p3, c3, ALU.mult)
                        t_hgev[(g, hq)] = ("dve", cnt["dve"])

            def emit_back_conv(g):
                last = (g == NT - 1)
                @blk.scalar
                def _(a):
                    a = SER(a, "act")
                    if g >= 1:
                        wait(a, t_n2a[(g - 1, 1)])
                    for hc in range(HC):
                        if hc % 4 == 0:
                            wait(a, t_hgev[(g, hc // 4)])
                        a.activation(hp[:, hc * 128:(hc + 1) * 128],
                                     hslot(g)[:, hc * 128:(hc + 1) * 128],
                                     AF.Identity, bias=cb_t[:, hc:hc + 1],
                                     scale=w1c_t[:, hc:hc + 1])
                        t_convA[(g, hc)] = ("act", cnt["act"])
                        if hc == HC // 2 - 1:
                            t_convA[(g, "h0")] = t_convA[(g, hc)]

                for half in range(2):
                    @blk.vector
                    def _(v, half=half):
                        v = SER(v, "dve")
                        if half == 0:
                            wait(v, t_convA[(g, "h0")])
                        else:
                            wait(v, t_convA[(g, HC - 1)])
                        co = half * (HC // 2)
                        sl = slice(half * 2048, (half + 1) * 2048)
                        h3 = hslot(g)[:, sl].rearrange("p (c t) -> p c t", t=128)
                        hp3 = hp[:, sl].rearrange("p (c t) -> p c t", t=128)
                        hw3 = scrA[:, :].rearrange("p (c t) -> p c t", t=128)
                        w0b, _u = bass.broadcast_tensor_aps(
                            w0c_t[:, co:co + HC // 2].rearrange("p (c o) -> p c o", o=1), h3)
                        w2b, _u = bass.broadcast_tensor_aps(
                            w2c_t[:, co:co + HC // 2].rearrange("p (c o) -> p c o", o=1), h3)
                        v.tensor_tensor(hw3, h3, w0b, ALU.mult)
                        v.tensor_tensor(hp3[:, :, 1:128], hp3[:, :, 1:128],
                                        hw3[:, :, 0:127], ALU.add)
                        v.tensor_tensor(hw3, h3, w2b, ALU.mult)
                        r = v.tensor_tensor(hp3[:, :, 0:127], hp3[:, :, 0:127],
                                            hw3[:, :, 1:128], ALU.add)
                        _ = r; t_convD[(g, half)] = ("dve", cnt["dve"])

                @blk.vector
                def _(v):
                    v = SER(v, "dve")
                    if not last:
                        wait(v, t_hgev[(g + 1, HC // 4 - 1)])
                    hview = lambda t_, tn: t_[:].rearrange("p (c t) -> p c t", t=128)[:, :, tn:tn + 1]
                    w3 = lambda w: w[:].rearrange("p (c o) -> p c o", o=1)
                    b3 = bnd[:].rearrange("p (c o) -> p c o", o=1)
                    if g > 0:
                        v.tensor_tensor(b3, hview(hslot(g - 1), 127), w3(w0c_t), ALU.mult)
                        v.tensor_tensor(hview(hp, 0), hview(hp, 0), b3, ALU.add)
                    if not last:
                        v.tensor_tensor(b3, hview(hslot(g + 1), 0), w3(w2c_t), ALU.mult)
                        v.tensor_tensor(hview(hp, 127), hview(hp, 127), b3, ALU.add)
                    t_convD[(g, "bnd")] = ("dve", cnt["dve"])

                @blk.scalar
                def _(a):
                    a = SER(a, "act")
                    wait(a, t_convD[(g, "bnd")])
                    r = a.activation(hp[:], hp[:], AF.Silu)
                    _ = r; t_silu[g] = ("act", cnt["act"])

                @blk.vector
                def _(v):
                    v = SER(v, "dve")
                    wait(v, t_silu[g])
                    if g >= 1:
                        wait(v, t_s2[g - 1])
                    hpt = hp[:, :].rearrange("p (c t) -> p t c", t=128)
                    r = v.tensor_reduce(scrG[:, 1024:1152], hpt, AX.X, ALU.max,
                                        apply_absolute_value=True)
                    _ = r; t_qmx[g] = ("dve", cnt["dve"])

                @blk.tensor
                def _(te):
                    wait(te, t_qmx[g])
                    r = te.transpose(ps_sm[:, 128:256], scrG[:, 1024:1152], ident_f[:])
                    t_qtr[g] = inc(r, "pe")

                @blk.vector
                def _(v):
                    v = SER(v, "dve")
                    wait(v, t_qtr[g])
                    if g >= 2:
                        wait(v, t_ev2s[g - 2])
                    cs = slice(g % 2, g % 2 + 1)
                    v.tensor_reduce(s2v[:], ps_sm[:, 128:256], AX.X, ALU.max)
                    v.tensor_scalar(s2v[:], s2v[:], 1e-5, None, ALU.max)
                    v.tensor_scalar(c2v[:, cs], s2v[:], 0.5 / 127.0, None, ALU.mult)
                    v.tensor_tensor(c2v[:, cs], c2v[:, cs], m2b[:], ALU.mult)
                    v.reciprocal(s2v[:], s2v[:])
                    r = v.tensor_scalar(s2v[:], s2v[:], 127.0, None, ALU.mult)
                    _ = r; t_s2[g] = ("dve", cnt["dve"])

                @blk.tensor
                def _(te):
                    wait(te, t_s2[g])
                    r = te.transpose(ps_sm[0:1, 384:512], s2v[:], ident_f[:])
                    t_qtr2[g] = inc(r, "pe")

                @blk.vector
                def _(v):
                    v = SER(v, "dve")
                    wait(v, t_qtr2[g])
                    r = v.tensor_copy(scrR[0:1, 1920:2048], ps_sm[0:1, 384:512])
                    _ = r; t_s2b[("row", g)] = ("dve", cnt["dve"])

                @blk.tensor
                def _(te):
                    wait(te, t_s2b[("row", g)])
                    r = te.matmul(ps_sm[:, 256:384], ones_row_f[:],
                                  scrR[0:1, 1920:2048], start=True, stop=True)
                    t_qtr2[("b", g)] = inc(r, "pe")

                @blk.vector
                def _(v):
                    v = SER(v, "dve")
                    wait(v, t_qtr2[("b", g)])
                    if g >= 1:
                        wait(v, t_n2a[(g - 1, 1)])
                    r = v.tensor_copy(s2B[:], ps_sm[:, 256:384])
                    _ = r; t_s2b[g] = ("dve", cnt["dve"])

            def _emit_q(g, q):
                sl = slice(q * 2048, (q + 1) * 2048)

                @blk.vector
                def _(v, q=q, sl=sl):
                    v = SER(v, "dve")
                    wait(v, t_silu[g])
                    wait(v, t_s2b[g])
                    a3 = scrA[:, :].rearrange("p (c t) -> p c t", t=128)
                    h3 = hp[:, sl].rearrange("p (c t) -> p c t", t=128)
                    b3, _u = bass.broadcast_tensor_aps(
                        s2B[:].rearrange("p (c t) -> p c t", c=1), a3)
                    r = v.tensor_tensor(a3, h3, b3, ALU.mult)
                    _ = r; t_n2a[(g, q)] = ("dve", cnt["dve"])

                @blk.vector
                def _(v, q=q, sl=sl):
                    v = SER(v, "dve")
                    if q == 0 and g >= 1:
                        wait(v, t_mm2[(g - 1, 1)])
                    for off in range(0, 2048, 512):
                        v.tensor_copy(scrI[:, 0:512], scrA[:, off:off + 512])
                        r = v.tensor_copy(n2g[:, sl.start + off:sl.start + off + 512],
                                          scrI[:, 0:512])
                    _ = r
                    t_n2[(g, q)] = ("dve", cnt["dve"])
                    last_peel[0] = t_n2[(g, q)]

            def emit_back_restA(g):
                _emit_q(g, 0)

            def emit_back_restB(g):
                _emit_q(g, 1)

                for dh in range(2):
                    @blk.tensor
                    def _(te, dh=dh):
                        wait(te, t_n2[(g, 1)])
                        if g >= 1 and dh == 0:
                            wait(te, t_ev2s[g - 1])
                        r = None
                        for hc in range(HC):
                            r = te.matmul(ps_o[dh][:],
                                          n2g[:, hc * 128:(hc + 1) * 128],
                                          w2t[:, hc * D + dh * 512:hc * D + (dh + 1) * 512],
                                          start=(hc == 0), stop=(hc == HC - 1))
                        t_mm2[(g, dh)] = inc(r, "pe")

            def emit_evst(g):
                @blk.vector
                def _(v):
                    v = SER(v, "dve")
                    wait(v, t_mm2[(g, 1)])
                    if g >= 2:
                        wait(v, t_st[g - 2])
                    else:
                        for _tk in (wait_src_st or []):
                            wait(v, _tk)
                    r = None
                    for dh in range(2):
                        r = v.scalar_tensor_tensor(oslot(g)[:, dh * 512:(dh + 1) * 512],
                                                   ps_o[dh][:], c2v[:, g % 2:g % 2 + 1],
                                                   xslot(g)[:, dh * 512:(dh + 1) * 512],
                                                   ALU.mult, ALU.add)
                    _ = r; t_ev2s[g] = ("dve", cnt["dve"])

                @blk.gpsimd
                def _(gp):
                    wait(gp, t_ev2s[g])
                    r = gp.dma_start(out=dst[g * 128:(g + 1) * 128, :], in_=oslot(g))
                    t_st[g] = inc(r, "st%d" % (g % 2), 16)

            for it in range(NT + 2):
                if it == 0:
                    emit_front_a(0)
                    emit_front_mm(0)
                if 1 <= it < NT + 1:
                    emit_back_conv(it - 1)
                if 0 <= it - 2 < NT:
                    emit_evst(it - 2)
                if it + 1 < NT:
                    emit_front_a(it + 1)
                if 1 <= it < NT + 1:
                    emit_back_restA(it - 1)
                if it + 1 < NT:
                    emit_front_mm(it + 1)
                if 1 <= it < NT + 1:
                    emit_back_restB(it - 1)

        fctx.close()
        return [("st0", cnt["st0"]), ("st1", cnt["st1"])]

    def attn_phase(src, dst, wait_src_st):
        actx = contextlib.ExitStack()
        asb = lambda name, shp, dt: actx.enter_context(nc.sbuf_tensor(name, shp, dt))
        aps = lambda name, shp, dt: actx.enter_context(nc.psum_tensor(name, shp, dt))

        xnT = asb("xnT", [128, DC * S], BF16)
        kT = asb("kT", [128, DC * S], BF16)
        v_sb = asb("v_sb", [128, NT * D], BF16)
        wb0 = asb("wb0", [128, DC * D], BF16)
        wb1 = asb("wb1", [128, DC * D], BF16)
        stgA = asb("stgA", [128, 4 * D], F32)
        aslot = lambda s: stgA[:, (s % 4) * D:(s % 4) * D + D]
        xnb = asb("xnb", [128, D], BF16)
        qTc = asb("qTc", [128, D], BF16)
        pscr = asb("pscr", [128, S], F32)
        pbf = asb("pbf", [128, S], BF16)
        ptb = asb("ptb", [128, S], BF16)
        ybf = asb("ybf", [128, D], BF16)
        ytb = asb("ytb", [128, D], BF16)
        oscr = asb("oscr", [128, D], F32)
        rmx = asb("rmx", [128, 4], F32)
        nrm = asb("nrm", [128, 1], F32)
        ssum = asb("ssum", [128, 1], F32)
        ps_sc = [aps("ps_sc%d" % i, [128, 512], F32) for i in range(4)]
        ps_ax = [aps("ps_ax%d" % i, [128, 512], F32) for i in range(2)]
        ps_tr = aps("ps_tr", [128, 512], BF16)
        ps_tr2 = aps("ps_tr2", [128, 512], BF16)

        with nc.Block("attn") as blk:
            @blk.vector
            def _(v):
                wait(v, t_wcache)

            @blk.scalar
            def _(a):
                wait(a, t_wcache)

            @blk.sync
            def _(sy):
                wait(sy, t_wcache)

            @blk.gpsimd
            def _(gp):
                wait(gp, t_wcache)

            t_wk, t_wkc = {}, {}
            for c in range(DC):
                @blk.sync
                def _(sy, c=c):
                    if c == 0:
                        for _tk in (wait_src_st or []):
                            wait(sy, _tk)
                    if c >= 4:
                        wait(sy, t_wkc[c - 4])
                    r = sy.dma_start(out=aslot(c), in_=wk_h[c * 128:(c + 1) * 128, :])
                    t_wk[c] = inc(r, "lw%d" % (c % 4), 16)

                @blk.gpsimd
                def _(gp, c=c):
                    gp = SER(gp, "pool")
                    if c == 0:
                        for _tk in (wait_src_st or []):
                            wait(gp, _tk)
                    wait(gp, t_wk[c])
                    r = gp.tensor_copy(wb0[:, c * D:(c + 1) * D], aslot(c))
                    _ = r; t_wkc[c] = ("pool", cnt["pool"])

            t_lds, t_sq, t_red = {}, {}, {}
            for g in range(NT):
                @blk.sync
                def _(sy, g=g):
                    if g == 0:
                        for _tk in (wait_src_st or []):
                            wait(sy, _tk)
                    if g >= XS:
                        wait(sy, t_red[g - XS])
                        wait(sy, t_sq[g - XS])
                    r = sy.dma_start(out=xslot(g), in_=src[g * 128:(g + 1) * 128, :])
                    t_lds[g] = inc(r, "ldx%d" % (g % XS), 16)

                @blk.scalar
                def _(a, g=g):
                    a = SER(a, "act")
                    wait(a, t_lds[g])
                    r = a.activation(oscr[:], xslot(g), AF.Square,
                                     accum_out=st_var[:, g:g + 1])
                    _ = r; t_sq[g] = ("act", cnt["act"])

                @blk.vector
                def _(v, g=g):
                    v = SER(v, "dve")
                    wait(v, t_lds[g])
                    r = v.tensor_reduce(st_m[:, g:g + 1], xslot(g), AX.X, ALU.add)
                    _ = r; t_red[g] = ("dve", cnt["dve"])

            @blk.vector
            def _(v):
                v = SER(v, "dve")
                wait(v, t_red[NT - 1])
                wait(v, t_sq[NT - 1])
                v.tensor_scalar(st_m[:], st_m[:], 1.0 / D, None, ALU.mult)
                v.tensor_scalar(st_var[:], st_var[:], 1.0 / D, None, ALU.mult)
                v.tensor_tensor(st_t[:], st_m[:], st_m[:], ALU.mult)
                r = v.tensor_tensor(st_var[:], st_var[:], st_t[:], ALU.subtract)
                _ = ("dve", cnt["dve"])
            t_var = ("dve", cnt["dve"])

            @blk.scalar
            def _(a):
                a = SER(a, "act")
                wait(a, t_var)
                r = a.activation(st_rstd[:], st_var[:], AF.Sqrt, bias=CB[1e-5], scale=1.0)
                _ = ("act", cnt["act"])
            t_sd = ("act", cnt["act"])

            @blk.vector
            def _(v):
                v = SER(v, "dve")
                wait(v, t_sd)
                v.reciprocal(st_rstd[:], st_rstd[:])
                r = v.scalar_tensor_tensor(st_B[:], st_m[:], -1.0, st_rstd[:],
                                           ALU.mult, ALU.mult)
                _ = ("dve", cnt["dve"])
            t_ab = ("dve", cnt["dve"])

            t_wv, t_wvc = {}, {}
            for c in range(DC):
                @blk.sync
                def _(sy, c=c):
                    if c < 4:
                        wait(sy, t_wkc[c + 4])
                    else:
                        wait(sy, t_wvc[c - 4])
                    r = sy.dma_start(out=aslot(c), in_=wv_h[c * 128:(c + 1) * 128, :])
                    t_wv[c] = inc(r, "lw%d" % (c % 4), 16)

                @blk.gpsimd
                def _(gp, c=c):
                    gp = SER(gp, "pool")
                    wait(gp, t_wv[c])
                    r = gp.tensor_copy(wb1[:, c * D:(c + 1) * D], aslot(c))
                    _ = r; t_wvc[c] = ("pool", cnt["pool"])

            t_ldx2, t_xn, t_xtp, t_xnt = {}, {}, {}, {}
            for g in range(NT):
                @blk.sync
                def _(sy, g=g):
                    if g < XS:
                        wait(sy, t_sq[NT - 1])
                        wait(sy, t_red[NT - 1])
                    else:
                        wait(sy, t_xn[g - XS])
                    r = sy.dma_start(out=xslot(g), in_=src[g * 128:(g + 1) * 128, :])
                    t_ldx2[g] = inc(r, "ldx%d" % (g % XS), 16)

                @blk.scalar
                def _(a, g=g):
                    a = SER(a, "act")
                    wait(a, t_ldx2[g])
                    if g == 0:
                        wait(a, t_ab)
                    else:
                        wait(a, t_xtp[g - 1])
                    r = a.activation(xnb[:], xslot(g), AF.Identity,
                                     bias=st_B[:, g:g + 1], scale=st_rstd[:, g:g + 1])
                    _ = r; t_xn[g] = ("act", cnt["act"])

                @blk.tensor
                def _(te, g=g):
                    wait(te, t_xn[g])
                    if g >= 1:
                        wait(te, t_xnt[g - 1])
                    r = None
                    for dc in range(DC):
                        pt = ps_tr if dc < 4 else ps_tr2
                        r = te.transpose(pt[:, (dc % 4) * 128:(dc % 4) * 128 + 128],
                                         xnb[:, dc * 128:(dc + 1) * 128], ident_b[:])
                    t_xtp[g] = inc(r, "pe")

                @blk.vector
                def _(v, g=g):
                    v = SER(v, "dve")
                    wait(v, t_xtp[g])
                    xv = xnT[:].rearrange("p (c t) -> p c t", t=S)[:, :, g * 128:(g + 1) * 128]
                    v.tensor_copy(xv[:, 0:4, :], ps_tr[:].rearrange("p (c t) -> p c t", t=128))
                    v.tensor_copy(xv[:, 4:8, :], ps_tr2[:].rearrange("p (c t) -> p c t", t=128))
                    t_xnt[g] = ("dve", cnt["dve"])

            if ATTN_CUT == "ln":
                actx.close()
                return None
            kv_banks = [ps_ax[0], ps_ax[1], ps_sc[0], ps_sc[1]]
            t_kmm, t_kev = {}, {}
            for jc in range(DC):
                for sp in range(4):
                    i = jc * 4 + sp
                    bank = kv_banks[i % 4]

                    @blk.tensor
                    def _(te, jc=jc, sp=sp, i=i, bank=bank):
                        if i == 0:
                            wait(te, t_wkc[DC - 1])
                            wait(te, t_xnt[NT - 1])
                        if i >= 4:
                            wait(te, t_kev[i - 4])
                        r = None
                        for dc in range(DC):
                            r = te.matmul(bank[:],
                                          wb0[:, dc * D + jc * 128:dc * D + (jc + 1) * 128],
                                          xnT[:, dc * S + sp * 512:dc * S + (sp + 1) * 512],
                                          start=(dc == 0), stop=(dc == DC - 1))
                        t_kmm[i] = inc(r, "pe")

                @blk.scalar
                def _(a, jc=jc):
                    a = SER(a, "act")
                    for sp in range(4):
                        i = jc * 4 + sp
                        wait(a, t_kmm[i])
                        a.activation(kT[:, jc * S + sp * 512:jc * S + (sp + 1) * 512],
                                     kv_banks[i % 4][:], AF.Relu,
                                     bias=bk_t[:, jc:jc + 1], scale=1.0)
                        t_kev[i] = ("act", cnt["act"])

            t_vmm, t_vev = {}, {}
            for tc in range(NT):
                for jh in range(2):
                    i = tc * 2 + jh
                    bank = kv_banks[i % 4]

                    @blk.tensor
                    def _(te, tc=tc, jh=jh, i=i, bank=bank):
                        if i == 0:
                            wait(te, t_wvc[DC - 1])
                            wait(te, t_kev[DC * 4 - 1])
                        if i >= 4:
                            wait(te, t_vev[i - 4])
                        for dc in range(DC):
                            te.matmul(bank[:],
                                      xnT[:, dc * S + tc * 128:dc * S + (tc + 1) * 128],
                                      wb1[:, dc * D + jh * 512:dc * D + (jh + 1) * 512],
                                      start=(dc == 0), stop=False)
                        r = te.matmul(bank[:], ones_row_b[:],
                                      bv_row[:, jh * 512:(jh + 1) * 512],
                                      start=False, stop=True)
                        t_vmm[i] = inc(r, "pe")

                if tc % 2 == 1:
                    @blk.scalar
                    def _(a, tc=tc):
                        a = SER(a, "act")
                        for i in range((tc - 1) * 2, (tc + 1) * 2):
                            wait(a, t_vmm[i])
                            a.activation(v_sb[:, (i // 2) * D + (i % 2) * 512:
                                               (i // 2) * D + (i % 2 + 1) * 512],
                                         kv_banks[i % 4][:], AF.Relu)
                            t_vev[i] = ("act", cnt["act"])

            if ATTN_CUT == "kv":
                actx.close()
                return None
            t_wq, t_wqc, t_wo, t_woc = {}, {}, {}, {}
            for c in range(DC):
                @blk.sync
                def _(sy, c=c):
                    if c < 4:
                        wait(sy, t_wvc[c + 4])
                    else:
                        wait(sy, t_wqc[c - 4])
                    r = sy.dma_start(out=aslot(c), in_=wq_h[c * 128:(c + 1) * 128, :])
                    t_wq[c] = inc(r, "lw%d" % (c % 4), 16)

                @blk.gpsimd
                def _(gp, c=c):
                    gp = SER(gp, "pool")
                    wait(gp, t_wq[c])
                    wait(gp, t_kev[DC * 4 - 1])
                    r = gp.tensor_copy(wb0[:, c * D:(c + 1) * D], aslot(c))
                    _ = r; t_wqc[c] = ("pool", cnt["pool"])

            for c in range(DC):
                @blk.sync
                def _(sy, c=c):
                    if c < 4:
                        wait(sy, t_wqc[c + 4])
                    else:
                        wait(sy, t_woc[c - 4])
                    r = sy.dma_start(out=aslot(c), in_=wo_h[c * 128:(c + 1) * 128, :])
                    t_wo[c] = inc(r, "lw%d" % (c % 4), 16)

                @blk.gpsimd
                def _(gp, c=c):
                    gp = SER(gp, "pool")
                    wait(gp, t_wo[c])
                    wait(gp, t_vev[NT * 2 - 1])
                    r = gp.tensor_copy(wb1[:, c * D:(c + 1) * D], aslot(c))
                    _ = r; t_woc[c] = ("pool", cnt["pool"])

            if ATTN_CUT == "wq":
                actx.close()
                return None
            t_q, t_qe = {}, {}
            t_sc, t_mx, t_exp, t_pb = {}, {}, {}, {}
            t_pt, t_ptd = {}, {}
            t_ymm, t_yev, t_ytp, t_ytd = {}, {}, {}, {}
            t_omm, t_oev, t_res, t_xst, t_xl3 = {}, {}, {}, {}, {}

            for i in range(NT):
                W = (i + 1) * 128
                nb = (W + 511) // 512

                if ATTN_CUT not in ("", "full"):
                    pass
                else:
                    @blk.sync
                    def _(sy, i=i):
                        if i < XS:
                            wait(sy, t_xn[NT - 1])
                        elif (i - XS) in t_res:
                            wait(sy, t_res[i - XS])
                        r = sy.dma_start(out=xslot(i), in_=src[i * 128:(i + 1) * 128, :])
                        t_xl3[i] = inc(r, "ldx%d" % (i % XS), 16)

                if ATTN_CUT == "xl":
                    continue
                qreg = lambda jc: ps_ax[jc % 2][:, (jc // 2) * 128:(jc // 2 + 1) * 128]
                for jc in range(DC):
                    @blk.tensor
                    def _(te, jc=jc, i=i):
                        if i == 0 and jc == 0:
                            wait(te, t_woc[DC - 1])
                            wait(te, t_vev[NT * 2 - 1])
                        if jc == 0 and i >= 1:
                            if (i - 1) in t_oev:
                                wait(te, t_oev[i - 1])
                            else:
                                wait(te, t_qe[(i - 1, DC - 1)])
                        r = None
                        for dc in range(DC):
                            r = te.matmul(qreg(jc),
                                          wb0[:, dc * D + jc * 128:dc * D + (jc + 1) * 128],
                                          xnT[:, dc * S + i * 128:dc * S + (i + 1) * 128],
                                          start=(dc == 0), stop=(dc == DC - 1))
                        t_q[(i, jc)] = inc(r, "pe")

                @blk.scalar
                def _(a, i=i):
                    a = SER(a, "act")
                    for jc in range(DC):
                        wait(a, t_q[(i, jc)])
                    for jc in range(DC):
                        a.activation(qTc[:, jc * 128:(jc + 1) * 128], qreg(jc),
                                     AF.Relu, bias=bq_t[:, jc:jc + 1], scale=1.0)
                        t_qe[(i, jc)] = ("act", cnt["act"])

                if ATTN_CUT == "q":
                    continue
                for tkb in range(nb):
                    wdt = min(512, W - tkb * 512)

                    @blk.tensor
                    def _(te, tkb=tkb, wdt=wdt, i=i):
                        if tkb == 0:
                            wait(te, t_qe[(i, DC - 1)])
                            if i >= 1:
                                wait(te, t_mx[i - 1])
                                if (i - 1) in t_yev:
                                    wait(te, t_yev[i - 1])
                                else:
                                    wait(te, t_exp[(i - 1, (i * 128 + 511) // 512 - 1)])
                        r = None
                        for jc in range(DC):
                            r = te.matmul(ps_sc[tkb][:, 0:wdt],
                                          qTc[:, jc * 128:(jc + 1) * 128],
                                          kT[:, jc * S + tkb * 512:jc * S + tkb * 512 + wdt],
                                          start=(jc == 0), stop=(jc == DC - 1))
                        t_sc[(i, tkb)] = inc(r, "pe")

                @blk.vector
                def _(v, i=i, nb=nb, W=W):
                    v = SER(v, "dve")
                    for tkb in range(nb):
                        wdt = min(512, W - tkb * 512)
                        wait(v, t_sc[(i, tkb)])
                        v.tensor_reduce(rmx[:, tkb:tkb + 1], ps_sc[tkb][:, 0:wdt],
                                        AX.X, ALU.max)
                    if i >= 1:
                        wait(v, t_exp[(i - 1, (i * 128 + 511) // 512 - 1)])
                    v.tensor_reduce(nrm[:], rmx[:, 0:nb], AX.X, ALU.max)
                    r = v.tensor_scalar(nrm[:], nrm[:], -1.0 / 32.0, None, ALU.mult)
                    _ = r; t_mx[i] = ("dve", cnt["dve"])

                @blk.scalar
                def _(a, i=i, nb=nb, W=W):
                    a = SER(a, "act")
                    wait(a, t_mx[i])
                    for tkb in range(nb):
                        wdt = min(512, W - tkb * 512)
                        a.activation(pscr[:, tkb * 512:tkb * 512 + wdt],
                                     ps_sc[tkb][:, 0:wdt], AF.Exp,
                                     bias=nrm[:], scale=1.0 / 32.0)
                        t_exp[(i, tkb)] = ("act", cnt["act"])

                @blk.vector
                def _(v, i=i, nb=nb, W=W):
                    v = SER(v, "dve")
                    wait(v, t_exp[(i, nb - 1)])
                    if i >= 1 and (i - 1, i - 1) in t_pt:
                        wait(v, t_pt[(i - 1, i - 1)])
                    v.tensor_tensor(pscr[:, i * 128:W], pscr[:, i * 128:W], m01_b[:], ALU.mult)
                    v.tensor_reduce(ssum[:], pscr[:, 0:W], AX.X, ALU.add)
                    v.reciprocal(ssum[:], ssum[:])
                    r = v.tensor_scalar(pbf[:, 0:W], pscr[:, 0:W], ssum[:], None, ALU.mult)
                    _ = r; t_pb[i] = ("dve", cnt["dve"])

                if ATTN_CUT == "sc":
                    continue
                nsb = (i + 8) // 8
                for sb in range(nsb):
                    lo, hi = sb * 8, min(i + 1, sb * 8 + 8)

                    @blk.tensor
                    def _(te, sb=sb, lo=lo, hi=hi, i=i):
                        if sb == 0:
                            wait(te, t_pb[i])
                            if i >= 1:
                                if (i - 1) in t_ytd:
                                    wait(te, t_ytd[i - 1])
                                else:
                                    wait(te, t_ptd.get((i - 1, (i + 7) // 8 - 1), 0))
                        else:
                            wait(te, t_ptd[(i, sb - 1)])
                        r = None
                        for tkc in range(lo, hi):
                            pt = ps_tr if (tkc % 8) < 4 else ps_tr2
                            r = te.transpose(pt[:, (tkc % 4) * 128:(tkc % 4) * 128 + 128],
                                             pbf[:, tkc * 128:(tkc + 1) * 128], ident_b[:])
                        t_pt[(i, sb)] = inc(r, "pe")

                    @blk.vector
                    def _(v, sb=sb, lo=lo, hi=hi, i=i):
                        v = SER(v, "dve")
                        wait(v, t_pt[(i, sb)])
                        n1 = min(hi - lo, 4)
                        v.tensor_copy(ptb[:, lo * 128:(lo + n1) * 128], ps_tr[:, 0:n1 * 128])
                        if hi - lo > 4:
                            n2 = hi - lo - 4
                            v.tensor_copy(ptb[:, (lo + 4) * 128:hi * 128], ps_tr2[:, 0:n2 * 128])
                        t_ptd[(i, sb)] = ("dve", cnt["dve"])

                if ATTN_CUT == "pt":
                    continue
                for dh in range(2):
                    @blk.tensor
                    def _(te, dh=dh, i=i):
                        if dh == 0:
                            wait(te, t_ptd[(i, (i + 8) // 8 - 1)])
                        r = None
                        for tkc in range(i + 1):
                            r = te.matmul(ps_sc[dh][:],
                                          ptb[:, tkc * 128:(tkc + 1) * 128],
                                          v_sb[:, tkc * D + dh * 512:tkc * D + (dh + 1) * 512],
                                          start=(tkc == 0), stop=(tkc == i))
                        t_ymm[(i, dh)] = inc(r, "pe")

                @blk.scalar
                def _(a, i=i):
                    a = SER(a, "act")
                    wait(a, t_ymm[(i, 1)])
                    a.activation(ybf[:, 0:512], ps_sc[0][:], AF.Relu)
                    r = a.activation(ybf[:, 512:1024], ps_sc[1][:], AF.Relu)
                    _ = r; t_yev[i] = ("act", cnt["act"])

                if ATTN_CUT == "pv":
                    continue
                @blk.tensor
                def _(te, i=i):
                    wait(te, t_yev[i])
                    wait(te, t_ptd[(i, (i + 8) // 8 - 1)])
                    r = None
                    for dc in range(DC):
                        pt = ps_tr if dc < 4 else ps_tr2
                        r = te.transpose(pt[:, (dc % 4) * 128:(dc % 4) * 128 + 128],
                                         ybf[:, dc * 128:(dc + 1) * 128], ident_b[:])
                    t_ytp[i] = inc(r, "pe")

                @blk.vector
                def _(v, i=i):
                    v = SER(v, "dve")
                    wait(v, t_ytp[i])
                    v.tensor_copy(ytb[:, 0:512], ps_tr[:])
                    v.tensor_copy(ytb[:, 512:1024], ps_tr2[:])
                    t_ytd[i] = ("dve", cnt["dve"])

                for jh in range(2):
                    @blk.tensor
                    def _(te, jh=jh, i=i):
                        wait(te, t_ytd[i])
                        for dc in range(DC):
                            te.matmul(ps_ax[jh][:],
                                      ytb[:, dc * 128:(dc + 1) * 128],
                                      wb1[:, dc * D + jh * 512:dc * D + (jh + 1) * 512],
                                      start=(dc == 0), stop=False)
                        r = te.matmul(ps_ax[jh][:], ones_row_b[:],
                                      bo_row[:, jh * 512:(jh + 1) * 512],
                                      start=False, stop=True)
                        t_omm[(i, jh)] = inc(r, "pe")

                @blk.scalar
                def _(a, i=i):
                    a = SER(a, "act")
                    wait(a, t_omm[(i, 1)])
                    a.activation(oscr[:, 0:512], ps_ax[0][:], AF.Relu)
                    r = a.activation(oscr[:, 512:1024], ps_ax[1][:], AF.Relu)
                    _ = r; t_oev[i] = ("act", cnt["act"])

                @blk.vector
                def _(v, i=i):
                    v = SER(v, "dve")
                    wait(v, t_oev[i])
                    wait(v, t_xl3[i])
                    if i >= 2:
                        wait(v, t_xst[i - 2])
                    else:
                        for _tk in (wait_src_st or []):
                            wait(v, _tk)
                    r = v.tensor_tensor(oslot(i), oscr[:], xslot(i), ALU.add)
                    _ = r; t_res[i] = ("dve", cnt["dve"])

                @blk.gpsimd
                def _(gp, i=i):
                    wait(gp, t_res[i])
                    r = gp.dma_start(out=dst[i * 128:(i + 1) * 128, :], in_=oslot(i))
                    t_xst[i] = inc(r, "st%d" % (i % 2), 16)

        actx.close()
        return [("st0", cnt["st0"]), ("st1", cnt["st1"])]

    if phases == 4:
        wctx.close()
        attn_phase(x_h, y_h, None)
    elif phases == 1:
        ffn_phase("f1", x_h, y_h, w1t, w2t)
        wctx.close()
    elif phases == 2:
        t1 = ffn_phase("f1", x_h, sx1, w1t, w2t)
        wctx.close()
        attn_phase(sx1, y_h, t1)
    else:
        t1 = ffn_phase("f1", x_h, sx1, w1t, w2t)
        wctx.close()
        t2 = attn_phase(sx1, sx2, t1)
        w3ctx = contextlib.ExitStack()
        w1t3 = w3ctx.enter_context(nc.sbuf_tensor("w1t3", [128, DC * H], FP8))
        w2t3 = w3ctx.enter_context(nc.sbuf_tensor("w2t3", [128, HC * D], FP8))
        ffn_phase("f3", sx2, y_h, w1t3, w2t3, wait_src_st=t2, preload_wt=t_wcache)
        w3ctx.close()

    with nc.Block("fin") as blk:
        @blk.gpsimd
        def _(g):
            wait(g, "st0", cnt["st0"])
            wait(g, "st1", cnt["st1"])
            wait(g, "stw", cnt["stw"])

    gctx.close()
    return nc


def kernel(L0_attn_x, L0_attn_Wq, L0_attn_bq, L0_attn_Wk, L0_attn_bk,
           L0_attn_Wv, L0_attn_bv, L0_attn_Wo, L0_attn_bo,
           L0_ffn_w1, L0_ffn_conv_w, L0_ffn_conv_b, L0_ffn_w2):
    nc = build_nc(PHASES)
    x = np.ascontiguousarray(np.asarray(L0_attn_x, dtype=np.float32))
    com = {
        "wq": np.ascontiguousarray(np.asarray(L0_attn_Wq, np.float32)),
        "bq": np.ascontiguousarray(np.asarray(L0_attn_bq, np.float32)),
        "wk": np.ascontiguousarray(np.asarray(L0_attn_Wk, np.float32)),
        "bk": np.ascontiguousarray(np.asarray(L0_attn_bk, np.float32)),
        "wv": np.ascontiguousarray(np.asarray(L0_attn_Wv, np.float32)),
        "bv": np.ascontiguousarray(np.asarray(L0_attn_bv, np.float32)),
        "wo": np.ascontiguousarray(np.asarray(L0_attn_Wo, np.float32)),
        "bo": np.ascontiguousarray(np.asarray(L0_attn_bo, np.float32)),
        "w1": np.ascontiguousarray(np.asarray(L0_ffn_w1, np.float32)),
        "cw": np.ascontiguousarray(np.asarray(L0_ffn_conv_w, np.float32)),
        "cb": np.ascontiguousarray(np.asarray(L0_ffn_conv_b, np.float32)),
        "w2": np.ascontiguousarray(np.asarray(L0_ffn_w2, np.float32)),
    }
    in_maps = [dict(com, x=np.ascontiguousarray(x[b])) for b in range(N_CORES)]
    res = run_bass_kernel_spmd(nc, in_maps, core_ids=list(range(N_CORES)))
    return np.stack([np.asarray(r["y"], dtype=np.float32) for r in res.results], axis=0)
